# version 22
# baseline (speedup 1.0000x reference)
import numpy as np
import ml_dtypes
from contextlib import ExitStack
import concourse.bass as bass
import concourse.mybir as mybir
from concourse.bass_utils import run_bass_kernel_spmd

F32 = mybir.dt.float32
BF16 = mybir.dt.bfloat16
I32 = mybir.dt.int32
AF = mybir.ActivationFunctionType
ALU = mybir.AluOpType
AX = mybir.AxisListType

NCORES = 8
L = 4096
E = 1024
EC = 8
H = 16
D = 64
MLP = 4096
MC = 32
IF = 2816
IC = 22
T = 512
NT = L // T
TC = T // 128
LN_EPS = 1e-5
DEN_EPS = 1e-6


class P:

    COMPUTE = ("tensor", "vector", "scalar", "gpsimd")
    CHANNELS = ("tensor", "vector", "scalar", "gpsimd", "dsync", "dgp")

    def __init__(self, serial=False):
        self.nc = bass.Bass("TRN2", target_bir_lowering=False)
        self.steps = []
        self.outs = []
        self.serial = serial
        self._pending = []
        self._barrier = {}
        self._ser_now = "g" if serial else None
        self._cur_fence = {}
        self._alloc_fence = {}

    def set_serial(self, v, tag="g"):
        self._ser_now = tag if v else None

    def fence(self):
        snap = {}
        for i, st in enumerate(self.steps):
            snap[st["ch"]] = i
        self._cur_fence = snap

    def barrier(self):
        snap = {}
        for i, st in enumerate(self.steps):
            snap[st["ch"]] = i
        self._barrier = dict(snap)

    def nm(self, n):
        if not hasattr(self, "_names"):
            self._names = {}
        i = self._names.get(n, 0)
        self._names[n] = i + 1
        return n if i == 0 else f"{n}_{i}"

    def sbuf(self, n, s, d=F32):
        nm = self.nm(n)
        self._alloc_fence[nm] = self._cur_fence
        return self.nc.sbuf_tensor(nm, s, d)

    def psum(self, n, s, d=F32):
        esz = 2 if d == BF16 else 4
        per_bank = 2048 // esz
        cols = ((s[1] + per_bank - 1) // per_bank) * per_bank
        nm = self.nm(n)
        self._alloc_fence[nm] = self._cur_fence
        return self.nc.psum_tensor(nm, [s[0], cols], d)

    def din(self, name, shape, dtype=F32):
        return self.nc.dram_tensor(name, list(shape), dtype, kind="ExternalInput").ap()

    def dout(self, name, shape, dtype=F32):
        self.outs.append(name)
        return self.nc.dram_tensor(name, list(shape), dtype, kind="ExternalOutput").ap()

    def dtmp(self, name, shape, dtype=F32):
        return self.nc.dram_tensor(name, list(shape), dtype, kind="Internal").ap()

    @staticmethod
    def A(x):
        if hasattr(x, "ap") and callable(getattr(x, "ap")):
            try:
                return x.ap()
            except TypeError:
                return x
        return x

    @staticmethod
    def _tn(x):
        x = P.A(x)
        t = getattr(x, "tensor", None)
        return t.name if t is not None else None

    def _add(self, ch, inc, fn, rd=(), wr=()):
        rdn = {n for n in (P._tn(a) for a in rd) if n is not None}
        wrn = {n for n in (P._tn(a) for a in wr) if n is not None}
        extra = self._pending
        self._pending = []
        self.steps.append(dict(ch=ch, inc=inc, fn=fn, rd=rdn, wr=wrn,
                               extra=extra, bar=self._barrier,
                               ser=self._ser_now))
        return len(self.steps) - 1

    def mark(self):
        return len(self.steps) - 1

    def dep_on(self, *idxs):
        self._pending = [i for i in idxs if i is not None and i >= 0]

    def dma(self, out, in_, eng="sync", **kw):
        ch = "dsync" if eng == "sync" else "dgp"
        self._add(ch, 16,
                  lambda e, a=P.A(out), b=P.A(in_), kw=kw: e.dma_start(out=a, in_=b, **kw),
                  rd=(in_,), wr=(out,))

    def mm(self, out, lhsT, rhs, start=True, stop=True):
        self._add("tensor", 1,
                  lambda e, a=P.A(out), b=P.A(lhsT), c=P.A(rhs), s0=start, s1=stop:
                  e.matmul(a, b, c, start=s0, stop=s1),
                  rd=(lhsT, rhs, out), wr=(out,))

    def trans(self, out, in_, ident):
        self._add("tensor", 1,
                  lambda e, a=P.A(out), b=P.A(in_), c=P.A(ident): e.transpose(a, b, c),
                  rd=(in_, ident), wr=(out,))

    def act(self, out, in_, func, bias=0.0, scale=1.0):
        self._add("scalar", 1,
                  lambda e, a=P.A(out), b=P.A(in_), f=func, bi=P.A(bias), sc=P.A(scale):
                  e.activation(a, b, f, bias=bi, scale=sc),
                  rd=(in_, bias, scale), wr=(out,))

    def ts(self, out, in0, s1, s2, op0, op1=None, eng="vector"):
        if op1 is None:
            self._add(eng, 1,
                      lambda e, a=P.A(out), b=P.A(in0), x=P.A(s1), o0=op0:
                      e.tensor_scalar(a, b, x, None, o0),
                      rd=(in0, s1), wr=(out,))
        else:
            self._add(eng, 1,
                      lambda e, a=P.A(out), b=P.A(in0), x=P.A(s1), y=P.A(s2), o0=op0, o1=op1:
                      e.tensor_scalar(a, b, x, y, o0, o1),
                      rd=(in0, s1, s2), wr=(out,))

    def tt(self, out, in0, in1, op, eng="vector"):
        self._add(eng, 1,
                  lambda e, a=P.A(out), b=P.A(in0), c=P.A(in1), o=op:
                  e.tensor_tensor(a, b, c, o),
                  rd=(in0, in1), wr=(out,))

    def stt(self, out, in0, scalar, in1, op0, op1, eng="vector"):
        self._add(eng, 1,
                  lambda e, a=P.A(out), b=P.A(in0), s=P.A(scalar), c=P.A(in1), o0=op0, o1=op1:
                  e.scalar_tensor_tensor(a, b, s, c, o0, o1),
                  rd=(in0, scalar, in1), wr=(out,))

    def red(self, out, in_, op, absval=False, eng="vector"):
        self._add(eng, 1,
                  lambda e, a=P.A(out), b=P.A(in_), o=op, v=absval:
                  e.tensor_reduce(a, b, AX.X, o, apply_absolute_value=v),
                  rd=(in_,), wr=(out,))

    def copy(self, out, in_, eng="vector"):
        self._add(eng, 1,
                  lambda e, a=P.A(out), b=P.A(in_): e.tensor_copy(a, b),
                  rd=(in_,), wr=(out,))

    def recip(self, out, in_):
        self._add("vector", 1,
                  lambda e, a=P.A(out), b=P.A(in_): e.reciprocal(a, b),
                  rd=(in_,), wr=(out,))

    def memset(self, ap, v, eng="gpsimd"):
        self._add(eng, 1, lambda e, a=P.A(ap), x=v: e.memset(a, x), wr=(ap,))

    def identity_fill(self, ap, n):
        self.memset(ap, 0.0)
        self._add("gpsimd", 1, lambda e, a=P.A(ap), m=n: e.affine_select(
            out=a, in_=a, compare_op=ALU.not_equal, fill=1.0,
            base=0, pattern=[[-1, m]], channel_multiplier=1),
            rd=(ap,), wr=(ap,))

    def sw_pair_write(self, sh, w4_ap):
        w4 = P.A(w4_ap)

        def fn(e, sem, inc):
            for arm in e.switch_core_id(n=128):
                role = int(arm) % 2
                e.dma_start(out=sh[role * 4:(role + 1) * 4, :],
                            in_=w4[0:4, :]).then_inc(sem, inc)
        fn._takes_sem = True
        self._add("dgp", 16, fn, rd=(w4_ap,), wr=(sh,))

    def emit(self, ctxs):
        nc = self.nc
        steps = self.steps
        n = len(steps)
        last_w = {}
        readers = {}
        last_ser = {}
        afence = self._alloc_fence
        deps = [set() for _ in range(n)]
        for i, st in enumerate(steps):
            d = set(st["extra"])
            d.update(st["bar"].values())
            for t in st["rd"]:
                j = last_w.get(t)
                if j is not None:
                    d.add(j)
                f = afence.get(t)
                if f:
                    d.update(f.values())
            for t in st["wr"]:
                j = last_w.get(t)
                if j is not None:
                    d.add(j)
                for j in readers.get(t, ()):
                    d.add(j)
                f = afence.get(t)
                if f:
                    d.update(f.values())
            tag = st["ser"]
            if tag:
                j = last_ser.get(tag)
                if j is not None:
                    d.add(j)
                last_ser[tag] = i
            d.discard(i)
            deps[i] = d
            for t in st["wr"]:
                last_w[t] = i
                readers[t] = []
            for t in st["rd"]:
                readers.setdefault(t, []).append(i)
        for i, st in enumerate(steps):
            if st["ch"] == "tensor":
                deps[i] = {j for j in deps[i] if steps[j]["ch"] != "tensor"}
        needed = [st["ch"] in ("dsync", "dgp") for st in steps]
        for i in range(n):
            for j in deps[i]:
                needed[j] = True
        last_of = {}
        for i, st in enumerate(steps):
            last_of[st["ch"]] = i
        for ch, i in last_of.items():
            needed[i] = True
        inc_upto = {ch: 0 for ch in P.CHANNELS}
        count_at = [0] * n
        for i, st in enumerate(steps):
            if needed[i]:
                inc_upto[st["ch"]] += st["inc"]
            count_at[i] = inc_upto[st["ch"]]
        totals = dict(inc_upto)
        self.deps = deps
        self.count_at = count_at
        self.needed = needed

        sems = {ch: ctxs.enter_context(nc.semaphore(f"sem_{ch}"))
                for ch in P.CHANNELS}
        block = ctxs.enter_context(nc.Block())

        eng_of_ch = {"tensor": "tensor", "vector": "vector", "scalar": "scalar",
                     "gpsimd": "gpsimd", "dsync": "sync", "dgp": "gpsimd"}

        def body_for(eng_name):
            my_chs = [ch for ch, e in eng_of_ch.items() if e == eng_name]

            def body(eng):
                wm = {ch: 0 for ch in P.CHANNELS}
                for i, st in enumerate(steps):
                    if eng_of_ch[st["ch"]] != eng_name:
                        continue
                    req = {}
                    for j in deps[i]:
                        cj = steps[j]["ch"]
                        req[cj] = max(req.get(cj, 0), count_at[j])
                    for cj, v in sorted(req.items()):
                        if v > wm[cj]:
                            eng.wait_ge(sems[cj], v)
                            wm[cj] = v
                    if getattr(st["fn"], "_takes_sem", False):
                        st["fn"](eng, sems[st["ch"]], st["inc"])
                    else:
                        inst = st["fn"](eng)
                        if needed[i]:
                            inst.then_inc(sems[st["ch"]], st["inc"])
                if eng_name == "sync":
                    for ch in P.CHANNELS:
                        if totals[ch] > 0:
                            eng.wait_ge(sems[ch], totals[ch])
            return body

        block.sync(body_for("sync"))
        block.vector(body_for("vector"))
        block.scalar(body_for("scalar"))
        block.tensor(body_for("tensor"))
        block.gpsimd(body_for("gpsimd"))


def fm_slice(x_hbm, t0, tn, nchunk=EC):
    r = x_hbm.rearrange("(c p) l -> p c l", p=128)
    return r[:, 0:nchunk, t0:t0 + tn]


def fm_slice_one(x_hbm, t0, tn, c):
    r = x_hbm.rearrange("(c p) l -> p c l", p=128)
    return r[:, c:c + 1, t0:t0 + tn]


def tile_ap(sb, nchunk, tn):
    return sb.ap().rearrange("p (c t) -> p c t", t=tn)[:, 0:nchunk, :]


def row3(sb_ap, tn):
    return sb_ap.rearrange("p (o t) -> p o t", o=1)


def load_weight_bf16(p, sb_bf, w_hbm, kdim, mdim, stage):
    for kc in range(kdim // 128):
        p.dma(stage[:, 0:mdim], w_hbm[kc * 128:(kc + 1) * 128, :])
        p.copy(sb_bf[:, kc * mdim:(kc + 1) * mdim], stage[:, 0:mdim])


def ln_stats(p, src_bf, sq_bf, ones1, ps1, ps2, tn, mu, rstd, scr, nchunk=EC):
    inv_n = 1.0 / (nchunk * 128)
    p.act(sq_bf[:, 0:nchunk * tn], src_bf[:, 0:nchunk * tn], AF.Square)
    for c in range(nchunk):
        p.mm(ps1[0:1, 0:tn], ones1, src_bf[:, c * tn:(c + 1) * tn],
             start=(c == 0), stop=(c == nchunk - 1))
    for c in range(nchunk):
        p.mm(ps2[0:1, 0:tn], ones1, sq_bf[:, c * tn:(c + 1) * tn],
             start=(c == 0), stop=(c == nchunk - 1))
    p.ts(mu, ps1[0:1, 0:tn], inv_n, None, ALU.mult)
    p.tt(scr, mu, mu, ALU.mult)
    p.stt(rstd, ps2[0:1, 0:tn], inv_n, scr, ALU.mult, ALU.subtract)
    p.ts(rstd, rstd, LN_EPS, None, ALU.add)
    p.act(rstd, rstd, AF.Sqrt)
    p.recip(rstd, rstd)


def ln_stats3(p, src_bf, sq_full, ones1, ps1, ps2, tn, mu, rstd, scr_row,
              nchunk=EC):
    inv_n = 1.0 / (nchunk * 128)
    p.act(sq_full[:, 0:nchunk * tn], src_bf[:, 0:nchunk * tn], AF.Square)
    for c in range(nchunk):
        seg = slice(c * tn, (c + 1) * tn)
        p.mm(ps1[0:1, 0:tn], ones1, src_bf[:, seg], start=(c == 0),
             stop=(c == nchunk - 1))
        p.mm(ps2[0:1, 0:tn], ones1, sq_full[:, seg], start=(c == 0),
             stop=(c == nchunk - 1))
    p.ts(mu, ps1[0:1, 0:tn], inv_n, None, ALU.mult)
    p.tt(scr_row, mu, mu, ALU.mult)
    p.stt(rstd, ps2[0:1, 0:tn], inv_n, scr_row, ALU.mult, ALU.subtract)
    p.ts(rstd, rstd, LN_EPS, None, ALU.add)
    p.act(rstd, rstd, AF.Sqrt)
    p.recip(rstd, rstd)


def apply_ln(p, dst_bf, src_bf, ps_mu, ps_rs, scr_f, tn, g_vec, b_vec,
             nchunk=EC, eng="vector"):
    for c in range(nchunk):
        seg = slice(c * tn, (c + 1) * tn)
        p.tt(scr_f[:, 0:tn], src_bf[:, seg], ps_mu[:, 0:tn], ALU.subtract, eng=eng)
        p.tt(scr_f[:, 0:tn], scr_f[:, 0:tn], ps_rs[:, 0:tn], ALU.mult, eng=eng)
        p.ts(dst_bf[:, seg], scr_f[:, 0:tn], g_vec[:, c:c + 1],
             b_vec[:, c:c + 1], ALU.mult, ALU.add, eng=eng)


def quant_round(p, y_f32, i1, i2, tf, out_bf, n):
    p.copy(i1[:, 0:n], y_f32[:, 0:n])
    p.stt(i2[:, 0:n], y_f32[:, 0:n], -0.5, i1[:, 0:n], ALU.add, ALU.subtract)
    p.stt(tf[:, 0:n], i1[:, 0:n], -256.0, i2[:, 0:n], ALU.add, ALU.add)
    p.ts(out_bf[:, 0:n], tf[:, 0:n], 127.0, -128.0, ALU.min, ALU.max)


def fm_slice_off(x_hbm, t0, tn, c0, nchunk=EC):
    r = x_hbm.rearrange("(c p) l -> p c l", p=128)
    return r[:, c0:c0 + nchunk, t0:t0 + tn]


def load_weight_bf16_2q(p, sb_bf, w_hbm, kdim, mdim, stg0, stg1):
    for kc in range(kdim // 128):
        stg = stg0 if kc % 2 == 0 else stg1
        p.dma(stg[:, 0:mdim], w_hbm[kc * 128:(kc + 1) * 128, :],
              eng=("sync" if kc % 2 == 0 else "gpsimd"))
        p.copy(sb_bf[:, kc * mdim:(kc + 1) * mdim], stg[:, 0:mdim])


def ln_stats2(p, src_bf, sqs, ones1, ps1, ps2, tn, mu, rstd, scr_row, nchunk=EC):
    inv_n = 1.0 / (nchunk * 128)
    for c in range(nchunk):
        sq = sqs[c % 2]
        seg = slice(c * tn, (c + 1) * tn)
        p.act(sq[:, 0:tn], src_bf[:, seg], AF.Square)
        p.mm(ps1[0:1, 0:tn], ones1, src_bf[:, seg], start=(c == 0),
             stop=(c == nchunk - 1))
        p.mm(ps2[0:1, 0:tn], ones1, sq[:, 0:tn], start=(c == 0),
             stop=(c == nchunk - 1))
    p.ts(mu, ps1[0:1, 0:tn], inv_n, None, ALU.mult)
    p.tt(scr_row, mu, mu, ALU.mult)
    p.stt(rstd, ps2[0:1, 0:tn], inv_n, scr_row, ALU.mult, ALU.subtract)
    p.ts(rstd, rstd, LN_EPS, None, ALU.add)
    p.act(rstd, rstd, AF.Sqrt)
    p.recip(rstd, rstd)


def load_stats(p, stats_hbm, srow, ones128, ps_mu, ps_rs, t0, tn):
    p.dma(srow[0:1, 0:tn], stats_hbm[0:1, t0:t0 + tn])
    p.dma(srow[0:1, tn:2 * tn], stats_hbm[1:2, t0:t0 + tn])
    p.mm(ps_mu[:, 0:tn], ones128, srow[0:1, 0:tn])
    p.mm(ps_rs[:, 0:tn], ones128, srow[0:1, tn:2 * tn])


def bcast_sb(p, ps_mu, ps_rs, mu_sb, rs_sb, tn):
    p.act(mu_sb[:, 0:tn], ps_mu[:, 0:tn], AF.Copy)
    p.act(rs_sb[:, 0:tn], ps_rs[:, 0:tn], AF.Copy)


def xn_chunk(p, dst_f, xt, ps_mu, ps_rs, c, tn, eng="vector"):
    seg = slice(c * tn, (c + 1) * tn)
    p.tt(dst_f[:, 0:tn], xt[:, seg], ps_mu[:, 0:tn], ALU.subtract, eng=eng)
    p.tt(dst_f[:, 0:tn], dst_f[:, 0:tn], ps_rs[:, 0:tn], ALU.mult, eng=eng)


def cross_part_max(p, vmax, bounce_dram, row_sb, out_sb):
    p.dma(bounce_dram, vmax[:])
    p.dma(row_sb[0:1, 0:128], bounce_dram.rearrange("p c -> c p"))
    p.red(out_sb, row_sb[0:1, 0:128], ALU.max)


def make_scale(p, mx_row_sb, clip_hbm, sdr, ones128, ps4, rs_pp, s_pp):
    p.red(sdr[0:1, 0:1], mx_row_sb[0:1, 0:128], ALU.max)
    p.dma(sdr[0:1, 1:2], clip_hbm)
    p.act(sdr[0:1, 1:2], sdr[0:1, 1:2], AF.Sigmoid)
    p.tt(sdr[0:1, 0:1], sdr[0:1, 0:1], sdr[0:1, 1:2], ALU.mult)
    p.ts(sdr[0:1, 0:1], sdr[0:1, 0:1], 1.0 / 127.0, None, ALU.mult)
    p.recip(sdr[0:1, 2:3], sdr[0:1, 0:1])
    p.mm(ps4[:, 0:4], ones128, sdr[0:1, 0:4])
    p.copy(rs_pp[:, 0:1], ps4[:, 2:3])
    p.copy(s_pp[:, 0:1], ps4[:, 0:1])


def emit_sh_payload(p, bnc4, nonce4_in, w4, sh):
    p.memset(w4[:], 0.0, eng="vector")
    p.dma(w4[0:4, 0:1], bnc4[0:4, 0:1], allow_slow_non_contiguous=True)
    p.dma(w4[0:4, 3:5], nonce4_in)
    p.stt(w4[0:4, 1:2], w4[0:4, 0:1], 3.0, w4[0:4, 3:4], ALU.mult, ALU.add)
    p.stt(w4[0:4, 2:3], w4[0:4, 0:1], -5.0, w4[0:4, 4:5], ALU.mult, ALU.add)
    p.sw_pair_write(sh, w4)


def emit_sh_read(p, sh, nonce_sb, rd, msk, cnd, gm):
    p.dma(rd[0:1, 0:128],
          sh.rearrange("p c -> (p c)").rearrange("(o f) -> o f", o=1))
    r3 = rd.ap()[0:1, 0:128].rearrange("o (s c) -> o s c", c=16)
    vals, chk1, chk2 = r3[:, :, 0], r3[:, :, 1], r3[:, :, 2]
    p.ts(cnd[0:1, 8:16], vals, 3.0, None, ALU.mult)
    p.tt(cnd[0:1, 8:16], cnd[0:1, 8:16], nonce_sb[0:1, 0:8], ALU.add)
    p.tt(msk[0:1, 0:8], cnd.ap()[0:1, 8:16].bitcast(I32), chk1.bitcast(I32),
         ALU.is_equal)
    p.ts(cnd[0:1, 8:16], vals, -5.0, None, ALU.mult)
    p.tt(cnd[0:1, 8:16], cnd[0:1, 8:16], nonce_sb[0:1, 8:16], ALU.add)
    p.tt(msk[0:1, 8:16], cnd.ap()[0:1, 8:16].bitcast(I32), chk2.bitcast(I32),
         ALU.is_equal)
    p.tt(msk[0:1, 0:8], msk[0:1, 0:8], msk[0:1, 8:16], ALU.bitwise_and)
    p.copy(cnd[0:1, 0:8], msk[0:1, 0:8])
    p.tt(cnd[0:1, 0:8], cnd[0:1, 0:8], vals, ALU.mult)
    p.tt(cnd[0:1, 0:4], cnd[0:1, 0:4], cnd[0:1, 4:8], ALU.max)
    p.tt(cnd[0:1, 0:2], cnd[0:1, 0:2], cnd[0:1, 2:4], ALU.max)
    p.tt(gm, cnd[0:1, 0:1], cnd[0:1, 1:2], ALU.max)


def make_scale_g(p, gm, clip_hbm, sdr, ones128, ps4, rs_pp, s_pp):
    p.copy(sdr[0:1, 0:1], gm)
    p.dma(sdr[0:1, 1:2], clip_hbm)
    p.act(sdr[0:1, 1:2], sdr[0:1, 1:2], AF.Sigmoid)
    p.tt(sdr[0:1, 0:1], sdr[0:1, 0:1], sdr[0:1, 1:2], ALU.mult)
    p.ts(sdr[0:1, 0:1], sdr[0:1, 0:1], 1.0 / 127.0, None, ALU.mult)
    p.recip(sdr[0:1, 2:3], sdr[0:1, 0:1])
    p.mm(ps4[:, 0:4], ones128, sdr[0:1, 0:4])
    p.copy(rs_pp[:, 0:1], ps4[:, 2:3])
    p.copy(s_pp[:, 0:1], ps4[:, 0:1])


def build_L1():
    p = P()
    nc = p.nc
    x = p.din("x", [E, L])
    Wq = p.din("Wq", [E, E])
    Wv = p.din("Wv", [E, E])
    vecs = p.din("vecs", [128, 6 * EC])

    o_stats = p.dout("stats", [2, L])
    o_Q = p.dout("Qc", [E, L], BF16)
    o_V = p.dout("Vc", [E, L], BF16)
    o_qg = p.dout("qg", [128, EC])
    o_mx = p.dout("mx", [1, 16])
    bnc = p.dtmp("bnc", [128, 1])

    ctxs = ExitStack()
    with ctxs:
        sb = lambda n, s, d=F32: ctxs.enter_context(p.sbuf(n, s, d))
        ps = lambda n, s, d=F32: ctxs.enter_context(p.psum(n, s, d))

        stage = sb("stage", [128, E])
        wq_bf = sb("wq_bf", [128, EC * E], BF16)
        wv_bf = sb("wv_bf", [128, EC * E], BF16)
        vec = sb("vec", [128, 6 * EC])
        ones1 = sb("ones1", [128, 1], BF16)
        ones128 = sb("ones128", [1, 128], F32)
        xt = sb("xt", [128, EC * T])
        xn_f = sb("xn_f", [128, T])
        xn_bf = sb("xn_bf", [128, EC * T], BF16)
        sq_bf = sb("sq_bf", [128, EC * T], BF16)
        q_bf = sb("q_bf", [128, EC * T], BF16)
        out_bf = sb("out_bf", [128, EC * T], BF16)
        scr = sb("scr", [128, T])
        strow = sb("strow", [1, 5 * T])
        qsum = sb("qsum", [128, EC])
        vmax = sb("vmax", [128, 1])
        rtmp = sb("rtmp", [128, 1])
        mrow = sb("mrow", [1, 16])
        rrow = sb("rrow", [1, 128])
        ps_s1 = ps("ps_s1", [1, T])
        ps_s2 = ps("ps_s2", [1, T])
        ps_mu = ps("ps_mu", [128, T])
        ps_rs = ps("ps_rs", [128, T])
        ps_p = ps("ps_p", [128, T])

        p.memset(ones1[:], 1.0)
        p.memset(ones1f[:], 1.0)
        p.memset(ones128[:], 1.0)
        p.memset(qsum[:], 0.0)
        p.memset(vmax[:], 0.0)
        p.memset(mrow[:], 0.0)
        p.dma(vec[:], vecs)
        load_weight_bf16(p, wq_bf, Wq, E, E, stage)
        load_weight_bf16(p, wv_bf, Wv, E, E, stage)

        mu = strow[0:1, 0:T]
        rs = strow[0:1, T:2 * T]
        scr1 = strow[0:1, 2 * T:3 * T]
        mu2 = strow[0:1, 3 * T:4 * T]
        rs2 = strow[0:1, 4 * T:5 * T]

        for ti in range(NT):
            t0 = ti * T
            p.dma(tile_ap(xt, EC, T), fm_slice(x, t0, T))
            p.copy(xn_bf[:], xt[:])
            ln_stats(p, xn_bf, sq_bf, ones1, ps_s1, ps_s2, T, mu, rs, scr1)
            p.dma(o_stats[0:1, t0:t0 + T], mu)
            p.dma(o_stats[1:2, t0:t0 + T], rs)
            p.mm(ps_mu[:, 0:T], ones128, mu)
            p.mm(ps_rs[:, 0:T], ones128, rs)
            for c in range(EC):
                xn_chunk(p, xn_f, xt, ps_mu, ps_rs, c, T)
                p.red(rtmp, xn_f[:, 0:T], ALU.max, absval=True)
                p.tt(vmax, vmax, rtmp, ALU.max)
                p.copy(xn_bf[:, c * T:(c + 1) * T], xn_f[:, 0:T])

            for (w_bf, b0, hbm, do_qsum) in [(wq_bf, 0, o_Q, True),
                                             (wv_bf, 3, o_V, False)]:
                for mc in range(EC):
                    for kc in range(EC):
                        p.mm(ps_p[:, 0:T],
                             w_bf[:, kc * E + mc * 128: kc * E + mc * 128 + 128],
                             xn_bf[:, kc * T:(kc + 1) * T],
                             start=(kc == 0), stop=(kc == EC - 1))
                    p.ts(q_bf[:, mc * T:(mc + 1) * T], ps_p[:, 0:T],
                         vec[:, b0 * EC + mc: b0 * EC + mc + 1], None, ALU.add)
                ln_stats(p, q_bf, sq_bf, ones1, ps_s1, ps_s2, T, mu2, rs2, scr1)
                p.mm(ps_mu[:, 0:T], ones128, mu2)
                p.mm(ps_rs[:, 0:T], ones128, rs2)
                apply_ln(p, out_bf, q_bf, ps_mu, ps_rs, scr, T,
                         g_vec=vec[:, (b0 + 1) * EC:(b0 + 2) * EC],
                         b_vec=vec[:, (b0 + 2) * EC:(b0 + 3) * EC])
                p.dma(fm_slice(hbm, t0, T), tile_ap(out_bf, EC, T), eng="gpsimd")
                if do_qsum:
                    for c in range(EC):
                        p.red(rtmp, out_bf[:, c * T:(c + 1) * T], ALU.add)
                        p.tt(qsum[:, c:c + 1], qsum[:, c:c + 1], rtmp, ALU.add)

        p.ts(qsum[:], qsum[:], 1.0 / (L * np.sqrt(D)), None, ALU.mult)
        p.dma(o_qg, qsum[:])
        cross_part_max(p, vmax, bnc, rrow, mrow[0:1, 0:1])
        p.dma(o_mx, mrow[:])
        p.emit(ctxs)
    return p


def build_L2():
    p = P()
    nc = p.nc
    x = p.din("x", [E, L])
    stats = p.din("stats", [2, L])
    qg = p.din("qg", [128, EC])
    mxs = p.din("mxs", [1, 128])
    clip_x = p.din("clip_x", [1, 1])
    Wk = p.din("Wk", [E, E])
    Wpk = p.din("Wpk", [E, E])
    veck = p.din("veck", [128, 4 * EC])
    bpk_row = p.din("bpk_row", [1, E])
    w_gate = p.din("w_gate", [IF, E])
    w_up = p.din("w_up", [IF, E])
    Vc = p.din("Vc", [E, L], BF16)

    o_h2 = p.dout("h2c", [IF, L], BF16)
    o_kv = p.dout("kv", [128, H * 32])
    o_ks = p.dout("ks", [16, E])
    o_mxh = p.dout("mxh", [1, 16])
    Kc = p.dtmp("Kc", [E, L], BF16)
    bnc = p.dtmp("bnc", [128, 1])
    bnc16 = p.dtmp("bnc16", [16, 1])

    ctxs = ExitStack()
    with ctxs:
        sb = lambda n, s, d=F32: ctxs.enter_context(p.sbuf(n, s, d))
        ps = lambda n, s, d=F32: ctxs.enter_context(p.psum(n, s, d))

        stage = sb("stage", [128, E])
        ones1 = sb("ones1", [128, 1], BF16)
        ones128 = sb("ones128", [1, 128], F32)
        ident_b = sb("ident_b", [128, 128], BF16)
        ident_f = sb("ident_f", [128, 128], F32)
        xt = sb("xt", [128, EC * T])
        scr = sb("scr", [128, T])
        strow = sb("strow", [1, 5 * T])
        scores = sb("scores", [16, L])
        srow16 = sb("srow16", [16, 4])
        hmax = sb("hmax", [128, 1])
        rtmp = sb("rtmp", [128, 1])
        mrow = sb("mrow", [1, 16])
        rrow = sb("rrow", [1, 128])
        sdr = sb("sdr", [1, 4])
        rsx_pp = sb("rsx_pp", [128, 1])
        sx_pp = sb("sx_pp", [128, 1])
        a_tm_f = sb("a_tm_f", [128, (L // 128) * 16])
        a_tm_b = sb("a_tm_b", [128, (L // 128) * 16], BF16)
        a0r_b = sb("a0r_b", [1, 16], BF16)
        pk0_b = sb("pk0_b", [1, E], BF16)
        pk0_f = sb("pk0_f", [1, E])

        mu2 = strow[0:1, 3 * T:4 * T]
        rs2 = strow[0:1, 4 * T:5 * T]
        scr1 = strow[0:1, 2 * T:3 * T]

        p.memset(ones1[:], 1.0)
        p.memset(ones1f[:], 1.0)
        p.memset(ones128[:], 1.0)
        p.memset(hmax[:], 0.0)
        p.memset(mrow[:], 0.0)
        p.identity_fill(ident_b[:], 128)
        p.identity_fill(ident_f[:], 128)

        with ExitStack() as pha:
            sba = lambda n, s, d=F32: pha.enter_context(p.sbuf(n, s, d))
            psa = lambda n, s, d=F32: pha.enter_context(p.psum(n, s, d))
            wgT = sba("wgT", [128, IC * EC * 128], BF16)
            wuT = sba("wuT", [128, IC * EC * 128], BF16)
            sg = sba("sg", [128, IC])
            su = sba("su", [128, IC])
            xq_bf = sba("xq_bf", [128, EC * T], BF16)
            yq = sba("yq", [128, E])
            i1 = sba("i1", [128, E], I32)
            i2 = sba("i2", [128, E], I32)
            tfq = sba("tfq", [128, E])
            wi_b = sba("wi_b", [128, E], BF16)
            gs = sba("gs", [128, T])
            us = sba("us", [128, T])
            h2f = sba("h2f", [128, T])
            h2b = sba("h2b", [128, T], BF16)
            rm = sba("rm", [128, 1])
            ps_mu = psa("ps_mu", [128, T])
            ps_rs = psa("ps_rs", [128, T])
            ps_g = psa("ps_g", [128, T])
            ps_4 = psa("ps_4", [128, 4])
            psq = psa("psq", [128, 128], BF16)

            p.dma(rrow[0:1, 0:128], mxs)
            make_scale(p, rrow, clip_x, sdr, ones128, ps_4, rsx_pp, sx_pp)

            for (w_hbm, wT, sc) in [(w_gate, wgT, sg), (w_up, wuT, su)]:
                for ic in range(IC):
                    p.dma(stage[:, 0:E], w_hbm[ic * 128:(ic + 1) * 128, :])
                    p.red(rm, stage[:, 0:E], ALU.max, absval=True)
                    p.ts(sc[:, ic:ic + 1], rm, 1.0 / 127.0, None, ALU.mult)
                    p.recip(rm, rm)
                    p.ts(rm, rm, 127.0, None, ALU.mult)
                    p.act(yq[:, 0:E], stage[:, 0:E], AF.Copy, bias=256.5, scale=rm)
                    quant_round(p, yq, i1, i2, tfq, wi_b, E)
                    for kc in range(EC):
                        p.trans(psq[:, 0:128], wi_b[:, kc * 128:(kc + 1) * 128], ident_b[:])
                        p.copy(wT[:, (ic * EC + kc) * 128:(ic * EC + kc) * 128 + 128],
                               psq[:, 0:128])
                for ic in range(IC):
                    p.tt(sc[:, ic:ic + 1], sc[:, ic:ic + 1], sx_pp[:, 0:1], ALU.mult)

            for ti in range(NT):
                t0 = ti * T
                p.dma(tile_ap(xt, EC, T), fm_slice(x, t0, T))
                load_stats(p, stats, strow, ones128, ps_mu, ps_rs, t0, T)
                for c in range(EC):
                    xn_chunk(p, scr, xt, ps_mu, ps_rs, c, T)
                    p.act(yq[:, 0:T], scr[:, 0:T], AF.Copy, bias=256.5,
                          scale=rsx_pp[:, 0:1])
                    quant_round(p, yq, i1, i2, tfq, h2b, T)
                    p.copy(xq_bf[:, c * T:(c + 1) * T], h2b[:, 0:T])
                for ic in range(IC):
                    for kc in range(EC):
                        p.mm(ps_g[:, 0:T],
                             wgT[:, (ic * EC + kc) * 128:(ic * EC + kc) * 128 + 128],
                             xq_bf[:, kc * T:(kc + 1) * T],
                             start=(kc == 0), stop=(kc == EC - 1))
                    p.act(gs[:, 0:T], ps_g[:, 0:T], AF.Silu, scale=sg[:, ic:ic + 1])
                    for kc in range(EC):
                        p.mm(ps_g[:, 0:T],
                             wuT[:, (ic * EC + kc) * 128:(ic * EC + kc) * 128 + 128],
                             xq_bf[:, kc * T:(kc + 1) * T],
                             start=(kc == 0), stop=(kc == EC - 1))
                    p.ts(us[:, 0:T], ps_g[:, 0:T], su[:, ic:ic + 1], None, ALU.mult)
                    p.tt(h2f[:, 0:T], gs[:, 0:T], us[:, 0:T], ALU.mult)
                    p.copy(h2b[:, 0:T], h2f[:, 0:T])
                    p.dma(fm_slice_one(o_h2, t0, T, ic), row3(h2b.ap()[:, 0:T], T),
                          eng="gpsimd")
                    p.red(rtmp, h2f[:, 0:T], ALU.max, absval=True)
                    p.tt(hmax, hmax, rtmp, ALU.max)

            cross_part_max(p, hmax, bnc, rrow, mrow[0:1, 0:1])
            p.dma(o_mxh, mrow[:])

        with ExitStack() as phb:
            sbb = lambda n, s, d=F32: phb.enter_context(p.sbuf(n, s, d))
            psb = lambda n, s, d=F32: phb.enter_context(p.psum(n, s, d))
            wk_bf = sbb("wk_bf", [128, EC * E], BF16)
            vec = sbb("vec", [128, 4 * EC])
            qsl = sbb("qsl", [128, EC * 16], BF16)
            qg_sb = sbb("qg_sb", [128, EC])
            xn_bf = sbb("xn_bf", [128, EC * T], BF16)
            k_bf = sbb("k_bf", [128, EC * T], BF16)
            sq_bf = sbb("sq_bf", [128, EC * T], BF16)
            out_bf = sbb("out_bf", [128, EC * T], BF16)
            ps_s1 = psb("ps_s1", [1, T])
            ps_s2 = psb("ps_s2", [1, T])
            ps_mu = psb("ps_mu", [128, T])
            ps_rs = psb("ps_rs", [128, T])
            ps_p = psb("ps_p", [128, T])
            ps_sc = psb("ps_sc", [16, T])

            p.dma(vec[:], veck)
            load_weight_bf16(p, wk_bf, Wk, E, E, stage)
            p.dma(qg_sb[:], qg)
            p.memset(qsl[:], 0.0)
            for c in range(EC):
                p.copy(qsl[0:64, c * 16 + 2 * c: c * 16 + 2 * c + 1],
                       qg_sb[0:64, c:c + 1])
                p.copy(qsl[64:128, c * 16 + 2 * c + 1: c * 16 + 2 * c + 2],
                       qg_sb[64:128, c:c + 1])
            p.set_serial(False)

            for ti in range(NT):
                t0 = ti * T
                p.dma(tile_ap(xt, EC, T), fm_slice(x, t0, T))
                load_stats(p, stats, strow, ones128, ps_mu, ps_rs, t0, T)
                for c in range(EC):
                    xn_chunk(p, scr, xt, ps_mu, ps_rs, c, T)
                    p.copy(xn_bf[:, c * T:(c + 1) * T], scr[:, 0:T])
                for mc in range(EC):
                    for kc in range(EC):
                        p.mm(ps_p[:, 0:T],
                             wk_bf[:, kc * E + mc * 128: kc * E + mc * 128 + 128],
                             xn_bf[:, kc * T:(kc + 1) * T],
                             start=(kc == 0), stop=(kc == EC - 1))
                    p.ts(k_bf[:, mc * T:(mc + 1) * T], ps_p[:, 0:T],
                         vec[:, mc:mc + 1], None, ALU.add)
                ln_stats(p, k_bf, sq_bf, ones1, ps_s1, ps_s2, T, mu2, rs2, scr1)
                p.mm(ps_mu[:, 0:T], ones128, mu2)
                p.mm(ps_rs[:, 0:T], ones128, rs2)
                apply_ln(p, out_bf, k_bf, ps_mu, ps_rs, scr, T,
                         g_vec=vec[:, EC:2 * EC], b_vec=vec[:, 2 * EC:3 * EC])
                p.dma(fm_slice(Kc, t0, T), tile_ap(out_bf, EC, T), eng="gpsimd")
                for kc in range(EC):
                    p.mm(ps_sc[0:16, 0:T], qsl[:, kc * 16:(kc + 1) * 16],
                         out_bf[:, kc * T:(kc + 1) * T],
                         start=(kc == 0), stop=(kc == EC - 1))
                p.copy(scores[0:16, t0:t0 + T], ps_sc[0:16, 0:T])

        p.red(srow16[0:16, 0:1], scores[:], ALU.max)
        p.ts(srow16[0:16, 0:1], srow16[0:16, 0:1], 0.0, None, ALU.max)
        p.ts(srow16[0:16, 1:2], srow16[0:16, 0:1], -1.0, None, ALU.mult)
        p.act(scores[:], scores[:], AF.Exp, bias=srow16[0:16, 1:2])
        p.red(srow16[0:16, 2:3], scores[:], ALU.add)
        p.act(srow16[0:16, 3:4], srow16[0:16, 1:2], AF.Exp)
        p.tt(srow16[0:16, 2:3], srow16[0:16, 2:3], srow16[0:16, 3:4], ALU.add)
        p.recip(srow16[0:16, 2:3], srow16[0:16, 2:3])
        p.ts(srow16[0:16, 2:3], srow16[0:16, 2:3], float(L), None, ALU.mult)
        p.ts(scores[:], scores[:], srow16[0:16, 2:3], None, ALU.mult)
        p.tt(srow16[0:16, 3:4], srow16[0:16, 3:4], srow16[0:16, 2:3], ALU.mult)
        p.dma(bnc16, srow16[0:16, 3:4])
        p.dma(rrow[0:1, 0:16], bnc16.rearrange("p c -> c p"))
        p.copy(a0r_b[0:1, 0:16], rrow[0:1, 0:16])
        with ExitStack() as phs:
            ptr_f = phs.enter_context(p.psum("ptr_f", [128, 16], F32))
            for tc in range(L // 128):
                p.trans(ptr_f[:, 0:16], scores[0:16, tc * 128:(tc + 1) * 128],
                        ident_f[0:16, 0:16])
                p.copy(a_tm_f[:, tc * 16:(tc + 1) * 16], ptr_f[:, 0:16])
        p.copy(a_tm_b[:], a_tm_f[:])
        p.dma(pk0_f[0:1, 0:E], bpk_row)
        p.act(pk0_f[0:1, 0:E], pk0_f[0:1, 0:E], AF.Tanh)
        p.ts(pk0_b[0:1, 0:E], pk0_f[0:1, 0:E], 1.0, None, ALU.add)

        with ExitStack() as phc:
            sbc = lambda n, s, d=F32: phc.enter_context(p.sbuf(n, s, d))
            psc = lambda n, s, d=F32: phc.enter_context(p.psum(n, s, d))
            wpk_bf = sbc("wpk_bf", [128, EC * E], BF16)
            vec = sbc("vec2", [128, 4 * EC])
            k_bf = sbc("k_bf2", [128, EC * T], BF16)
            v_bf = sbc("v_bf", [128, EC * T], BF16)
            pk_bf = sbc("pk_bf", [128, EC * T], BF16)
            pkt_b = sbc("pkt_b", [128, E], BF16)
            vst_b = sbc("vst_b", [128, E], BF16)
            kvout = sbc("kvout", [128, H * 32])
            ksout = sbc("ksout", [16, E])
            ps_p = psc("ps_p3", [128, T])
            ptr1 = psc("ptr1", [128, 1024], BF16)
            ptr2 = psc("ptr2", [128, 1024], BF16)
            ps_kv = psc("ps_kv", [128, H * 32])
            ps_ks = psc("ps_ks", [16, E])

            p.dma(vec[:], veck)
            load_weight_bf16(p, wpk_bf, Wpk, E, E, stage)
            p.memset(ps_kv[:], 0.0, eng="vector")
            p.memset(ps_ks[0:16, :], 0.0, eng="vector")
            for c in range(EC):
                p.mm(ps_ks[0:16, c * 128:(c + 1) * 128], a0r_b[0:1, 0:16],
                     pk0_b[0:1, c * 128:(c + 1) * 128], start=False, stop=False)
            p.set_serial(False)

            for ti in range(NT):
                t0 = ti * T
                p.dma(tile_ap(k_bf, EC, T), fm_slice(Kc, t0, T))
                p.dma(tile_ap(v_bf, EC, T), fm_slice(Vc, t0, T))
                for mc in range(EC):
                    for kc in range(EC):
                        p.mm(ps_p[:, 0:T],
                             wpk_bf[:, kc * E + mc * 128: kc * E + mc * 128 + 128],
                             k_bf[:, kc * T:(kc + 1) * T],
                             start=(kc == 0), stop=(kc == EC - 1))
                    p.act(scr[:, 0:T], ps_p[:, 0:T], AF.Tanh,
                          bias=vec[:, 3 * EC + mc:3 * EC + mc + 1])
                    p.ts(pk_bf[:, mc * T:(mc + 1) * T], scr[:, 0:T], 1.0, None,
                         ALU.add)
                for tc in range(TC):
                    gtc = ti * TC + tc
                    last = (ti == NT - 1 and tc == TC - 1)
                    for c in range(EC):
                        p.trans(ptr1[:, 0:128], pk_bf[:, c * T + tc * 128: c * T + tc * 128 + 128],
                                ident_b[:])
                        p.copy(pkt_b[:, c * 128:(c + 1) * 128], ptr1[:, 0:128])
                        p.trans(ptr2[:, 0:128], v_bf[:, c * T + tc * 128: c * T + tc * 128 + 128],
                                ident_b[:])
                        p.ts(vst_b[:, c * 128:c * 128 + 64], ptr2[:, 0:64],
                             a_tm_f[:, gtc * 16 + 2 * c: gtc * 16 + 2 * c + 1],
                             None, ALU.mult)
                        p.ts(vst_b[:, c * 128 + 64:c * 128 + 128], ptr2[:, 64:128],
                             a_tm_f[:, gtc * 16 + 2 * c + 1: gtc * 16 + 2 * c + 2],
                             None, ALU.mult)
                    for c in range(EC):
                        p.mm(ps_ks[0:16, c * 128:(c + 1) * 128],
                             a_tm_b[:, gtc * 16:(gtc + 1) * 16],
                             pkt_b[:, c * 128:(c + 1) * 128], start=False, stop=last)
                    for h in range(H):
                        c, r = h // 2, h % 2
                        p.mm(ps_kv[r * 64:(r + 1) * 64, c * 64:(c + 1) * 64],
                             pkt_b[:, h * 64:(h + 1) * 64],
                             vst_b[:, h * 64:(h + 1) * 64],
                             start=False, stop=last)

            p.copy(kvout[:], ps_kv[:])
            p.dma(o_kv, kvout[:])
            p.copy(ksout[0:16, :], ps_ks[0:16, :])
            p.dma(o_ks, ksout[0:16, :])

        p.emit(ctxs)
    return p


def build_L3():
    p = P()
    nc = p.nc
    x = p.din("x", [E, L])
    stats = p.din("stats", [2, L])
    Qc = p.din("Qc", [E, L], BF16)
    kv = p.din("kv", [128, H * 32])
    ks = p.din("ks", [16, E])
    Wpq = p.din("Wpq", [E, E])
    veca = p.din("veca", [128, 4 * EC])
    cwT = p.din("cwT", [5 * 64, E])
    gamma = p.din("gamma", [1, 1])
    W1 = p.din("W1", [E, MLP])
    b1 = p.din("b1", [128, MC])
    W2 = p.din("W2", [MLP, E])
    b2 = p.din("b2", [128, EC])
    w_down = p.din("w_down", [E, IF])
    clip_down = p.din("clip_down", [1, 1])
    mxh_all = p.din("mxh_all", [1, 128])
    h2c = p.din("h2c", [IF, L], BF16)
    sel = p.din("sel", [EC * 16, 128])

    o_out = p.dout("out", [E, L])
    attn2c = p.dtmp("attn2c", [E, L], BF16)
    h1c = p.dtmp("h1c", [MLP, L], BF16)
    s_c = p.dtmp("s_c", [E, L], BF16)

    ctxs = ExitStack()
    with ctxs:
        sb = lambda n, s, d=F32: ctxs.enter_context(p.sbuf(n, s, d))

        stage = sb("stage", [128, MLP])
        vec = sb("vec", [128, 4 * EC])
        ones1 = sb("ones1", [128, 1], BF16)
        ones128 = sb("ones128", [1, 128], F32)
        ident_b = sb("ident_b", [128, 128], BF16)
        xt = sb("xt", [128, EC * T])
        scr = sb("scr", [128, T])
        scr2 = sb("scr2", [128, T])
        strow = sb("strow", [1, 5 * T])

        mu2 = strow[0:1, 3 * T:4 * T]
        rs2 = strow[0:1, 4 * T:5 * T]
        scr1 = strow[0:1, 2 * T:3 * T]

        p.memset(ones1[:], 1.0)
        p.memset(ones1f[:], 1.0)
        p.memset(ones128[:], 1.0)
        p.identity_fill(ident_b[:], 128)
        p.dma(vec[:], veca)

        with ExitStack() as pha:
            sba = lambda n, s, d=F32: pha.enter_context(p.sbuf(n, s, d))
            psa = lambda n, s, d=F32: pha.enter_context(p.psum(n, s, d))
            wpq_bf = sba("wpq_bf", [128, EC * E], BF16)
            numT = sba("numT", [128, EC * 128], BF16)
            denT = sba("denT", [128, EC * 16], BF16)
            sel_b = sba("sel_b", [16, EC * 128], BF16)
            cvT = sba("cvT", [128, EC * 5 * 128], BF16)
            kv_f = sba("kv_f", [128, H * 32])
            kv_b = sba("kv_b", [128, H * 32], BF16)
            ksf = sba("ksf", [16, E])
            ks_b = sba("ks_b", [16, E], BF16)
            selt = sba("selt", [128, 128])
            q_bf = sba("q_bf", [128, EC * T], BF16)
            sq_bf = sba("sq_bf", [128, EC * T], BF16)
            pq_bf = sba("pq_bf", [128, EC * T], BF16)
            xn_f = sba("xn_f", [128, EC * T])
            attn_a = sba("attn_a", [128, EC * T], BF16)
            attn_b_ = sba("attn_b", [128, EC * T], BF16)
            tail2 = sba("tail2", [128, EC * 2], BF16)
            conv_win = sba("conv_win", [128, EC * (T + 4)], BF16)
            attn2_bf = sba("attn2_bf", [128, EC * T], BF16)
            rden = sba("rden", [16, T])
            rden_b = sba("rden_b", [16, T], BF16)
            rdb = sba("rdb", [128, T])
            gam_pp = sba("gam_pp", [128, 1])
            gcb = sba("gcb", [128, EC])
            ps_p = psa("ps_p", [128, T])
            ps_den = psa("ps_den", [16, T])
            ps_a = psa("ps_a", [128, T])
            ps_rdb = psa("ps_rdb", [128, T])

            load_weight_bf16(p, wpq_bf, Wpq, E, E, stage)
            for c in range(EC):
                p.dma(selt[0:16, 0:128], sel[c * 16:(c + 1) * 16, :])
                p.copy(sel_b[0:16, c * 128:(c + 1) * 128], selt[0:16, 0:128])
            p.memset(cvT[:], 0.0)
            for c in range(EC):
                for k in range(5):
                    col = (c * 5 + k) * 128
                    p.dma(selt[0:64, 0:64],
                          cwT[k * 64:(k + 1) * 64, 2 * c * 64:(2 * c + 1) * 64])
                    p.copy(cvT[0:64, col:col + 64], selt[0:64, 0:64])
                    p.dma(selt[64:128, 0:64],
                          cwT[k * 64:(k + 1) * 64, (2 * c + 1) * 64:(2 * c + 2) * 64])
                    p.copy(cvT[64:128, col + 64:col + 128], selt[64:128, 0:64])
            p.dma(kv_f[:], kv)
            p.copy(kv_b[:], kv_f[:])
            p.memset(numT[:], 0.0)
            for c in range(EC):
                p.copy(numT[0:64, c * 128: c * 128 + 64],
                       kv_b[0:64, c * 64:(c + 1) * 64])
                p.copy(numT[64:128, c * 128 + 64: c * 128 + 128],
                       kv_b[64:128, c * 64:(c + 1) * 64])
            p.dma(ksf[0:16, :], ks)
            p.copy(ks_b[0:16, :], ksf[0:16, :])
            p.memset(denT[:], 0.0)
            with ExitStack() as pht:
                ptr_b = pht.enter_context(p.psum("ptr_b", [128, 16], BF16))
                ps_t4 = pht.enter_context(p.psum("ps_t4", [128, 4], F32))
                p.dma(strow[0:1, 0:1], gamma)
                p.mm(ps_t4[:, 0:1], ones128, strow[0:1, 0:1])
                p.copy(gam_pp[:, 0:1], ps_t4[:, 0:1])
                p.ts(gcb[:], vec[:, 3 * EC:4 * EC], gam_pp[:, 0:1], None, ALU.mult)
                for c in range(EC):
                    p.trans(ptr_b[:, 0:16], ks_b[0:16, c * 128:(c + 1) * 128],
                            ident_b[0:16, 0:16])
                    p.copy(denT[0:64, c * 16 + 2 * c: c * 16 + 2 * c + 1],
                           ptr_b[0:64, 2 * c:2 * c + 1])
                    p.copy(denT[64:128, c * 16 + 2 * c + 1: c * 16 + 2 * c + 2],
                           ptr_b[64:128, 2 * c + 1:2 * c + 2])
            p.barrier()
            p.set_serial(False)
            ps_s1 = psa("ps_s1", [1, T])
            ps_s2 = psa("ps_s2", [1, T])
            ps_mu = psa("ps_mu", [128, T])
            ps_rs = psa("ps_rs", [128, T])

            def conv_ln_store(ti_prev, a_cur, a_prev):
                if "S4" in DBG_SER:
                    p.set_serial(True)
                t0p = ti_prev * T
                for c in range(EC):
                    w0 = c * (T + 4)
                    if ti_prev == 0:
                        p.memset(conv_win[:, w0:w0 + 2], 0.0, eng="vector")
                    else:
                        p.copy(conv_win[:, w0:w0 + 2], tail2[:, c * 2:(c + 1) * 2])
                    p.copy(conv_win[:, w0 + 2:w0 + 2 + T],
                           a_prev[:, c * T:(c + 1) * T])
                    if ti_prev == NT - 1:
                        p.memset(conv_win[:, w0 + 2 + T:w0 + 4 + T], 0.0, eng="vector")
                    else:
                        p.copy(conv_win[:, w0 + 2 + T:w0 + 4 + T],
                               a_cur[:, c * T:c * T + 2])
                    p.copy(tail2[:, c * 2:(c + 1) * 2],
                           a_prev[:, (c + 1) * T - 2:(c + 1) * T])
                for c in range(EC):
                    for k in range(5):
                        p.mm(ps_a[:, 0:T],
                             cvT[:, (c * 5 + k) * 128:(c * 5 + k) * 128 + 128],
                             conv_win[:, c * (T + 4) + k: c * (T + 4) + k + T],
                             start=(k == 0), stop=(k == 4))
                    p.ts(scr[:, 0:T], ps_a[:, 0:T], gam_pp[:, 0:1],
                         gcb[:, c:c + 1], ALU.mult, ALU.add)
                    p.tt(xn_f[:, c * T:(c + 1) * T], scr[:, 0:T],
                         a_prev[:, c * T:(c + 1) * T], ALU.add)
                p.copy(sq_bf[:], xn_f[:])
                ln_stats(p, sq_bf, q_bf, ones1, ps_s1, ps_s2, T, mu2, rs2, scr1)
                p.mm(ps_mu[:, 0:T], ones128, mu2)
                p.mm(ps_rs[:, 0:T], ones128, rs2)
                apply_ln(p, attn2_bf, sq_bf, ps_mu, ps_rs, scr, T,
                         g_vec=vec[:, EC:2 * EC], b_vec=vec[:, 2 * EC:3 * EC])
                p.dma(fm_slice(attn2c, t0p, T), tile_ap(attn2_bf, EC, T),
                      eng="gpsimd")
                if "S4" in DBG_SER:
                    p.set_serial(False)

            for ti in range(NT):
                t0 = ti * T
                a_cur = attn_a if ti % 2 == 0 else attn_b_
                a_prev = attn_b_ if ti % 2 == 0 else attn_a
                if "S1" in DBG_SER:
                    p.set_serial(True)
                p.dma(tile_ap(q_bf, EC, T), fm_slice(Qc, t0, T))
                for mc in range(EC):
                    for kc in range(EC):
                        p.mm(ps_p[:, 0:T],
                             wpq_bf[:, kc * E + mc * 128: kc * E + mc * 128 + 128],
                             q_bf[:, kc * T:(kc + 1) * T],
                             start=(kc == 0), stop=(kc == EC - 1))
                    p.act(scr[:, 0:T], ps_p[:, 0:T], AF.Tanh, bias=vec[:, mc:mc + 1])
                    p.ts(pq_bf[:, mc * T:(mc + 1) * T], scr[:, 0:T], 1.0, None,
                         ALU.add)
                if "S1" in DBG_SER:
                    p.set_serial(False)
                if "S2" in DBG_SER:
                    p.set_serial(True)
                for c in range(EC):
                    p.mm(ps_den[0:16, 0:T], denT[:, c * 16:(c + 1) * 16],
                         pq_bf[:, c * T:(c + 1) * T],
                         start=(c == 0), stop=(c == EC - 1))
                p.ts(rden[0:16, 0:T], ps_den[0:16, 0:T], DEN_EPS, None, ALU.add)
                p.recip(rden[0:16, 0:T], rden[0:16, 0:T])
                p.copy(rden_b[0:16, 0:T], rden[0:16, 0:T])
                if "S2" in DBG_SER:
                    p.set_serial(False)
                if "S3" in DBG_SER:
                    p.set_serial(True)
                for c in range(EC):
                    p.mm(ps_a[:, 0:T], numT[:, c * 128:(c + 1) * 128],
                         pq_bf[:, c * T:(c + 1) * T])
                    p.mm(ps_rdb[:, 0:T], sel_b[0:16, c * 128:(c + 1) * 128],
                         rden_b[0:16, 0:T])
                    p.copy(rdb[:, 0:T], ps_rdb[:, 0:T])
                    p.tt(a_cur[:, c * T:(c + 1) * T], ps_a[:, 0:T], rdb[:, 0:T],
                         ALU.mult)
                if "S3" in DBG_SER:
                    p.set_serial(False)
                if ti > 0:
                    conv_ln_store(ti - 1, a_cur, a_prev)
            last_buf = attn_b_ if (NT - 1) % 2 == 1 else attn_a
            conv_ln_store(NT - 1, last_buf, last_buf)

        with ExitStack() as phb:
            w1_bf = phb.enter_context(p.sbuf("w1_bf", [128, EC * MLP], BF16))
            ps_p = phb.enter_context(p.psum("ps_pb", [128, T], F32))
            h1_bf = phb.enter_context(p.sbuf("h1_bf", [128, T], BF16))
            a2_bf = phb.enter_context(p.sbuf("a2b_bf", [128, EC * T], BF16))
            b1_sb = phb.enter_context(p.sbuf("b1_sb", [128, MC], F32))
            p.dma(b1_sb[:], b1)
            load_weight_bf16_2q(p, w1_bf, W1, E, MLP, stage, stage2)
            for ti in range(NT):
                t0 = ti * T
                p.dma(tile_ap(a2_bf, EC, T), fm_slice(attn2c, t0, T))
                for mc in range(MC):
                    for kc in range(EC):
                        p.mm(ps_p[:, 0:T],
                             w1_bf[:, kc * MLP + mc * 128: kc * MLP + mc * 128 + 128],
                             a2_bf[:, kc * T:(kc + 1) * T],
                             start=(kc == 0), stop=(kc == EC - 1))
                    p.act(h1_bf[:, 0:T], ps_p[:, 0:T], AF.Gelu,
                          bias=b1_sb[:, mc:mc + 1])
                    p.dma(fm_slice_one(h1c, t0, T, mc), row3(h1_bf.ap()[:, 0:T], T),
                          eng="gpsimd")

        with ExitStack() as phc:
            w2_bf = phc.enter_context(p.sbuf("w2_bf", [128, MC * E], BF16))
            ps_p = phc.enter_context(p.psum("ps_pc", [128, T], F32))
            ps_mu = phc.enter_context(p.psum("ps_muc", [128, T], F32))
            ps_rs = phc.enter_context(p.psum("ps_rsc", [128, T], F32))
            h1t = phc.enter_context(p.sbuf("h1t", [128, MC * T], BF16))
            b2_sb = phc.enter_context(p.sbuf("b2_sb", [128, EC], F32))
            s_bf = phc.enter_context(p.sbuf("s_bf", [128, EC * T], BF16))
            a2_bf = phc.enter_context(p.sbuf("a2c_bf", [128, EC * T], BF16))
            p.dma(b2_sb[:], b2)
            load_weight_bf16_2q(p, w2_bf, W2, MLP, E, stage, stage2)
            for ti in range(NT):
                t0 = ti * T
                p.dma(tile_ap(h1t, MC, T), fm_slice(h1c, t0, T, nchunk=MC))
                p.dma(tile_ap(a2_bf, EC, T), fm_slice(attn2c, t0, T))
                p.dma(tile_ap(xt, EC, T), fm_slice(x, t0, T))
                load_stats(p, stats, strow, ones128, ps_mu, ps_rs, t0, T)
                for mc in range(EC):
                    for kc in range(MC):
                        p.mm(ps_p[:, 0:T],
                             w2_bf[:, kc * E + mc * 128: kc * E + mc * 128 + 128],
                             h1t[:, kc * T:(kc + 1) * T],
                             start=(kc == 0), stop=(kc == MC - 1))
                    p.ts(scr[:, 0:T], ps_p[:, 0:T], b2_sb[:, mc:mc + 1], None, ALU.add)
                    p.tt(scr[:, 0:T], scr[:, 0:T], a2_bf[:, mc * T:(mc + 1) * T],
                         ALU.add)
                    xn_chunk(p, scr2, xt, ps_mu, ps_rs, mc, T)
                    p.tt(scr[:, 0:T], scr[:, 0:T], scr2[:, 0:T], ALU.add)
                    p.copy(s_bf[:, mc * T:(mc + 1) * T], scr[:, 0:T])
                p.dma(fm_slice(s_c, t0, T), tile_ap(s_bf, EC, T), eng="gpsimd")

        with ExitStack() as phd:
            sbd = lambda n, s, d=F32: phd.enter_context(p.sbuf(n, s, d))
            psd = lambda n, s, d=F32: phd.enter_context(p.psum(n, s, d))
            wdT = sbd("wdT", [128, EC * IC * 128], BF16)
            swd = sbd("swd", [128, EC])
            h2t = sbd("h2t", [128, IC * T], BF16)
            h2q = sbd("h2q", [128, IC * T], BF16)
            yq = sbd("yq", [128, IF])
            i1 = sbd("i1", [128, IF], I32)
            i2 = sbd("i2", [128, IF], I32)
            tfq = sbd("tfq", [128, IF])
            s_b = sbd("s_b", [128, EC * T], BF16)
            rsd_pp = sbd("rsd_pp", [128, 1])
            sd_pp = sbd("sd_pp", [128, 1])
            rrow = sbd("rrow", [1, 128])
            sdr = sbd("sdr", [1, 4])
            rm = sbd("rm", [128, 1])
            wi_b = sbd("wi_b", [128, IF], BF16)
            ps_p = psd("ps_pd", [128, T])
            ps_4 = psd("ps_4d", [128, 4])
            psq = psd("psqd", [128, 128], BF16)

            p.dma(rrow[0:1, 0:128], mxh_all)
            make_scale(p, rrow, clip_down, sdr, ones128, ps_4, rsd_pp, sd_pp)
            for hc in range(EC):
                p.dma(stage[:, 0:IF], w_down[hc * 128:(hc + 1) * 128, :])
                p.red(rm, stage[:, 0:IF], ALU.max, absval=True)
                p.ts(swd[:, hc:hc + 1], rm, 1.0 / 127.0, None, ALU.mult)
                p.recip(rm, rm)
                p.ts(rm, rm, 127.0, None, ALU.mult)
                p.act(yq[:, 0:IF], stage[:, 0:IF], AF.Copy, bias=256.5, scale=rm)
                quant_round(p, yq, i1, i2, tfq, wi_b, IF)
                for ic in range(IC):
                    p.trans(psq[:, 0:128], wi_b[:, ic * 128:(ic + 1) * 128], ident_b[:])
                    p.copy(wdT[:, (hc * IC + ic) * 128:(hc * IC + ic) * 128 + 128],
                           psq[:, 0:128])
                p.tt(swd[:, hc:hc + 1], swd[:, hc:hc + 1], sd_pp[:, 0:1], ALU.mult)
            p.set_serial(False)

            for ti in range(NT):
                t0 = ti * T
                p.dma(tile_ap(h2t, IC, T), fm_slice(h2c, t0, T, nchunk=IC))
                p.dma(tile_ap(s_b, EC, T), fm_slice(s_c, t0, T))
                p.dma(tile_ap(xt, EC, T), fm_slice(x, t0, T))
                for ic in range(IC):
                    p.act(yq[:, 0:T], h2t[:, ic * T:(ic + 1) * T], AF.Copy,
                          bias=256.5, scale=rsd_pp[:, 0:1])
                    p.copy(i1[:, 0:T], yq[:, 0:T])
                    p.stt(i2[:, 0:T], yq[:, 0:T], -0.5, i1[:, 0:T], ALU.add,
                          ALU.subtract)
                    p.stt(tfq[:, 0:T], i1[:, 0:T], -256.0, i2[:, 0:T], ALU.add,
                          ALU.add)
                    p.ts(h2q[:, ic * T:(ic + 1) * T], tfq[:, 0:T], 127.0, -128.0,
                         ALU.min, ALU.max)
                for mc in range(EC):
                    for ic in range(IC):
                        p.mm(ps_p[:, 0:T],
                             wdT[:, (mc * IC + ic) * 128:(mc * IC + ic) * 128 + 128],
                             h2q[:, ic * T:(ic + 1) * T],
                             start=(ic == 0), stop=(ic == IC - 1))
                    p.act(scr[:, 0:T], ps_p[:, 0:T], AF.Silu, scale=swd[:, mc:mc + 1])
                    p.tt(scr[:, 0:T], scr[:, 0:T], s_b[:, mc * T:(mc + 1) * T],
                         ALU.mult)
                    p.tt(scr2[:, 0:T], scr[:, 0:T], xt[:, mc * T:(mc + 1) * T],
                         ALU.add)
                    p.dma(fm_slice_one(o_out, t0, T, mc), row3(scr2.ap()[:, 0:T], T),
                          eng="gpsimd")

        p.emit(ctxs)
    return p


SERIAL = False
DBG_SER = set()


def build_fused_v1():
    p = P(serial=SERIAL)
    x = p.din("x", [E, L])
    xe = p.din("xe", [3 * E, L])
    Wq = p.din("Wq", [E, E])
    Wv = p.din("Wv", [E, E])
    vecs = p.din("vecs", [128, 6 * EC])
    Wk = p.din("Wk", [E, E])
    Wpk = p.din("Wpk", [E, E])
    veck = p.din("veck", [128, 4 * EC])
    bpk_row = p.din("bpk_row", [1, E])
    w_gate = p.din("w_gate", [IF, E])
    w_up = p.din("w_up", [IF, E])
    clip_x = p.din("clip_x", [1, 1])
    Wpq = p.din("Wpq", [E, E])
    veca = p.din("veca", [128, 4 * EC])
    cwT = p.din("cwT", [5 * 64, E])
    gamma = p.din("gamma", [1, 1])
    W1 = p.din("W1", [E, MLP])
    b1 = p.din("b1", [128, MC])
    W2 = p.din("W2", [MLP, E])
    b2 = p.din("b2", [128, EC])
    w_down = p.din("w_down", [E, IF])
    clip_down = p.din("clip_down", [1, 1])
    sel = p.din("sel", [EC * 16, 128])
    nonce4 = p.din("nonce4", [4, 2])
    nonce_r = p.din("nonce_r", [1, 16])

    o_out = p.dout("out", [E, L])

    DBG = False
    dz = p.dout if DBG else p.dtmp
    stats = dz("stats", [2, L])
    stats_e = p.dtmp("stats_e", [2, 3 * L])
    Qc = dz("Qc", [E, L], BF16)
    Vc = dz("Vc", [E, L], BF16)
    qg_d = dz("qg_d", [128, EC])
    Kc = dz("Kc", [E, L], BF16)
    scores_d = dz("scores_d", [16, L])
    h2c = dz("h2c", [IF, L], BF16)
    kv_d = dz("kv_d", [128, H * 32])
    ks_d = dz("ks_d", [16, E])
    attn2c = dz("attn2c", [E, L], BF16)
    h1c = dz("h1c", [MLP, L], BF16)
    s_c = dz("s_c", [E, L], BF16)
    bnc = p.dtmp("bnc", [128, 1])
    bnc16 = p.dtmp("bnc16", [16, 1])
    bnc4a = p.dtmp("bnc4a", [4, 16])
    bnc4b = p.dtmp("bnc4b", [4, 16])
    sh1 = p.nc.dram_tensor("sh1", [8, 16], F32, kind="Internal",
                           addr_space="Shared").ap()
    sh2 = p.nc.dram_tensor("sh2", [8, 16], F32, kind="Internal",
                           addr_space="Shared").ap()

    ctxs = ExitStack()
    with ctxs:
        sb = lambda n, s, d=F32: ctxs.enter_context(p.sbuf(n, s, d))

        ones1 = sb("ones1", [128, 1], BF16)
        ones128 = sb("ones128", [1, 128], F32)
        ident_b = sb("ident_b", [128, 128], BF16)
        strow = sb("strow", [1, 5 * T])
        rrow = sb("rrow", [1, 128])
        mrow = sb("mrow", [1, 16])
        rtmp = sb("rtmp", [128, 1])
        w4 = sb("w4", [4, 16])
        nc_sb = sb("nc_sb", [1, 16])
        sh_rd = sb("sh_rd", [1, 128])
        sh_msk = sb("sh_msk", [1, 16], I32)
        sh_cnd = sb("sh_cnd", [1, 16])

        mu = strow[0:1, 0:T]
        rs = strow[0:1, T:2 * T]
        scr1 = strow[0:1, 2 * T:3 * T]
        mu2 = strow[0:1, 3 * T:4 * T]
        rs2 = strow[0:1, 4 * T:5 * T]

        p.memset(ones1[:], 1.0)
        p.memset(ones1f[:], 1.0)
        p.memset(ones128[:], 1.0)
        p.identity_fill(ident_b[:], 128)
        p.dma(nc_sb[:], nonce_r)

        with ExitStack() as ph:
            sba = lambda n, s, d=F32: ph.enter_context(p.sbuf(n, s, d))
            psa = lambda n, s, d=F32: ph.enter_context(p.psum(n, s, d))
            stage = sba("stage", [128, E])
            wq_bf = sba("wq_bf", [128, EC * E], BF16)
            wv_bf = sba("wv_bf", [128, EC * E], BF16)
            vec = sba("vec", [128, 6 * EC])
            qsum = sba("qsum", [128, EC])
            vmax = sba("vmax", [128, 1])
            P1B = []
            for _i in range(2):
                P1B.append(dict(
                    xt=sba(f"xtP1{_i}", [128, EC * T]),
                    scr=sba(f"scrP1{_i}", [128, T]),
                    xn_f=sba(f"xn_f{_i}", [128, T]),
                    xn_bf=sba(f"xn_bf{_i}", [128, EC * T], BF16),
                    sq_bf=sba(f"sq_bf{_i}", [128, EC * T], BF16),
                    q_bf=sba(f"q_bf{_i}", [128, EC * T], BF16),
                    out_bf=sba(f"out_bf{_i}", [128, EC * T], BF16),
                    mu_sb=sba(f"mu_sb{_i}", [128, T]),
                    rs_sb=sba(f"rs_sb{_i}", [128, T]),
                    rtmp=sba(f"rtmpP1{_i}", [128, 1]),
                    strow=sba(f"strowP1{_i}", [1, 5 * T]),
                    ps_s1=psa(f"ps_s1{_i}", [1, T]),
                    ps_s2=psa(f"ps_s2{_i}", [1, T]),
                    ps_mu=psa(f"ps_mu{_i}", [128, T]),
                    ps_p=psa(f"ps_p{_i}", [128, T]),
                ))

            p.memset(qsum[:], 0.0)
            p.memset(vmax[:], 0.0)
            p.memset(mrow[:], 0.0)
            p.dma(vec[:], vecs)
            load_weight_bf16(p, wq_bf, Wq, E, E, stage)
            load_weight_bf16(p, wv_bf, Wv, E, E, stage)

            for ti in range(NT):
                bb = P1B[ti % 2]
                xt, scr = bb["xt"], bb["scr"]
                xn_f, xn_bf = bb["xn_f"], bb["xn_bf"]
                sq_bf, q_bf, out_bf = bb["sq_bf"], bb["q_bf"], bb["out_bf"]
                mu_sb, rs_sb, rtmp_ = bb["mu_sb"], bb["rs_sb"], bb["rtmp"]
                ps_s1, ps_s2 = bb["ps_s1"], bb["ps_s2"]
                ps_mu, ps_p = bb["ps_mu"], bb["ps_p"]
                st_ = bb["strow"]
                mu_, rs_ = st_[0:1, 0:T], st_[0:1, T:2 * T]
                scr1_ = st_[0:1, 2 * T:3 * T]
                mu2_, rs2_ = st_[0:1, 3 * T:4 * T], st_[0:1, 4 * T:5 * T]
                t0 = ti * T
                p.dma(tile_ap(xt, EC, T), fm_slice(x, t0, T))
                p.copy(xn_bf[:], xt[:])
                ln_stats(p, xn_bf, sq_bf, ones1, ps_s1, ps_s2, T, mu_, rs_, scr1_)
                p.dma(stats[0:1, t0:t0 + T], mu_)
                p.dma(stats[1:2, t0:t0 + T], rs_)
                p.mm(ps_mu[:, 0:T], ones128, mu_)
                p.act(mu_sb[:, 0:T], ps_mu[:, 0:T], AF.Copy)
                p.mm(ps_mu[:, 0:T], ones128, rs_)
                p.act(rs_sb[:, 0:T], ps_mu[:, 0:T], AF.Copy)
                for c in range(EC):
                    xn_chunk(p, xn_f, xt, mu_sb, rs_sb, c, T, eng="gpsimd")
                    p.red(rtmp_, xn_f[:, 0:T], ALU.max, absval=True)
                    p.tt(vmax, vmax, rtmp_, ALU.max)
                    p.copy(xn_bf[:, c * T:(c + 1) * T], xn_f[:, 0:T], eng="gpsimd")
                for (w_bf, b0, hbm, do_qsum) in [(wq_bf, 0, Qc, True),
                                                 (wv_bf, 3, Vc, False)]:
                    for mc in range(EC):
                        psx = ps_p if mc % 2 == 0 else ps_s1
                        psx = ps_p
                        for kc in range(EC):
                            p.mm(psx[:, 0:T],
                                 w_bf[:, kc * E + mc * 128: kc * E + mc * 128 + 128],
                                 xn_bf[:, kc * T:(kc + 1) * T],
                                 start=(kc == 0), stop=(kc == EC - 1))
                        p.act(q_bf[:, mc * T:(mc + 1) * T], psx[:, 0:T], AF.Identity,
                              bias=vec[:, b0 * EC + mc: b0 * EC + mc + 1])
                    ln_stats(p, q_bf, sq_bf, ones1, ps_s1, ps_s2, T, mu2_, rs2_, scr1_)
                    p.mm(ps_mu[:, 0:T], ones128, mu2_)
                    p.act(mu_sb[:, 0:T], ps_mu[:, 0:T], AF.Copy)
                    p.mm(ps_mu[:, 0:T], ones128, rs2_)
                    p.act(rs_sb[:, 0:T], ps_mu[:, 0:T], AF.Copy)
                    apply_ln(p, out_bf, q_bf, mu_sb, rs_sb, scr, T,
                             g_vec=vec[:, (b0 + 1) * EC:(b0 + 2) * EC],
                             b_vec=vec[:, (b0 + 2) * EC:(b0 + 3) * EC],
                             eng="gpsimd")
                    p.dma(fm_slice(hbm, t0, T), tile_ap(out_bf, EC, T), eng="gpsimd")
                    if do_qsum:
                        for c in range(EC):
                            p.red(rtmp_, out_bf[:, c * T:(c + 1) * T], ALU.add)
                            p.tt(qsum[:, c:c + 1], qsum[:, c:c + 1], rtmp_, ALU.add)

            p.ts(qsum[:], qsum[:], 1.0 / (L * np.sqrt(D)), None, ALU.mult)
            p.dma(qg_d, qsum[:])
            cross_part_max(p, vmax, bnc, rrow, mrow[0:1, 0:1])
            p.dma(bnc4a[0:1, 0:1], mrow[0:1, 0:1])

            for eb in range(3):
                p.memset(vmax[:], 0.0, eng="vector")
                for ti in range(NT):
                    bb = P1B[ti % 2]
                    xt, xn_f, xn_bf = bb["xt"], bb["xn_f"], bb["xn_bf"]
                    sq_bf, mu_sb, rs_sb = bb["sq_bf"], bb["mu_sb"], bb["rs_sb"]
                    rtmp_ = bb["rtmp"]
                    ps_s1, ps_s2, ps_mu = bb["ps_s1"], bb["ps_s2"], bb["ps_mu"]
                    st_ = bb["strow"]
                    mu_, rs_ = st_[0:1, 0:T], st_[0:1, T:2 * T]
                    scr1_ = st_[0:1, 2 * T:3 * T]
                    t0 = ti * T
                    p.dma(tile_ap(xt, EC, T),
                          xe.rearrange("(c p) l -> p c l", p=128)
                          [:, eb * EC:(eb + 1) * EC, t0:t0 + T])
                    p.copy(xn_bf[:], xt[:])
                    ln_stats(p, xn_bf, sq_bf, ones1, ps_s1, ps_s2, T, mu_, rs_, scr1_)
                    p.dma(stats_e[0:1, eb * L + t0: eb * L + t0 + T], mu_)
                    p.dma(stats_e[1:2, eb * L + t0: eb * L + t0 + T], rs_)
                    p.mm(ps_mu[:, 0:T], ones128, mu_)
                    p.act(mu_sb[:, 0:T], ps_mu[:, 0:T], AF.Copy)
                    p.mm(ps_mu[:, 0:T], ones128, rs_)
                    p.act(rs_sb[:, 0:T], ps_mu[:, 0:T], AF.Copy)
                    for c in range(EC):
                        xn_chunk(p, xn_f, xt, mu_sb, rs_sb, c, T, eng="gpsimd")
                        p.red(rtmp_, xn_f[:, 0:T], ALU.max, absval=True)
                        p.tt(vmax, vmax, rtmp_, ALU.max)
                cross_part_max(p, vmax, bnc, rrow, mrow[0:1, 0:1])
                p.dma(bnc4a[1 + eb:2 + eb, 0:1], mrow[0:1, 0:1])

        p.barrier()
        emit_sh_payload(p, bnc4a, nonce4, w4, sh1)
        mW1 = p.mark()

        p.set_serial(True)
        with ExitStack() as ph:
            sbb = lambda n, s, d=F32: ph.enter_context(p.sbuf(n, s, d))
            psb = lambda n, s, d=F32: ph.enter_context(p.psum(n, s, d))
            stage = sbb("stage", [128, E])
            wk_bf = sbb("wk_bf", [128, EC * E], BF16)
            vec = sbb("vec", [128, 4 * EC])
            qsl = sbb("qsl", [128, EC * 16], BF16)
            qg_sb = sbb("qg_sb", [128, EC])
            BB = []
            for _i in range(2):
                BB.append(dict(
                    xt=sbb(f"xtB{_i}", [128, EC * T]),
                    scr=sbb(f"scrB{_i}", [128, T]),
                    xn_bf=sbb(f"xn_bfB{_i}", [128, EC * T], BF16),
                    k_bf=sbb(f"k_bfB{_i}", [128, EC * T], BF16),
                    sq_bf=sbb(f"sq_bfB{_i}", [128, EC * T], BF16),
                    out_bf=sbb(f"out_bfB{_i}", [128, EC * T], BF16),
                    scb=sbb(f"scbB{_i}", [16, T]),
                    mu_sb=sbb(f"mu_sbB{_i}", [128, T]),
                    rs_sb=sbb(f"rs_sbB{_i}", [128, T]),
                    strow=sbb(f"strowB{_i}", [1, 5 * T]),
                    ps_mu=psb(f"ps_muB{_i}", [128, T]),
                    ps_p=psb(f"ps_pB{_i}", [128, T]),
                    ps_sc=psb(f"ps_scB{_i}", [16, T]),
                ))
            ps_s1 = psb("ps_s1B", [1, T])
            ps_s2 = psb("ps_s2B", [1, T])

            p.dma(vec[:], veck)
            load_weight_bf16(p, wk_bf, Wk, E, E, stage)
            p.dma(qg_sb[:], qg_d)
            p.memset(qsl[:], 0.0)
            for c in range(EC):
                p.copy(qsl[0:64, c * 16 + 2 * c: c * 16 + 2 * c + 1],
                       qg_sb[0:64, c:c + 1])
                p.copy(qsl[64:128, c * 16 + 2 * c + 1: c * 16 + 2 * c + 2],
                       qg_sb[64:128, c:c + 1])
            p.set_serial(False)

            for ti in range(NT):
                bb = BB[ti % 2]
                xt, scr = bb["xt"], bb["scr"]
                xn_bf, k_bf = bb["xn_bf"], bb["k_bf"]
                sq_bf, out_bf, scb = bb["sq_bf"], bb["out_bf"], bb["scb"]
                mu_sb, rs_sb = bb["mu_sb"], bb["rs_sb"]
                ps_mu, ps_p = bb["ps_mu"], bb["ps_p"]
                st_ = bb["strow"]
                scr1_ = st_[0:1, 2 * T:3 * T]
                mu2_, rs2_ = st_[0:1, 3 * T:4 * T], st_[0:1, 4 * T:5 * T]
                t0 = ti * T
                p.dma(tile_ap(xt, EC, T), fm_slice(x, t0, T))
                p.dma(st_[0:1, 0:T], stats[0:1, t0:t0 + T])
                p.dma(st_[0:1, T:2 * T], stats[1:2, t0:t0 + T])
                p.mm(ps_mu[:, 0:T], ones128, st_[0:1, 0:T])
                p.act(mu_sb[:, 0:T], ps_mu[:, 0:T], AF.Copy)
                p.mm(ps_mu[:, 0:T], ones128, st_[0:1, T:2 * T])
                p.act(rs_sb[:, 0:T], ps_mu[:, 0:T], AF.Copy)
                for c in range(EC):
                    xn_chunk(p, scr, xt, mu_sb, rs_sb, c, T, eng="gpsimd")
                    p.copy(xn_bf[:, c * T:(c + 1) * T], scr[:, 0:T], eng="gpsimd")
                for mc in range(EC):
                    for kc in range(EC):
                        p.mm(ps_p[:, 0:T],
                             wk_bf[:, kc * E + mc * 128: kc * E + mc * 128 + 128],
                             xn_bf[:, kc * T:(kc + 1) * T],
                             start=(kc == 0), stop=(kc == EC - 1))
                    p.act(k_bf[:, mc * T:(mc + 1) * T], ps_p[:, 0:T], AF.Identity,
                          bias=vec[:, mc:mc + 1])
                ln_stats(p, k_bf, sq_bf, ones1, ps_s1, ps_s2, T, mu2_, rs2_, scr1_)
                p.mm(ps_mu[:, 0:T], ones128, mu2_)
                p.act(mu_sb[:, 0:T], ps_mu[:, 0:T], AF.Copy)
                p.mm(ps_mu[:, 0:T], ones128, rs2_)
                p.act(rs_sb[:, 0:T], ps_mu[:, 0:T], AF.Copy)
                apply_ln(p, out_bf, k_bf, mu_sb, rs_sb, scr, T,
                         g_vec=vec[:, EC:2 * EC], b_vec=vec[:, 2 * EC:3 * EC],
                         eng="gpsimd")
                p.dma(fm_slice(Kc, t0, T), tile_ap(out_bf, EC, T), eng="gpsimd")
                ps_sc = bb["ps_sc"]
                for kc in range(EC):
                    p.mm(ps_sc[0:16, 0:T], qsl[:, kc * 16:(kc + 1) * 16],
                         out_bf[:, kc * T:(kc + 1) * T],
                         start=(kc == 0), stop=(kc == EC - 1))
                p.copy(scb[0:16, 0:T], ps_sc[0:16, 0:T])
                p.dma(scores_d[0:16, t0:t0 + T], scb[0:16, 0:T])
        mB = p.mark()
        p.set_serial(False)
        p.barrier()

        with ExitStack() as ph:
            sba = lambda n, s, d=F32: ph.enter_context(p.sbuf(n, s, d))
            psa = lambda n, s, d=F32: ph.enter_context(p.psum(n, s, d))
            stage = sba("stage", [128, E])
            wgT = sba("wgT", [128, IC * EC * 128], BF16)
            wuT = sba("wuT", [128, IC * EC * 128], BF16)
            sg = sba("sg", [128, IC])
            su = sba("su", [128, IC])
            rm = sba("rm", [128, 1])
            hmax = sba("hmax", [128, 1])
            sdr = sba("sdr", [1, 4])
            rsx_pp = sba("rsx_pp", [128, 1])
            sx_pp = sba("sx_pp", [128, 1])
            gm1 = sba("gm1", [1, 1])
            ps_4 = psa("ps_4", [128, 4])
            psq = psa("psq", [128, 128], BF16)

            p.dep_on(mW1, mB)
            emit_sh_read(p, sh1, nc_sb, sh_rd, sh_msk, sh_cnd, gm1[0:1, 0:1])
            make_scale_g(p, gm1[0:1, 0:1], clip_x, sdr, ones128, ps_4,
                         rsx_pp, sx_pp)

            with ExitStack() as phw:
                sbw = lambda n, s, d=F32: phw.enter_context(p.sbuf(n, s, d))
                yq = sbw("yqw", [128, E])
                i1 = sbw("i1w", [128, E], I32)
                i2 = sbw("i2w", [128, E], I32)
                tfq = sbw("tfqw", [128, E])
                wi_b = sbw("wi_bw", [128, E], BF16)
                for (w_hbm, wT, sc) in [(w_gate, wgT, sg), (w_up, wuT, su)]:
                    for ic in range(IC):
                        p.dma(stage[:, 0:E], w_hbm[ic * 128:(ic + 1) * 128, :])
                        p.red(rm, stage[:, 0:E], ALU.max, absval=True)
                        p.ts(sc[:, ic:ic + 1], rm, 1.0 / 127.0, None, ALU.mult)
                        p.recip(rm, rm)
                        p.ts(rm, rm, 127.0, None, ALU.mult)
                        p.act(yq[:, 0:E], stage[:, 0:E], AF.Copy, bias=256.5, scale=rm)
                        quant_round(p, yq, i1, i2, tfq, wi_b, E)
                        for kc in range(EC):
                            p.trans(psq[:, 0:128], wi_b[:, kc * 128:(kc + 1) * 128], ident_b[:])
                            p.copy(wT[:, (ic * EC + kc) * 128:(ic * EC + kc) * 128 + 128],
                                   psq[:, 0:128])
                    for ic in range(IC):
                        p.tt(sc[:, ic:ic + 1], sc[:, ic:ic + 1], sx_pp[:, 0:1], ALU.mult)
            p.barrier()
            AB = []
            for _i in range(2):
                AB.append(dict(
                    xt=sba(f"xtA{_i}", [128, EC * T]),
                    scr=sba(f"scrA{_i}", [128, T]),
                    srow=sba(f"srowA{_i}", [1, 2 * T]),
                    xq_bf=sba(f"xq_bf{_i}", [128, EC * T], BF16),
                    yq=sba(f"yqA{_i}", [128, T]),
                    i1=sba(f"i1A{_i}", [128, T], I32),
                    i2=sba(f"i2A{_i}", [128, T], I32),
                    tfq=sba(f"tfqA{_i}", [128, T]),
                    gs=sba(f"gsA{_i}", [128, T]),
                    us=sba(f"usA{_i}", [128, T]),
                    h2f=sba(f"h2fA{_i}", [128, T]),
                    h2b=sba(f"h2bA{_i}", [128, T], BF16),
                    mu_sb=sba(f"mu_sbA{_i}", [128, T]),
                    rs_sb=sba(f"rs_sbA{_i}", [128, T]),
                    ps_mu=psa(f"ps_muA{_i}", [128, T]),
                    ps_g=psa(f"ps_gA{_i}", [128, T]),
                    ps_g2=psa(f"ps_g2A{_i}", [128, T]),
                ))

            def ffn_batch(src_hbm, c0, st_hbm, st_off, store_h2, base_ti=0):
                for ti in range(NT):
                    bb = AB[(base_ti + ti) % 2]
                    t0 = ti * T
                    xt_, scr_ = bb["xt"], bb["scr"]
                    p.dma(tile_ap(xt_, EC, T),
                          src_hbm.rearrange("(c p) l -> p c l", p=128)
                          [:, c0:c0 + EC, t0:t0 + T])
                    load_stats(p, st_hbm, bb["srow"], ones128, bb["ps_mu"],
                               bb["ps_g2"], st_off + t0, T)
                    bcast_sb(p, bb["ps_mu"], bb["ps_g2"], bb["mu_sb"],
                             bb["rs_sb"], T)
                    for c in range(EC):
                        xn_chunk(p, scr_, xt_, bb["mu_sb"], bb["rs_sb"], c, T,
                                 eng="gpsimd")
                        p.act(bb["yq"][:, 0:T], scr_[:, 0:T], AF.Copy, bias=256.5,
                              scale=rsx_pp[:, 0:1])
                        quant_round(p, bb["yq"], bb["i1"], bb["i2"], bb["tfq"],
                                    bb["h2b"], T)
                        p.copy(bb["xq_bf"][:, c * T:(c + 1) * T],
                               bb["h2b"][:, 0:T])
                    for ic in range(IC):
                        for kc in range(EC):
                            p.mm(bb["ps_g"][:, 0:T],
                                 wgT[:, (ic * EC + kc) * 128:(ic * EC + kc) * 128 + 128],
                                 bb["xq_bf"][:, kc * T:(kc + 1) * T],
                                 start=(kc == 0), stop=(kc == EC - 1))
                        p.act(bb["gs"][:, 0:T], bb["ps_g"][:, 0:T], AF.Silu,
                              scale=sg[:, ic:ic + 1])
                        for kc in range(EC):
                            p.mm(bb["ps_g2"][:, 0:T],
                                 wuT[:, (ic * EC + kc) * 128:(ic * EC + kc) * 128 + 128],
                                 bb["xq_bf"][:, kc * T:(kc + 1) * T],
                                 start=(kc == 0), stop=(kc == EC - 1))
                        p.act(bb["us"][:, 0:T], bb["ps_g2"][:, 0:T], AF.Identity,
                              scale=su[:, ic:ic + 1])
                        p.tt(bb["h2f"][:, 0:T], bb["gs"][:, 0:T],
                             bb["us"][:, 0:T], ALU.mult, eng="gpsimd")
                        if store_h2:
                            p.copy(bb["h2b"][:, 0:T], bb["h2f"][:, 0:T],
                                   eng="gpsimd")
                            p.dma(fm_slice_one(h2c, t0, T, ic),
                                  row3(bb["h2b"].ap()[:, 0:T], T), eng="gpsimd")
                        p.red(bb["yq"][:, 0:1].bitcast(F32) if False else rtmp,
                              bb["h2f"][:, 0:T], ALU.max, absval=True)
                        p.tt(hmax, hmax, rtmp, ALU.max)

            p.memset(hmax[:], 0.0, eng="vector")
            ffn_batch(x, 0, stats, 0, True, base_ti=0)
            cross_part_max(p, hmax, bnc, rrow, mrow[0:1, 0:1])
            p.dma(bnc4b[0:1, 0:1], mrow[0:1, 0:1])
            for eb in range(3):
                p.memset(hmax[:], 0.0, eng="vector")
                ffn_batch(xe, eb * EC, stats_e, eb * L, False, base_ti=eb + 1)
                cross_part_max(p, hmax, bnc, rrow, mrow[0:1, 0:1])
                p.dma(bnc4b[1 + eb:2 + eb, 0:1], mrow[0:1, 0:1])

        p.barrier()
        emit_sh_payload(p, bnc4b, nonce4, w4, sh2)
        mW2 = p.mark()

        p.set_serial(True)
        with ExitStack() as ph:
            sbc = lambda n, s, d=F32: ph.enter_context(p.sbuf(n, s, d))
            psc = lambda n, s, d=F32: ph.enter_context(p.psum(n, s, d))
            stage = sbc("stage", [128, E])
            ident_f = sbc("ident_f", [128, 128], F32)
            scores = sbc("scores", [16, L])
            srow16 = sbc("srow16", [16, 4])
            a_tm_f = sbc("a_tm_f", [128, (L // 128) * 16])
            a_tm_b = sbc("a_tm_b", [128, (L // 128) * 16], BF16)
            a0r_b = sbc("a0r_b", [1, 16], BF16)
            scr = sbc("scrP3", [128, T])
            pk0_b = sbc("pk0_b", [1, E], BF16)
            pk0_f = sbc("pk0_f", [1, E])
            wpk_bf = sbc("wpk_bf", [128, EC * E], BF16)
            vec = sbc("vec2", [128, 4 * EC])
            k_bf = sbc("k_bf2", [128, EC * T], BF16)
            v_bf = sbc("v_bf", [128, EC * T], BF16)
            pk_bf = sbc("pk_bf", [128, EC * T], BF16)
            pkt_b = sbc("pkt_b", [128, E], BF16)
            vst_b = sbc("vst_b", [128, E], BF16)
            kvout = sbc("kvout", [128, H * 32])
            ksout = sbc("ksout", [16, E])

            p.identity_fill(ident_f[:], 128)
            p.dma(scores[0:16, :], scores_d)
            p.red(srow16[0:16, 0:1], scores[:], ALU.max)
            p.ts(srow16[0:16, 0:1], srow16[0:16, 0:1], 0.0, None, ALU.max)
            p.ts(srow16[0:16, 1:2], srow16[0:16, 0:1], -1.0, None, ALU.mult)
            p.act(scores[:], scores[:], AF.Exp, bias=srow16[0:16, 1:2])
            p.red(srow16[0:16, 2:3], scores[:], ALU.add)
            p.act(srow16[0:16, 3:4], srow16[0:16, 1:2], AF.Exp)
            p.tt(srow16[0:16, 2:3], srow16[0:16, 2:3], srow16[0:16, 3:4], ALU.add)
            p.recip(srow16[0:16, 2:3], srow16[0:16, 2:3])
            p.ts(srow16[0:16, 2:3], srow16[0:16, 2:3], float(L), None, ALU.mult)
            p.ts(scores[:], scores[:], srow16[0:16, 2:3], None, ALU.mult)
            p.tt(srow16[0:16, 3:4], srow16[0:16, 3:4], srow16[0:16, 2:3], ALU.mult)
            p.dma(bnc16, srow16[0:16, 3:4])
            p.dma(rrow[0:1, 0:16], bnc16.rearrange("p c -> c p"))
            p.copy(a0r_b[0:1, 0:16], rrow[0:1, 0:16])
            with ExitStack() as phs:
                ptr_f = phs.enter_context(p.psum("ptr_f", [128, 16], F32))
                for tc in range(L // 128):
                    p.trans(ptr_f[:, 0:16], scores[0:16, tc * 128:(tc + 1) * 128],
                            ident_f[0:16, 0:16])
                    p.copy(a_tm_f[:, tc * 16:(tc + 1) * 16], ptr_f[:, 0:16])
            p.barrier()
            p.copy(a_tm_b[:], a_tm_f[:])
            p.dma(pk0_f[0:1, 0:E], bpk_row)
            p.act(pk0_f[0:1, 0:E], pk0_f[0:1, 0:E], AF.Tanh)
            p.ts(pk0_b[0:1, 0:E], pk0_f[0:1, 0:E], 1.0, None, ALU.add)

            ps_p = psc("ps_p3", [128, T])
            ps_pB = psc("ps_p3B", [128, T])
            ptr1 = psc("ptr1", [128, 1024], BF16)
            ptr2 = psc("ptr2", [128, 1024], BF16)
            ps_kv = psc("ps_kv", [128, H * 32])
            ps_ks = psc("ps_ks", [16, E])

            p.dma(vec[:], veck)
            load_weight_bf16(p, wpk_bf, Wpk, E, E, stage)
            p.memset(ps_kv[:], 0.0, eng="vector")
            p.memset(ps_ks[0:16, :], 0.0, eng="vector")
            for c in range(EC):
                p.mm(ps_ks[0:16, c * 128:(c + 1) * 128], a0r_b[0:1, 0:16],
                     pk0_b[0:1, c * 128:(c + 1) * 128], start=False, stop=False)
            p.set_serial(False)

            for ti in range(NT):
                t0 = ti * T
                p.dma(tile_ap(k_bf, EC, T), fm_slice(Kc, t0, T))
                p.dma(tile_ap(v_bf, EC, T), fm_slice(Vc, t0, T))
                for mc in range(EC):
                    psx = ps_p if mc % 2 == 0 else ps_pB
                    for kc in range(EC):
                        p.mm(psx[:, 0:T],
                             wpk_bf[:, kc * E + mc * 128: kc * E + mc * 128 + 128],
                             k_bf[:, kc * T:(kc + 1) * T],
                             start=(kc == 0), stop=(kc == EC - 1))
                    p.act(scr[:, 0:T], psx[:, 0:T], AF.Tanh,
                          bias=vec[:, 3 * EC + mc:3 * EC + mc + 1])
                    p.ts(pk_bf[:, mc * T:(mc + 1) * T], scr[:, 0:T], 1.0, None,
                         ALU.add, eng="gpsimd")
                for tc in range(TC):
                    gtc = ti * TC + tc
                    last = (ti == NT - 1 and tc == TC - 1)
                    for c in range(EC):
                        p.trans(ptr1[:, 0:128],
                                pk_bf[:, c * T + tc * 128: c * T + tc * 128 + 128],
                                ident_b[:])
                        p.act(pkt_b[:, c * 128:(c + 1) * 128], ptr1[:, 0:128],
                              AF.Copy)
                        p.trans(ptr2[:, 0:128],
                                v_bf[:, c * T + tc * 128: c * T + tc * 128 + 128],
                                ident_b[:])
                        p.ts(vst_b[:, c * 128:c * 128 + 64], ptr2[:, 0:64],
                             a_tm_f[:, gtc * 16 + 2 * c: gtc * 16 + 2 * c + 1],
                             None, ALU.mult)
                        p.ts(vst_b[:, c * 128 + 64:c * 128 + 128], ptr2[:, 64:128],
                             a_tm_f[:, gtc * 16 + 2 * c + 1: gtc * 16 + 2 * c + 2],
                             None, ALU.mult)
                    for c in range(EC):
                        p.mm(ps_ks[0:16, c * 128:(c + 1) * 128],
                             a_tm_b[:, gtc * 16:(gtc + 1) * 16],
                             pkt_b[:, c * 128:(c + 1) * 128], start=False, stop=last)
                    for h in range(H):
                        c, r = h // 2, h % 2
                        p.mm(ps_kv[r * 64:(r + 1) * 64, c * 64:(c + 1) * 64],
                             pkt_b[:, h * 64:(h + 1) * 64],
                             vst_b[:, h * 64:(h + 1) * 64],
                             start=False, stop=last)

            p.copy(kvout[:], ps_kv[:])
            p.dma(kv_d, kvout[:])
            p.copy(ksout[0:16, :], ps_ks[0:16, :])
            p.dma(ks_d, ksout[0:16, :])

        p.barrier()
        p.set_serial(True)
        with ExitStack() as ph:
            sba = lambda n, s, d=F32: ph.enter_context(p.sbuf(n, s, d))
            psa = lambda n, s, d=F32: ph.enter_context(p.psum(n, s, d))
            stage = sba("stage", [128, E])
            vec = sba("vec", [128, 4 * EC])
            wpq_bf = sba("wpq_bf", [128, EC * E], BF16)
            numT = sba("numT", [128, EC * 128], BF16)
            denT = sba("denT", [128, EC * 16], BF16)
            sel_b = sba("sel_b", [16, EC * 128], BF16)
            cvT = sba("cvT", [128, EC * 5 * 128], BF16)
            kv_f = sba("kv_f", [128, H * 32])
            kv_b = sba("kv_b", [128, H * 32], BF16)
            ksf = sba("ksf", [16, E])
            ks_b = sba("ks_b", [16, E], BF16)
            scr = sba("scrP4", [128, T])
            selt = sba("selt", [128, 128])
            q_bf = sba("q_bf", [128, EC * T], BF16)
            sq_bf = sba("sq_bf", [128, EC * T], BF16)
            pq_bf = sba("pq_bf", [128, EC * T], BF16)
            xn_f = sba("xn_f", [128, EC * T])
            attn_a = sba("attn_a", [128, EC * T], BF16)
            attn_b_ = sba("attn_b", [128, EC * T], BF16)
            tail2 = sba("tail2", [128, EC * 2], BF16)
            conv_win = sba("conv_win", [128, EC * (T + 4)], BF16)
            attn2_bf = sba("attn2_bf", [128, EC * T], BF16)
            rden = sba("rden", [16, T])
            rden_b = sba("rden_b", [16, T], BF16)
            rdb = sba("rdb", [128, T])
            gam_pp = sba("gam_pp", [128, 1])
            gcb = sba("gcb", [128, EC])
            mu_sb = sba("mu_sbP", [128, T])
            rs_sb = sba("rs_sbP", [128, T])
            ps_p = psa("ps_p", [128, T])
            ps_dr = psa("ps_dr", [128, T])
            ps_a = psa("ps_a", [128, T])
            ps_pB = psa("ps_pB", [128, T])

            p.dma(vec[:], veca)
            load_weight_bf16(p, wpq_bf, Wpq, E, E, stage)
            for c in range(EC):
                p.dma(selt[0:16, 0:128], sel[c * 16:(c + 1) * 16, :])
                p.copy(sel_b[0:16, c * 128:(c + 1) * 128], selt[0:16, 0:128])
            p.memset(cvT[:], 0.0)
            for c in range(EC):
                for k in range(5):
                    col = (c * 5 + k) * 128
                    p.dma(selt[0:64, 0:64],
                          cwT[k * 64:(k + 1) * 64, 2 * c * 64:(2 * c + 1) * 64])
                    p.copy(cvT[0:64, col:col + 64], selt[0:64, 0:64])
                    p.dma(selt[64:128, 0:64],
                          cwT[k * 64:(k + 1) * 64, (2 * c + 1) * 64:(2 * c + 2) * 64])
                    p.copy(cvT[64:128, col + 64:col + 128], selt[64:128, 0:64])
            p.dma(kv_f[:], kv_d)
            p.copy(kv_b[:], kv_f[:])
            p.memset(numT[:], 0.0)
            for c in range(EC):
                p.copy(numT[0:64, c * 128: c * 128 + 64],
                       kv_b[0:64, c * 64:(c + 1) * 64])
                p.copy(numT[64:128, c * 128 + 64: c * 128 + 128],
                       kv_b[64:128, c * 64:(c + 1) * 64])
            p.dma(ksf[0:16, :], ks_d)
            p.copy(ks_b[0:16, :], ksf[0:16, :])
            p.memset(denT[:], 0.0)
            with ExitStack() as pht:
                ptr_b = pht.enter_context(p.psum("ptr_b", [128, 16], BF16))
                ps_t4 = pht.enter_context(p.psum("ps_t4", [128, 4], F32))
                p.dma(strow[0:1, 0:1], gamma)
                p.mm(ps_t4[:, 0:1], ones128, strow[0:1, 0:1])
                p.copy(gam_pp[:, 0:1], ps_t4[:, 0:1])
                p.ts(gcb[:], vec[:, 3 * EC:4 * EC], gam_pp[:, 0:1], None, ALU.mult)
                for c in range(EC):
                    p.trans(ptr_b[:, 0:16], ks_b[0:16, c * 128:(c + 1) * 128],
                            ident_b[0:16, 0:16])
                    p.copy(denT[0:64, c * 16 + 2 * c: c * 16 + 2 * c + 1],
                           ptr_b[0:64, 2 * c:2 * c + 1])
                    p.copy(denT[64:128, c * 16 + 2 * c + 1: c * 16 + 2 * c + 2],
                           ptr_b[64:128, 2 * c + 1:2 * c + 2])
            p.barrier()
            p.set_serial(False)
            ps_s1 = psa("ps_s1", [1, T])
            ps_s2 = psa("ps_s2", [1, T])
            ps_b = psa("ps_b", [128, T])

            def conv_ln_store(ti_prev, a_cur, a_prev):
                t0p = ti_prev * T
                for c in range(EC):
                    w0 = c * (T + 4)
                    if ti_prev == 0:
                        p.memset(conv_win[:, w0:w0 + 2], 0.0, eng="vector")
                    else:
                        p.copy(conv_win[:, w0:w0 + 2], tail2[:, c * 2:(c + 1) * 2])
                    p.copy(conv_win[:, w0 + 2:w0 + 2 + T],
                           a_prev[:, c * T:(c + 1) * T])
                    if ti_prev == NT - 1:
                        p.memset(conv_win[:, w0 + 2 + T:w0 + 4 + T], 0.0,
                                 eng="vector")
                    else:
                        p.copy(conv_win[:, w0 + 2 + T:w0 + 4 + T],
                               a_cur[:, c * T:c * T + 2])
                    p.copy(tail2[:, c * 2:(c + 1) * 2],
                           a_prev[:, (c + 1) * T - 2:(c + 1) * T])
                for c in range(EC):
                    psc_ = ps_a if c % 2 == 0 else ps_p
                    for k in range(5):
                        p.mm(psc_[:, 0:T],
                             cvT[:, (c * 5 + k) * 128:(c * 5 + k) * 128 + 128],
                             conv_win[:, c * (T + 4) + k: c * (T + 4) + k + T],
                             start=(k == 0), stop=(k == 4))
                    p.act(scr[:, 0:T], psc_[:, 0:T], AF.Identity,
                          bias=gcb[:, c:c + 1], scale=gam_pp[:, 0:1])
                    p.tt(xn_f[:, c * T:(c + 1) * T], scr[:, 0:T],
                         a_prev[:, c * T:(c + 1) * T], ALU.add)
                p.copy(sq_bf[:], xn_f[:], eng="gpsimd")
                ln_stats(p, sq_bf, q_bf, ones1, ps_s1, ps_s2, T, mu2, rs2, scr1)
                p.mm(ps_b[:, 0:T], ones128, mu2)
                p.act(mu_sb[:, 0:T], ps_b[:, 0:T], AF.Copy)
                p.mm(ps_b[:, 0:T], ones128, rs2)
                p.act(rs_sb[:, 0:T], ps_b[:, 0:T], AF.Copy)
                apply_ln(p, attn2_bf, sq_bf, mu_sb, rs_sb, scr, T,
                         g_vec=vec[:, EC:2 * EC], b_vec=vec[:, 2 * EC:3 * EC])
                p.dma(fm_slice(attn2c, t0p, T), tile_ap(attn2_bf, EC, T),
                      eng="gpsimd")

            for ti in range(NT):
                t0 = ti * T
                a_cur = attn_a if ti % 2 == 0 else attn_b_
                a_prev = attn_b_ if ti % 2 == 0 else attn_a
                p.dma(tile_ap(q_bf, EC, T), fm_slice(Qc, t0, T))
                for mc in range(EC):
                    psx = ps_p if mc % 2 == 0 else ps_pB
                    for kc in range(EC):
                        p.mm(psx[:, 0:T],
                             wpq_bf[:, kc * E + mc * 128: kc * E + mc * 128 + 128],
                             q_bf[:, kc * T:(kc + 1) * T],
                             start=(kc == 0), stop=(kc == EC - 1))
                    p.act(scr[:, 0:T], psx[:, 0:T], AF.Tanh, bias=vec[:, mc:mc + 1])
                    p.ts(pq_bf[:, mc * T:(mc + 1) * T], scr[:, 0:T], 1.0, None,
                         ALU.add, eng="gpsimd")
                for c in range(EC):
                    p.mm(ps_dr[0:16, 0:T], denT[:, c * 16:(c + 1) * 16],
                         pq_bf[:, c * T:(c + 1) * T],
                         start=(c == 0), stop=(c == EC - 1))
                p.ts(rden[0:16, 0:T], ps_dr[0:16, 0:T], DEN_EPS, None, ALU.add)
                p.recip(rden[0:16, 0:T], rden[0:16, 0:T])
                p.copy(rden_b[0:16, 0:T], rden[0:16, 0:T], eng="gpsimd")
                for c in range(EC):
                    psn = ps_a if c % 2 == 0 else ps_pB
                    p.mm(psn[:, 0:T], numT[:, c * 128:(c + 1) * 128],
                         pq_bf[:, c * T:(c + 1) * T])
                    p.mm(ps_dr[:, 0:T], sel_b[0:16, c * 128:(c + 1) * 128],
                         rden_b[0:16, 0:T])
                    p.act(rdb[:, 0:T], ps_dr[:, 0:T], AF.Copy)
                    p.tt(a_cur[:, c * T:(c + 1) * T], psn[:, 0:T], rdb[:, 0:T],
                         ALU.mult)
                if ti > 0:
                    conv_ln_store(ti - 1, a_cur, a_prev)
            last_buf = attn_b_ if (NT - 1) % 2 == 1 else attn_a
            conv_ln_store(NT - 1, last_buf, last_buf)

        p.barrier()
        with ExitStack() as ph:
            stage = ph.enter_context(p.sbuf("stageb", [128, MLP]))
            stage2 = ph.enter_context(p.sbuf("stageb2", [128, MLP]))
            w1_bf = ph.enter_context(p.sbuf("w1_bf", [128, EC * MLP], BF16))
            ps_p = ph.enter_context(p.psum("ps_pb", [128, T], F32))
            ps_p2 = ph.enter_context(p.psum("ps_pb2", [128, T], F32))
            h1_bf2 = ph.enter_context(p.sbuf("h1_bf2", [128, T], BF16))
            h1_bf = ph.enter_context(p.sbuf("h1_bf", [128, T], BF16))
            a2_bf = ph.enter_context(p.sbuf("a2b_bf", [128, EC * T], BF16))
            b1_sb = ph.enter_context(p.sbuf("b1_sb", [128, MC], F32))
            p.dma(b1_sb[:], b1)
            load_weight_bf16_2q(p, w1_bf, W1, E, MLP, stage, stage2)
            for ti in range(NT):
                t0 = ti * T
                p.dma(tile_ap(a2_bf, EC, T), fm_slice(attn2c, t0, T))
                for mc in range(MC):
                    psx = ps_p if mc % 2 == 0 else ps_p2
                    hbx = h1_bf if mc % 2 == 0 else h1_bf2
                    for kc in range(EC):
                        p.mm(psx[:, 0:T],
                             w1_bf[:, kc * MLP + mc * 128: kc * MLP + mc * 128 + 128],
                             a2_bf[:, kc * T:(kc + 1) * T],
                             start=(kc == 0), stop=(kc == EC - 1))
                    p.act(hbx[:, 0:T], psx[:, 0:T], AF.Gelu,
                          bias=b1_sb[:, mc:mc + 1])
                    p.dma(fm_slice_one(h1c, t0, T, mc), row3(hbx.ap()[:, 0:T], T),
                          eng="gpsimd")

        p.barrier()
        with ExitStack() as ph:
            stage = ph.enter_context(p.sbuf("stagec", [128, MLP]))
            stage2 = ph.enter_context(p.sbuf("stagec2", [128, MLP]))
            w2_bf = ph.enter_context(p.sbuf("w2_bf", [128, MC * E], BF16))
            ps_p = ph.enter_context(p.psum("ps_pc", [128, T], F32))
            ps_p2 = ph.enter_context(p.psum("ps_pc2", [128, T], F32))
            ps_mu = ph.enter_context(p.psum("ps_muc", [128, T], F32))
            ps_rs = ph.enter_context(p.psum("ps_rsc", [128, T], F32))
            mu_sb = ph.enter_context(p.sbuf("mu_sbC", [128, T], F32))
            rs_sb = ph.enter_context(p.sbuf("rs_sbC", [128, T], F32))
            xt = ph.enter_context(p.sbuf("xtC", [128, EC * T], F32))
            scr = ph.enter_context(p.sbuf("scrC", [128, T], F32))
            scr2 = ph.enter_context(p.sbuf("scr2C", [128, T], F32))
            h1t = ph.enter_context(p.sbuf("h1t", [128, MC * T], BF16))
            b2_sb = ph.enter_context(p.sbuf("b2_sb", [128, EC], F32))
            s_bf = ph.enter_context(p.sbuf("s_bf", [128, EC * T], BF16))
            a2_bf = ph.enter_context(p.sbuf("a2c_bf", [128, EC * T], BF16))
            p.dma(b2_sb[:], b2)
            load_weight_bf16_2q(p, w2_bf, W2, MLP, E, stage, stage2)
            for ti in range(NT):
                t0 = ti * T
                p.dma(tile_ap(h1t, MC, T), fm_slice(h1c, t0, T, nchunk=MC))
                p.dma(tile_ap(a2_bf, EC, T), fm_slice(attn2c, t0, T))
                p.dma(tile_ap(xt, EC, T), fm_slice(x, t0, T))
                load_stats(p, stats, strow, ones128, ps_mu, ps_rs, t0, T)
                bcast_sb(p, ps_mu, ps_rs, mu_sb, rs_sb, T)
                for mc in range(EC):
                    psx = ps_p if mc % 2 == 0 else ps_p2
                    for kc in range(MC):
                        p.mm(psx[:, 0:T],
                             w2_bf[:, kc * E + mc * 128: kc * E + mc * 128 + 128],
                             h1t[:, kc * T:(kc + 1) * T],
                             start=(kc == 0), stop=(kc == MC - 1))
                    p.act(scr[:, 0:T], psx[:, 0:T], AF.Identity,
                          bias=b2_sb[:, mc:mc + 1])
                    p.tt(scr[:, 0:T], scr[:, 0:T], a2_bf[:, mc * T:(mc + 1) * T],
                         ALU.add, eng="gpsimd")
                    xn_chunk(p, scr2, xt, mu_sb, rs_sb, mc, T)
                    p.tt(scr[:, 0:T], scr[:, 0:T], scr2[:, 0:T], ALU.add,
                         eng="gpsimd")
                    p.copy(s_bf[:, mc * T:(mc + 1) * T], scr[:, 0:T], eng="gpsimd")
                p.dma(fm_slice(s_c, t0, T), tile_ap(s_bf, EC, T), eng="gpsimd")
        m4c = p.mark()
        p.barrier()

        p.set_serial(True)
        with ExitStack() as ph:
            sbd = lambda n, s, d=F32: ph.enter_context(p.sbuf(n, s, d))
            psd = lambda n, s, d=F32: ph.enter_context(p.psum(n, s, d))
            wdT = sbd("wdT", [128, EC * IC * 128], BF16)
            swd = sbd("swd", [128, EC])
            h2t = sbd("h2t", [128, IC * T], BF16)
            rsd_pp = sbd("rsd_pp", [128, 1])
            sd_pp = sbd("sd_pp", [128, 1])
            sdr = sbd("sdr", [1, 4])
            gm2 = sbd("gm2", [1, 1])
            ps_p = psd("ps_pd", [128, T])
            ps_p2 = psd("ps_pd2", [128, T])
            ps_4 = psd("ps_4d", [128, 4])
            psq = psd("psqd", [128, 128], BF16)

            p.dep_on(mW2, m4c)
            emit_sh_read(p, sh2, nc_sb, sh_rd, sh_msk, sh_cnd, gm2[0:1, 0:1])
            make_scale_g(p, gm2[0:1, 0:1], clip_down, sdr, ones128, ps_4,
                         rsd_pp, sd_pp)
            with ExitStack() as phw:
                sbw = lambda n, s, d=F32: phw.enter_context(p.sbuf(n, s, d))
                stage = sbw("staged", [128, IF])
                yq = sbw("yqd", [128, IF])
                i1 = sbw("i1d", [128, IF], I32)
                i2 = sbw("i2d", [128, IF], I32)
                tfq = sbw("tfqd", [128, IF])
                wi_b = sbw("wi_bd", [128, IF], BF16)
                rm = sbw("rmd", [128, 1])
                for hc in range(EC):
                    p.dma(stage[:, 0:IF], w_down[hc * 128:(hc + 1) * 128, :])
                    p.red(rm, stage[:, 0:IF], ALU.max, absval=True)
                    p.ts(swd[:, hc:hc + 1], rm, 1.0 / 127.0, None, ALU.mult)
                    p.recip(rm, rm)
                    p.ts(rm, rm, 127.0, None, ALU.mult)
                    p.act(yq[:, 0:IF], stage[:, 0:IF], AF.Copy, bias=256.5, scale=rm)
                    quant_round(p, yq, i1, i2, tfq, wi_b, IF)
                    for ic in range(IC):
                        p.trans(psq[:, 0:128], wi_b[:, ic * 128:(ic + 1) * 128],
                                ident_b[:])
                        p.copy(wdT[:, (hc * IC + ic) * 128:(hc * IC + ic) * 128 + 128],
                               psq[:, 0:128])
                    p.tt(swd[:, hc:hc + 1], swd[:, hc:hc + 1], sd_pp[:, 0:1], ALU.mult)
            p.barrier()
            DB = []
            for _i in range(2):
                DB.append(dict(
                    h2q=sbd(f"h2qD{_i}", [128, IC * T], BF16),
                    s_b=sbd(f"s_bD{_i}", [128, EC * T], BF16),
                    xt=sbd(f"xtD{_i}", [128, EC * T]),
                    yq=sbd(f"yqD{_i}", [128, T]),
                    i1=sbd(f"i1D{_i}", [128, T], I32),
                    i2=sbd(f"i2D{_i}", [128, T], I32),
                    tfq=sbd(f"tfD{_i}", [128, T]),
                    scr=sbd(f"scrD{_i}", [128, T]),
                    scr2=sbd(f"scr2D{_i}", [128, T]),
                ))
            p.set_serial(False)

            for ti in range(NT):
                t0 = ti * T
                db = DB[ti % 2]
                h2q, s_b, xt = db["h2q"], db["s_b"], db["xt"]
                p.dma(tile_ap(h2t, IC, T), fm_slice(h2c, t0, T, nchunk=IC))
                p.dma(tile_ap(s_b, EC, T), fm_slice(s_c, t0, T))
                p.dma(tile_ap(xt, EC, T), fm_slice(x, t0, T))
                for ic in range(IC):
                    qb = DB[ic % 2]
                    yq, i1, i2, tfq = qb["yq"], qb["i1"], qb["i2"], qb["tfq"]
                    p.act(yq[:, 0:T], h2t[:, ic * T:(ic + 1) * T], AF.Copy,
                          bias=256.5, scale=rsd_pp[:, 0:1])
                    p.copy(i1[:, 0:T], yq[:, 0:T])
                    p.stt(i2[:, 0:T], yq[:, 0:T], -0.5, i1[:, 0:T], ALU.add,
                          ALU.subtract)
                    p.stt(tfq[:, 0:T], i1[:, 0:T], -256.0, i2[:, 0:T], ALU.add,
                          ALU.add)
                    p.ts(h2q[:, ic * T:(ic + 1) * T], tfq[:, 0:T], 127.0, -128.0,
                         ALU.min, ALU.max, eng="gpsimd")
                for mc in range(EC):
                    mb = DB[mc % 2]
                    psx = ps_p if mc % 2 == 0 else ps_p2
                    scr, scr2 = mb["scr"], mb["scr2"]
                    for ic in range(IC):
                        p.mm(psx[:, 0:T],
                             wdT[:, (mc * IC + ic) * 128:(mc * IC + ic) * 128 + 128],
                             h2q[:, ic * T:(ic + 1) * T],
                             start=(ic == 0), stop=(ic == IC - 1))
                    p.act(scr[:, 0:T], psx[:, 0:T], AF.Silu, scale=swd[:, mc:mc + 1])
                    p.tt(scr[:, 0:T], scr[:, 0:T], s_b[:, mc * T:(mc + 1) * T],
                         ALU.mult)
                    p.tt(scr2[:, 0:T], scr[:, 0:T], xt[:, mc * T:(mc + 1) * T],
                         ALU.add, eng="gpsimd")
                    p.dma(fm_slice_one(o_out, t0, T, mc), row3(scr2.ap()[:, 0:T], T),
                          eng="gpsimd")

        p.emit(ctxs)
    return p


def build_fused():
    p = P(serial=False)
    x = p.din("x", [E, L])
    xe = p.din("xe", [3 * E, L])
    Wq = p.din("Wq", [E, E])
    Wv = p.din("Wv", [E, E])
    vecs = p.din("vecs", [128, 6 * EC])
    Wk = p.din("Wk", [E, E])
    Wpk = p.din("Wpk", [E, E])
    veck = p.din("veck", [128, 4 * EC])
    bpk_row = p.din("bpk_row", [1, E])
    w_gate = p.din("w_gate", [IF, E])
    w_up = p.din("w_up", [IF, E])
    clip_x = p.din("clip_x", [1, 1])
    Wpq = p.din("Wpq", [E, E])
    veca = p.din("veca", [128, 4 * EC])
    cwT = p.din("cwT", [5 * 64, E])
    gamma = p.din("gamma", [1, 1])
    W1 = p.din("W1", [E, MLP])
    b1 = p.din("b1", [128, MC])
    W2 = p.din("W2", [MLP, E])
    b2 = p.din("b2", [128, EC])
    w_down = p.din("w_down", [E, IF])
    clip_down = p.din("clip_down", [1, 1])
    sel = p.din("sel", [EC * 16, 128])
    nonce4 = p.din("nonce4", [4, 2])
    nonce_r = p.din("nonce_r", [1, 16])

    o_out = p.dout("out", [E, L])

    xnc = p.dtmp("xnc", [E, L], BF16)
    xne = p.dtmp("xne", [3 * E, L], BF16)
    Qc = p.dtmp("Qc", [E, L], BF16)
    Vc = p.dtmp("Vc", [E, L], BF16)
    Kc = p.dtmp("Kc", [E, L], BF16)
    qg_d = p.dtmp("qg_d", [128, EC])
    scores_d = p.dtmp("scores_d", [16, L])
    h2c = p.dtmp("h2c", [IF, L], BF16)
    kv_d = p.dtmp("kv_d", [128, H * 32])
    ks_d = p.dtmp("ks_d", [16, E])
    attn2c = p.dtmp("attn2c", [E, L], BF16)
    h1c = p.dtmp("h1c", [MLP, L], BF16)
    s_c = p.dtmp("s_c", [E, L], BF16)
    bnc = p.dtmp("bnc", [128, 1])
    bnc16 = p.dtmp("bnc16", [16, 1])
    bnc4a = p.dtmp("bnc4a", [4, 16])
    bnc4b = p.dtmp("bnc4b", [4, 16])
    sh1 = p.nc.dram_tensor("sh1", [8, 16], F32, kind="Internal",
                           addr_space="Shared").ap()
    sh2 = p.nc.dram_tensor("sh2", [8, 16], F32, kind="Internal",
                           addr_space="Shared").ap()

    ctxs = ExitStack()
    with ctxs:
        sb = lambda n, s_, d=F32: ctxs.enter_context(p.sbuf(n, s_, d))

        ones1 = sb("ones1", [128, 1], BF16)
        ones128 = sb("ones128", [1, 128], F32)
        ident_b = sb("ident_b", [128, 128], BF16)
        rrow = sb("rrow", [1, 128])
        mrow = sb("mrow", [1, 16])
        rtmp = sb("rtmp", [128, 1])
        w4 = sb("w4", [4, 16])
        nc_sb = sb("nc_sb", [1, 16])
        sh_rd = sb("sh_rd", [1, 128])
        sh_msk = sb("sh_msk", [1, 16], I32)
        sh_cnd = sb("sh_cnd", [1, 16])
        stage_g = sb("stage_g", [128, E])
        stage_g2 = sb("stage_g2", [128, E])
        ones1f = sb("ones1f", [128, 1], F32)

        p.memset(ones1[:], 1.0)
        p.memset(ones1f[:], 1.0)
        p.memset(ones128[:], 1.0)
        p.identity_fill(ident_b[:], 128)
        p.dma(nc_sb[:], nonce_r)

        with ExitStack() as pha:
            sba = lambda n, s_, d=F32: pha.enter_context(p.sbuf(n, s_, d))
            psa = lambda n, s_, d=F32: pha.enter_context(p.psum(n, s_, d))
            arnA = sba("arnA", [128, IC * EC * 128], BF16)
            arnB = sba("arnB", [128, IC * EC * 128], BF16)
            xqA = (sba("xq0", [128, EC * T], BF16),
                   sba("xq1", [128, EC * T], BF16))
            yqA = sba("yqA", [128, T])
            i1A = sba("i1A", [128, T], I32)
            i2A = sba("i2A", [128, T], I32)
            tfA = sba("tfA", [128, T])
            wiqB = (sba("wi_bA0", [128, T], BF16), sba("wi_bA1", [128, T], BF16))
            gsA = sba("gsA", [128, T])
            usA = sba("usA", [128, T])
            h2fA = sba("h2fA", [128, T])
            _h2b = sba("h2b0", [128, T], BF16)
            h2bA = (_h2b, _h2b)
            sg = sba("sgA", [128, IC])
            su = sba("suA", [128, IC])
            rm_a = sba("rm_aA", [128, 1])
            hmax = sba("hmaxA", [128, 1])
            sdr_a = sba("sdr_aA", [1, 4])
            rsx_pp = sba("rsx_ppA", [128, 1])
            sx_pp = sba("sx_ppA", [128, 1])
            gm1 = sba("gm1A", [1, 1])
            ps_g = psa("ps_gF", [128, T])
            ps_u = psa("ps_uF", [128, T])

            def aprep_unit(wi, ic):
                p._ser_now = "A"
                w_hbm, dstT, sc = ((w_gate, arnA, sg) if wi == 0
                                   else (w_up, arnB, su))
                stA = stageA[(wi * IC + ic) % 2]
                wiq = wiqB[(wi * IC + ic) % 2]
                p.dma(stA[:, 0:E], w_hbm[ic * 128:(ic + 1) * 128, :])
                p.red(rm_a, stA[:, 0:E], ALU.max, absval=True)
                p.ts(sc[:, ic:ic + 1], rm_a, 1.0 / 127.0, None, ALU.mult)
                p.recip(rm_a, rm_a)
                p.ts(rm_a, rm_a, 127.0, None, ALU.mult)
                for hf in range(2):
                    p.act(yqA[:, 0:T], stA[:, hf * T:(hf + 1) * T], AF.Copy,
                          bias=256.5, scale=rm_a)
                    quant_round(p, yqA, i1A, i2A, tfA, wiq, T)
                    for q in range(4):
                        kc = hf * 4 + q
                        psqx = psqB[kc % 2]
                        p.trans(psqx[:, 0:128], wiq[:, q * 128:(q + 1) * 128],
                                ident_b[:])
                        p.copy(dstT[:, (ic * EC + kc) * 128:
                                    (ic * EC + kc) * 128 + 128],
                               psqx[:, 0:128])

            def a_units():
                for bi, (src, c0) in enumerate([(xnc, 0), (xne, 0), (xne, EC),
                                                (xne, 2 * EC)]):
                    p._ser_now = None
                    p.memset(hmax[:], 0.0, eng="vector")
                    for ti in range(NT):
                        p._ser_now = None
                        xq = xqA[ti % 2]
                        t0 = ti * T
                        p.dma(tile_ap(xq, EC, T), fm_slice_off(src, t0, T, c0))
                        for c in range(EC):
                            seg = slice(c * T, (c + 1) * T)
                            p.act(yqA[:, 0:T], xq[:, seg], AF.Copy, bias=256.5,
                                  scale=rsx_pp[:, 0:1])
                            p.copy(i1A[:, 0:T], yqA[:, 0:T])
                            p.stt(i2A[:, 0:T], yqA[:, 0:T], -0.5, i1A[:, 0:T],
                                  ALU.add, ALU.subtract)
                            p.stt(tfA[:, 0:T], i1A[:, 0:T], -256.0, i2A[:, 0:T],
                                  ALU.add, ALU.add)
                            p.ts(xq[:, seg], tfA[:, 0:T], 127.0, -128.0,
                                 ALU.min, ALU.max, eng="gpsimd")
                        for ic in range(IC):
                            for kc in range(EC):
                                p.mm(ps_g[:, 0:T],
                                     arnA[:, (ic * EC + kc) * 128:
                                          (ic * EC + kc) * 128 + 128],
                                     xq[:, kc * T:(kc + 1) * T],
                                     start=(kc == 0), stop=(kc == EC - 1))
                            p.act(gsA[:, 0:T], ps_g[:, 0:T], AF.Silu,
                                  scale=sg[:, ic:ic + 1])
                            for kc in range(EC):
                                p.mm(ps_u[:, 0:T],
                                     arnB[:, (ic * EC + kc) * 128:
                                          (ic * EC + kc) * 128 + 128],
                                     xq[:, kc * T:(kc + 1) * T],
                                     start=(kc == 0), stop=(kc == EC - 1))
                            p.ts(usA[:, 0:T], ps_u[:, 0:T], su[:, ic:ic + 1],
                                 None, ALU.mult)
                            p.tt(h2fA[:, 0:T], gsA[:, 0:T], usA[:, 0:T],
                                 ALU.mult)
                            if bi == 0:
                                h2b = h2bA[ic % 2]
                                p.copy(h2b[:, 0:T], h2fA[:, 0:T], eng="gpsimd")
                                p.dma(fm_slice_one(h2c, t0, T, ic),
                                      row3(h2b.ap()[:, 0:T], T), eng="gpsimd")
                            p.red(rtmp, h2fA[:, 0:T], ALU.max, absval=True)
                            p.tt(hmax, hmax, rtmp, ALU.max)
                        yield
                    cross_part_max(p, hmax, bnc, rrow, mrow[0:1, 0:1])
                    p.dma(bnc4b[bi:bi + 1, 0:1], mrow[0:1, 0:1])
                emit_sh_payload(p, bnc4b, nonce4, w4, sh2)
                yield

            ag = a_units()

            def adv(n):
                for _ in range(n):
                    next(ag, None)

            with ExitStack() as ph:
                sbr = lambda n, s_, d=F32: ph.enter_context(p.sbuf(n, s_, d))
                psr = lambda n, s_, d=F32: ph.enter_context(p.psum(n, s_, d))
                RS = []
                for _i in range(2):
                    RS.append(dict(
                        xt=sbr(f"xtR{_i}", [128, EC * T]),
                        xn_f=sbr(f"xn_fR{_i}", [128, T]),
                        xn_f2=sbr(f"xn_f2R{_i}", [128, T]),
                        rt=sbr(f"rtR{_i}", [128, 1]),
                        rows=sbr(f"rowsR{_i}", [1, 2 * T]),
                        ps_mu=psr(f"ps_muR{_i}", [128, T]),
                    ))
                _xob = sbr("xob0R", [128, EC * T], BF16)
                xob = (_xob, _xob)
                sqf = sbr("sqfR", [128, EC * T], BF16)
                rs_sb = sbr("rs_sbR", [128, T])
                vmax = sbr("vmaxR", [128, 1])
                stageA = (stage_g, sbr("stageA1", [128, E]))
                ps_s1 = psr("ps_s1R", [1, T])
                ps_s2 = psr("ps_s2R", [1, T])
                psqB = (psr("psqR0", [128, 128], BF16),
                        psr("psqR1", [128, 128], BF16))

                def r_load(bi, ti):
                    p._ser_now = None
                    S = RS[ti % 2]
                    t0 = ti * T
                    src, c0 = (x, 0) if bi == 0 else (xe, (bi - 1) * EC)
                    p.dma(tile_ap(S["xt"], EC, T),
                          fm_slice_off(src, t0, T, c0),
                          eng=("sync" if ti % 2 == 0 else "gpsimd"))

                def r_stats(bi, ti):
                    p._ser_now = None
                    S = RS[ti % 2]
                    xt = S["xt"]
                    rows = S["rows"]
                    mu, rstd = rows[0:1, 0:T], rows[0:1, T:2 * T]
                    inv_n = 1.0 / E
                    p.act(sqf[:, 0:EC * T], xt[:, 0:EC * T], AF.Square)
                    for c in range(EC):
                        seg = slice(c * T, (c + 1) * T)
                        p.mm(ps_s1[0:1, 0:T], ones1f, xt[:, seg],
                             start=(c == 0), stop=(c == EC - 1))
                        p.mm(ps_s2[0:1, 0:T], ones1, sqf[:, seg],
                             start=(c == 0), stop=(c == EC - 1))
                    scr_r = S["xn_f"][0:1, 0:T]
                    p.ts(mu, ps_s1[0:1, 0:T], inv_n, None, ALU.mult)
                    p.tt(scr_r, mu, mu, ALU.mult)
                    p.stt(rstd, ps_s2[0:1, 0:T], inv_n, scr_r,
                          ALU.mult, ALU.subtract)
                    p.ts(rstd, rstd, LN_EPS, None, ALU.add)
                    p.act(rstd, rstd, AF.Sqrt)
                    p.recip(rstd, rstd)

                def r_apply(bi, ti):
                    p._ser_now = None
                    S = RS[ti % 2]
                    xt = S["xt"]
                    xo = xob[ti % 2]
                    rows = S["rows"]
                    mu_, rs_ = rows[0:1, 0:T], rows[0:1, T:2 * T]
                    t0 = ti * T
                    p.mm(S["ps_mu"][:, 0:T], ones128, rs_)
                    p.act(rs_sb[:, 0:T], S["ps_mu"][:, 0:T], AF.Copy)
                    p.mm(S["ps_mu"][:, 0:T], ones128, mu_)
                    for c in range(EC):
                        seg = slice(c * T, (c + 1) * T)
                        xf = S["xn_f"] if c % 2 == 0 else S["xn_f2"]
                        p.tt(xf[:, 0:T], xt[:, seg], S["ps_mu"][:, 0:T],
                             ALU.subtract)
                        p.tt(xo[:, seg], xf[:, 0:T], rs_sb[:, 0:T],
                             ALU.mult, eng="gpsimd")
                    p.red(S["rt"], xo[:, 0:EC * T], ALU.max, absval=True)
                    p.tt(vmax, vmax, S["rt"], ALU.max)
                    dst, dc0 = (xnc, 0) if bi == 0 else (xne, (bi - 1) * EC)
                    p.dma(fm_slice_off(dst, t0, T, dc0), tile_ap(xo, EC, T),
                          eng="gpsimd")

                j = 0
                for bi in range(4):
                    p.memset(vmax[:], 0.0, eng="vector")
                    r_load(bi, 0)
                    r_load(bi, 1)
                    r_stats(bi, 0)
                    for ti in range(NT):
                        if ti + 1 < NT:
                            r_stats(bi, ti + 1)
                        r_apply(bi, ti)
                        if ti + 2 < NT:
                            r_load(bi, ti + 2)
                        u = bi * NT + ti + 1
                        while ("noaprep" not in EXP) and j < 2 * IC and j * 32 < u * 2 * IC:
                            aprep_unit(j // IC, j % IC)
                            j += 1
                    cross_part_max(p, vmax, bnc, rrow, mrow[0:1, 0:1])
                    p.dma(bnc4a[bi:bi + 1, 0:1], mrow[0:1, 0:1])
                while ("noaprep" not in EXP) and j < 2 * IC:
                    aprep_unit(j // IC, j % IC)
                    j += 1
                p._ser_now = None
                emit_sh_payload(p, bnc4a, nonce4, w4, sh1)
                mW1 = p.mark()
            p.fence()
            if BUILD_STOP == "era1":
                p.emit(ctxs)
                return p

            with ExitStack() as ph:
                sbq = lambda n, s_, d=F32: ph.enter_context(p.sbuf(n, s_, d))
                psv = lambda n, s_, d=F32: ph.enter_context(p.psum(n, s_, d))
                wb16 = sbq("wb16Q", [128, EC * E], BF16)
                xn_inB = (sbq("xn_in0", [128, EC * T], BF16),
                          sbq("xn_in1", [128, EC * T], BF16))
                qpreB = (sbq("qpre0", [128, EC * T], BF16),
                         sbq("qpre1", [128, EC * T], BF16))
                sqFQ = sbq("sqFQ", [128, EC * T], BF16)
                rows = sbq("rowsQ", [1, 3 * T])
                muB = (sbq("mu_sbQ0", [128, T]), sbq("mu_sbQ1", [128, T]))
                rsB = (sbq("rs_sbQ0", [128, T]), sbq("rs_sbQ1", [128, T]))
                scrB = (sbq("scrQ0", [128, T]), sbq("scrQ1", [128, T]))
                qsum = sbq("qsumQ", [128, EC])
                vec_sb = sbq("vecQ", [128, 6 * EC])
                ps_p0 = psv("ps_p0Q", [128, T])
                ps_p1 = psv("ps_p1Q", [128, T])
                ps_mu = psv("ps_muQ", [128, T])
                ps_s1 = psv("ps_s1Q", [1, T])
                ps_s2 = psv("ps_s2Q", [1, T])
                mu2_ = rows[0:1, 0:T]
                rs2_ = rows[0:1, T:2 * T]
                scr1_ = rows[0:1, 2 * T:3 * T]

                p.set_serial(True, "B")
                p.dma(vec_sb[:], vecs)
                p.memset(qsum[:], 0.0)
                p.set_serial(False)

                def proj_head(ti, b0, vec):
                    qpre, xn_in = qpreB[ti % 2], xn_inB[ti % 2]
                    t0 = ti * T
                    p.dma(tile_ap(xn_in, EC, T), fm_slice(xnc, t0, T))
                    for mc in range(EC):
                        psx = ps_p0 if mc % 2 == 0 else ps_p1
                        for kc in range(EC):
                            p.mm(psx[:, 0:T],
                                 wb16[:, kc * E + mc * 128: kc * E + mc * 128 + 128],
                                 xn_in[:, kc * T:(kc + 1) * T],
                                 start=(kc == 0), stop=(kc == EC - 1))
                        p.act(qpre[:, mc * T:(mc + 1) * T], psx[:, 0:T],
                              AF.Identity, bias=vec[:, b0 * EC + mc: b0 * EC + mc + 1])

                def proj_tail(ti, b0, vec, out_hbm, do_q):
                    qpre = qpreB[ti % 2]
                    mu_sb, rs_sb, scr = muB[ti % 2], rsB[ti % 2], scrB[ti % 2]
                    ln_stats3(p, qpre, sqFQ, ones1, ps_s1, ps_s2, T, mu2_,
                              rs2_, scr1_)
                    p.mm(ps_mu[:, 0:T], ones128, rs2_)
                    p.act(rs_sb[:, 0:T], ps_mu[:, 0:T], AF.Copy)
                    p.mm(ps_mu[:, 0:T], ones128, mu2_)
                    for c in range(EC):
                        seg = slice(c * T, (c + 1) * T)
                        p.tt(scr[:, 0:T], qpre[:, seg], ps_mu[:, 0:T],
                             ALU.subtract)
                        p.ts(qpre[:, seg], scr[:, 0:T],
                             vec[:, (b0 + 1) * EC + c:(b0 + 1) * EC + c + 1],
                             None, ALU.mult_mult if False else ALU.mult,
                             eng="gpsimd") if False else None
                        p.tt(scr[:, 0:T], scr[:, 0:T], rs_sb[:, 0:T], ALU.mult)
                        p.ts(qpre[:, seg], scr[:, 0:T],
                             vec[:, (b0 + 1) * EC + c:(b0 + 1) * EC + c + 1],
                             vec[:, (b0 + 2) * EC + c:(b0 + 2) * EC + c + 1],
                             ALU.mult, ALU.add, eng="gpsimd")
                    p.dma(fm_slice(out_hbm, ti * T, T), tile_ap(qpre, EC, T),
                          eng="gpsimd")
                    if do_q:
                        for c in range(EC):
                            p.red(rtmp, qpre[:, c * T:(c + 1) * T], ALU.add)
                            p.tt(qsum[:, c:c + 1], qsum[:, c:c + 1], rtmp,
                                 ALU.add)

                for (w_hbm, b0, out_hbm, do_q) in [(Wq, 0, Qc, True),
                                                   (Wv, 3, Vc, False)]:
                    load_weight_bf16_2q(p, wb16, w_hbm, E, E, stage_g, stage_g2)
                    proj_head(0, b0, vec_sb)
                    for ti in range(NT):
                        if ti + 1 < NT:
                            proj_head(ti + 1, b0, vec_sb)
                        proj_tail(ti, b0, vec_sb, out_hbm, do_q)
                p.ts(qsum[:], qsum[:], 1.0 / (L * np.sqrt(D)), None, ALU.mult)
                p.dma(qg_d, qsum[:])
            p.fence()
            if BUILD_STOP == "era2":
                p.emit(ctxs)
                return p

            with ExitStack() as ph:
                sbk = lambda n, s_, d=F32: ph.enter_context(p.sbuf(n, s_, d))
                psk = lambda n, s_, d=F32: ph.enter_context(p.psum(n, s_, d))
                wb16 = sbk("wb16K", [128, EC * E], BF16)
                xn_inB = (sbk("xn_in0K", [128, EC * T], BF16),
                          sbk("xn_in1K", [128, EC * T], BF16))
                _kpre = sbk("kpre0", [128, EC * T], BF16)
                qpreB = (_kpre, _kpre)
                sqFK = sbk("sqFK", [128, EC * T], BF16)
                rows = sbk("rowsK", [1, 3 * T])
                muKB = (sbk("mu_sbK0", [128, T]), sbk("mu_sbK1", [128, T]))
                rsKB = (sbk("rs_sbK0", [128, T]), sbk("rs_sbK1", [128, T]))
                scrKB = (sbk("scrK0", [128, T]), sbk("scrK1", [128, T]))
                qsl = sbk("qslK", [128, EC * 16], BF16)
                qg_sb = sbk("qg_sbK", [128, EC])
                _scbK = sbk("scbK", [16, T])
                scbB = (_scbK, _scbK)
                vec_sb = sbk("vecK", [128, 4 * EC])
                ps_p0 = psk("ps_p0K", [128, T])
                ps_p1 = psk("ps_p1K", [128, T])
                ps_mu = psk("ps_muK", [128, T])
                ps_s1 = psk("ps_s1K", [1, T])
                ps_s2 = psk("ps_s2K", [1, T])
                ps_sc = psk("ps_scK", [16, T])
                mu2_ = rows[0:1, 0:T]
                rs2_ = rows[0:1, T:2 * T]
                scr1_ = rows[0:1, 2 * T:3 * T]

                p.set_serial(True, "B")
                p.dma(vec_sb[:], veck)
                p.dma(qg_sb[:], qg_d)
                p.memset(qsl[:], 0.0)
                for c in range(EC):
                    p.copy(qsl[0:64, c * 16 + 2 * c: c * 16 + 2 * c + 1],
                           qg_sb[0:64, c:c + 1])
                    p.copy(qsl[64:128, c * 16 + 2 * c + 1: c * 16 + 2 * c + 2],
                           qg_sb[64:128, c:c + 1])
                p.set_serial(False)
                load_weight_bf16_2q(p, wb16, Wk, E, E, stage_g, stage_g2)

                def head_k(ti):
                    xn_in = xn_inB[ti % 2]
                    t0 = ti * T
                    p.dma(tile_ap(xn_in, EC, T), fm_slice(xnc, t0, T))

                def proj_tile_k(ti):
                    qpre, xn_in = qpreB[ti % 2], xn_inB[ti % 2]
                    mu_sb, rs_sb, scr = muKB[ti % 2], rsKB[ti % 2], scrKB[ti % 2]
                    t0 = ti * T
                    for mc in range(EC):
                        psx = ps_p0 if mc % 2 == 0 else ps_p1
                        for kc in range(EC):
                            p.mm(psx[:, 0:T],
                                 wb16[:, kc * E + mc * 128: kc * E + mc * 128 + 128],
                                 xn_in[:, kc * T:(kc + 1) * T],
                                 start=(kc == 0), stop=(kc == EC - 1))
                        p.act(qpre[:, mc * T:(mc + 1) * T], psx[:, 0:T],
                              AF.Identity, bias=vec_sb[:, mc:mc + 1])
                    ln_stats3(p, qpre, sqFK, ones1, ps_s1, ps_s2, T, mu2_,
                              rs2_, scr1_)
                    p.mm(ps_mu[:, 0:T], ones128, rs2_)
                    p.act(rs_sb[:, 0:T], ps_mu[:, 0:T], AF.Copy)
                    p.mm(ps_mu[:, 0:T], ones128, mu2_)
                    for c in range(EC):
                        seg = slice(c * T, (c + 1) * T)
                        p.tt(scr[:, 0:T], qpre[:, seg], ps_mu[:, 0:T],
                             ALU.subtract)
                        p.tt(scr[:, 0:T], scr[:, 0:T], rs_sb[:, 0:T], ALU.mult)
                        p.ts(qpre[:, seg], scr[:, 0:T],
                             vec_sb[:, EC + c:EC + c + 1],
                             vec_sb[:, 2 * EC + c:2 * EC + c + 1],
                             ALU.mult, ALU.add, eng="gpsimd")
                    p.dma(fm_slice(Kc, t0, T), tile_ap(qpre, EC, T),
                          eng="gpsimd")
                    scb = scbB[ti % 2]
                    for kc in range(EC):
                        p.mm(ps_sc[0:16, 0:T], qsl[:, kc * 16:(kc + 1) * 16],
                             qpre[:, kc * T:(kc + 1) * T],
                             start=(kc == 0), stop=(kc == EC - 1))
                    p.copy(scb[0:16, 0:T], ps_sc[0:16, 0:T])
                    p.dma(scores_d[0:16, t0:t0 + T], scb[0:16, 0:T])

                head_k(0)
                for ti in range(3):
                    head_k(ti + 1)
                    proj_tile_k(ti)
                p._ser_now = None
                p.dep_on(mW1, p.mark())
                emit_sh_read(p, sh1, nc_sb, sh_rd, sh_msk, sh_cnd,
                             gm1[0:1, 0:1])
                make_scale_g(p, gm1[0:1, 0:1], clip_x, sdr_a, ones128,
                             ps_mu[:, 0:4], rsx_pp, sx_pp)
                for ic in range(IC):
                    p.tt(sg[:, ic:ic + 1], sg[:, ic:ic + 1], sx_pp[:, 0:1],
                         ALU.mult)
                    p.tt(su[:, ic:ic + 1], su[:, ic:ic + 1], sx_pp[:, 0:1],
                         ALU.mult)
                for ti in range(3, NT):
                    if ti + 1 < NT:
                        head_k(ti + 1)
                    proj_tile_k(ti)
                    adv(1)
            p.fence()

            with ExitStack() as ph:
                sbc = lambda n, s_, d=F32: ph.enter_context(p.sbuf(n, s_, d))
                psc = lambda n, s_, d=F32: ph.enter_context(p.psum(n, s_, d))
                wb16 = sbc("wb16P3", [128, EC * E], BF16)
                ident_f = sbc("ident_f3", [128, 128], F32)
                a_tm_f = sbc("a_tm_f3", [128, (L // 128) * 16])
                a_tm_b = sbc("a_tm_b3", [128, (L // 128) * 16], BF16)
                a0r_b = sbc("a0r_b3", [1, 16], BF16)
                a0row = sbc("a0row3", [1, 16])
                pk0_b = sbc("pk0_b3", [1, E], BF16)
                pk0_f = sbc("pk0_f3", [1, E])
                kvout = sbc("kvout3", [128, H * 32])
                ksout = sbc("ksout3", [16, E])
                vec_sb = sbc("vecP3", [128, 4 * EC])

                p.set_serial(True, "B")
                p.identity_fill(ident_f[:], 128)
                p.dma(vec_sb[:], veck)
                with ExitStack() as phs:
                    scores = phs.enter_context(p.sbuf("scores3", [16, L]))
                    srow16 = phs.enter_context(p.sbuf("srow163", [16, 4]))
                    ptr_f = phs.enter_context(p.psum("ptr_f3", [128, 16], F32))
                    p.dma(scores[0:16, :], scores_d)
                    p.red(srow16[0:16, 0:1], scores[:], ALU.max)
                    p.ts(srow16[0:16, 0:1], srow16[0:16, 0:1], 0.0, None,
                         ALU.max)
                    p.ts(srow16[0:16, 1:2], srow16[0:16, 0:1], -1.0, None,
                         ALU.mult)
                    p.act(scores[:], scores[:], AF.Exp, bias=srow16[0:16, 1:2])
                    p.red(srow16[0:16, 2:3], scores[:], ALU.add)
                    p.act(srow16[0:16, 3:4], srow16[0:16, 1:2], AF.Exp)
                    p.tt(srow16[0:16, 2:3], srow16[0:16, 2:3],
                         srow16[0:16, 3:4], ALU.add)
                    p.recip(srow16[0:16, 2:3], srow16[0:16, 2:3])
                    p.ts(srow16[0:16, 2:3], srow16[0:16, 2:3], float(L), None,
                         ALU.mult)
                    p.ts(scores[:], scores[:], srow16[0:16, 2:3], None,
                         ALU.mult)
                    p.tt(srow16[0:16, 3:4], srow16[0:16, 3:4],
                         srow16[0:16, 2:3], ALU.mult)
                    p.dma(bnc16, srow16[0:16, 3:4])
                    p.dma(a0row[0:1, 0:16], bnc16.rearrange("p c -> c p"))
                    p.copy(a0r_b[0:1, 0:16], a0row[0:1, 0:16])
                    for tc in range(L // 128):
                        p.trans(ptr_f[:, 0:16],
                                scores[0:16, tc * 128:(tc + 1) * 128],
                                ident_f[0:16, 0:16])
                        p.copy(a_tm_f[:, tc * 16:(tc + 1) * 16], ptr_f[:, 0:16])
                p.fence()
                p.copy(a_tm_b[:], a_tm_f[:])
                p.dma(pk0_f[0:1, 0:E], bpk_row)
                p.act(pk0_f[0:1, 0:E], pk0_f[0:1, 0:E], AF.Tanh)
                p.ts(pk0_b[0:1, 0:E], pk0_f[0:1, 0:E], 1.0, None, ALU.add)
                p.set_serial(False)
                adv(2)

                k_bf = sbc("k_bf3", [128, EC * T], BF16)
                v_bf = sbc("v_bf3", [128, EC * T], BF16)
                pk_bf = sbc("pk_bf3", [128, EC * T], BF16)
                pkt_b = sbc("pkt_b3", [128, E], BF16)
                vst_b = sbc("vst_b3", [128, E], BF16)
                scr3B = (sbc("scr3a", [128, T]), sbc("scr3b", [128, T]))
                ps_p3a = psc("ps_p3a", [128, T])
                ps_p3b = psc("ps_p3b", [128, T])
                ptr = psc("ptr3", [128, 1024], BF16)
                ps_kv = psc("ps_kv3", [128, H * 32])
                ps_ks = psc("ps_ks3", [16, E])

                p.set_serial(True, "B")
                load_weight_bf16_2q(p, wb16, Wpk, E, E, stage_g, stage_g2)
                p.memset(ps_kv[:], 0.0, eng="vector")
                p.memset(ps_ks[0:16, :], 0.0, eng="vector")
                for c in range(EC):
                    p.mm(ps_ks[0:16, c * 128:(c + 1) * 128], a0r_b[0:1, 0:16],
                         pk0_b[0:1, c * 128:(c + 1) * 128],
                         start=False, stop=False)
                p.set_serial(False)

                for ti in range(NT):
                    t0 = ti * T
                    p.dma(tile_ap(k_bf, EC, T), fm_slice(Kc, t0, T))
                    p.dma(tile_ap(v_bf, EC, T), fm_slice(Vc, t0, T))
                    for mc in range(EC):
                        psx = ps_p3a if mc % 2 == 0 else ps_p3b
                        scr3 = scr3B[mc % 2]
                        for kc in range(EC):
                            p.mm(psx[:, 0:T],
                                 wb16[:, kc * E + mc * 128: kc * E + mc * 128 + 128],
                                 k_bf[:, kc * T:(kc + 1) * T],
                                 start=(kc == 0), stop=(kc == EC - 1))
                        p.act(scr3[:, 0:T], psx[:, 0:T], AF.Tanh,
                              bias=vec_sb[:, 3 * EC + mc:3 * EC + mc + 1])
                        p.ts(pk_bf[:, mc * T:(mc + 1) * T], scr3[:, 0:T], 1.0,
                             None, ALU.add, eng="gpsimd")
                    for tc in range(TC):
                        gtc = ti * TC + tc
                        last = (ti == NT - 1 and tc == TC - 1)
                        for c in range(EC):
                            p.trans(ptr[:, 0:128],
                                    pk_bf[:, c * T + tc * 128: c * T + tc * 128 + 128],
                                    ident_b[:])
                            p.act(pkt_b[:, c * 128:(c + 1) * 128], ptr[:, 0:128],
                                  AF.Copy)
                            p.trans(ptr[:, 128:256],
                                    v_bf[:, c * T + tc * 128: c * T + tc * 128 + 128],
                                    ident_b[:])
                            p.ts(vst_b[:, c * 128:c * 128 + 64], ptr[:, 128:192],
                                 a_tm_f[:, gtc * 16 + 2 * c: gtc * 16 + 2 * c + 1],
                                 None, ALU.mult)
                            p.ts(vst_b[:, c * 128 + 64:c * 128 + 128],
                                 ptr[:, 192:256],
                                 a_tm_f[:, gtc * 16 + 2 * c + 1: gtc * 16 + 2 * c + 2],
                                 None, ALU.mult)
                        for c in range(EC):
                            p.mm(ps_ks[0:16, c * 128:(c + 1) * 128],
                                 a_tm_b[:, gtc * 16:(gtc + 1) * 16],
                                 pkt_b[:, c * 128:(c + 1) * 128],
                                 start=False, stop=last)
                        for h in range(H):
                            c, r = h // 2, h % 2
                            p.mm(ps_kv[r * 64:(r + 1) * 64, c * 64:(c + 1) * 64],
                                 pkt_b[:, h * 64:(h + 1) * 64],
                                 vst_b[:, h * 64:(h + 1) * 64],
                                 start=False, stop=last)
                    adv(3)

                p.copy(kvout[:], ps_kv[:])
                p.dma(kv_d, kvout[:])
                p.copy(ksout[0:16, :], ps_ks[0:16, :])
                p.dma(ks_d, ksout[0:16, :])
                adv(3)
                mW2 = p.mark()
        p.fence()

        p.set_serial(True, "B")
        with ExitStack() as ph:
            sba = lambda n, s_, d=F32: ph.enter_context(p.sbuf(n, s_, d))
            psa = lambda n, s_, d=F32: ph.enter_context(p.psum(n, s_, d))
            wb16 = sba("wb16P4", [128, EC * E], BF16)
            vec = sba("vecP4", [128, 4 * EC])
            numT = sba("numT", [128, EC * 128], BF16)
            denT = sba("denT", [128, EC * 16], BF16)
            sel_b = sba("sel_b", [16, EC * 128], BF16)
            cvT = sba("cvT", [128, EC * 5 * 128], BF16)
            kv_f = sba("kv_f", [128, H * 32])
            kv_b = sba("kv_b", [128, H * 32], BF16)
            ksf = sba("ksf", [16, E])
            ks_b = sba("ks_b", [16, E], BF16)
            scr = sba("scrP4", [128, T])
            selt = sba("selt", [128, 128])
            q_bf = sba("q_bfP4", [128, EC * T], BF16)
            sq_bf = sba("sq_bfP4", [128, EC * T], BF16)
            pq_bf = sba("pq_bfP4", [128, EC * T], BF16)
            xn_f = sba("xn_fP4", [128, EC * T])
            attn_a = sba("attn_a", [128, EC * T], BF16)
            attn_b_ = sba("attn_b", [128, EC * T], BF16)
            tail2 = sba("tail2", [128, EC * 2], BF16)
            conv_win = sba("conv_win", [128, EC * (T + 4)], BF16)
            attn2_bf = sba("attn2_bf", [128, EC * T], BF16)
            rden = sba("rden", [16, T])
            rden_b = sba("rden_b", [16, T], BF16)
            rdb = sba("rdb", [128, T])
            gam_pp = sba("gam_pp", [128, 1])
            gcb = sba("gcb", [128, EC])
            mu_sb = sba("mu_sbP", [128, T])
            rs_sb = sba("rs_sbP", [128, T])
            rows = sba("rowsP4", [1, 3 * T])
            ps_p = psa("ps_p", [128, T])
            ps_dr = psa("ps_dr", [128, T])
            ps_a = psa("ps_a", [128, T])
            ps_pB = psa("ps_pB", [128, T])
            mu2_ = rows[0:1, 0:T]
            rs2_ = rows[0:1, T:2 * T]
            scr1_ = rows[0:1, 2 * T:3 * T]

            p.dma(vec[:], veca)
            load_weight_bf16(p, wb16, Wpq, E, E, stage_g)
            for c in range(EC):
                p.dma(selt[0:16, 0:128], sel[c * 16:(c + 1) * 16, :])
                p.copy(sel_b[0:16, c * 128:(c + 1) * 128], selt[0:16, 0:128])
            p.memset(cvT[:], 0.0)
            for c in range(EC):
                for k in range(5):
                    col = (c * 5 + k) * 128
                    p.dma(selt[0:64, 0:64],
                          cwT[k * 64:(k + 1) * 64, 2 * c * 64:(2 * c + 1) * 64])
                    p.copy(cvT[0:64, col:col + 64], selt[0:64, 0:64])
                    p.dma(selt[64:128, 0:64],
                          cwT[k * 64:(k + 1) * 64, (2 * c + 1) * 64:(2 * c + 2) * 64])
                    p.copy(cvT[64:128, col + 64:col + 128], selt[64:128, 0:64])
            p.dma(kv_f[:], kv_d)
            p.copy(kv_b[:], kv_f[:])
            p.memset(numT[:], 0.0)
            for c in range(EC):
                p.copy(numT[0:64, c * 128: c * 128 + 64],
                       kv_b[0:64, c * 64:(c + 1) * 64])
                p.copy(numT[64:128, c * 128 + 64: c * 128 + 128],
                       kv_b[64:128, c * 64:(c + 1) * 64])
            p.dma(ksf[0:16, :], ks_d)
            p.copy(ks_b[0:16, :], ksf[0:16, :])
            p.memset(denT[:], 0.0)
            with ExitStack() as pht:
                ptr_b = pht.enter_context(p.psum("ptr_b", [128, 16], BF16))
                ps_t4 = pht.enter_context(p.psum("ps_t4", [128, 4], F32))
                p.dma(rows[0:1, 0:1], gamma)
                p.mm(ps_t4[:, 0:1], ones128, rows[0:1, 0:1])
                p.copy(gam_pp[:, 0:1], ps_t4[:, 0:1])
                p.ts(gcb[:], vec[:, 3 * EC:4 * EC], gam_pp[:, 0:1], None,
                     ALU.mult)
                for c in range(EC):
                    p.trans(ptr_b[:, 0:16], ks_b[0:16, c * 128:(c + 1) * 128],
                            ident_b[0:16, 0:16])
                    p.copy(denT[0:64, c * 16 + 2 * c: c * 16 + 2 * c + 1],
                           ptr_b[0:64, 2 * c:2 * c + 1])
                    p.copy(denT[64:128, c * 16 + 2 * c + 1: c * 16 + 2 * c + 2],
                           ptr_b[64:128, 2 * c + 1:2 * c + 2])
            p.fence()
            p.set_serial(False)
            ps_s1 = psa("ps_s1P4", [1, T])
            ps_s2 = psa("ps_s2P4", [1, T])
            ps_b = psa("ps_bP4", [128, T])

            def conv_ln_store(ti_prev, a_cur, a_prev):
                t0p = ti_prev * T
                for c in range(EC):
                    w0 = c * (T + 4)
                    if ti_prev == 0:
                        p.memset(conv_win[:, w0:w0 + 2], 0.0, eng="vector")
                    else:
                        p.copy(conv_win[:, w0:w0 + 2], tail2[:, c * 2:(c + 1) * 2])
                    p.copy(conv_win[:, w0 + 2:w0 + 2 + T],
                           a_prev[:, c * T:(c + 1) * T])
                    if ti_prev == NT - 1:
                        p.memset(conv_win[:, w0 + 2 + T:w0 + 4 + T], 0.0,
                                 eng="vector")
                    else:
                        p.copy(conv_win[:, w0 + 2 + T:w0 + 4 + T],
                               a_cur[:, c * T:c * T + 2])
                    p.copy(tail2[:, c * 2:(c + 1) * 2],
                           a_prev[:, (c + 1) * T - 2:(c + 1) * T])
                for c in range(EC):
                    psc_ = ps_a if c % 2 == 0 else ps_p
                    for k in range(5):
                        p.mm(psc_[:, 0:T],
                             cvT[:, (c * 5 + k) * 128:(c * 5 + k) * 128 + 128],
                             conv_win[:, c * (T + 4) + k: c * (T + 4) + k + T],
                             start=(k == 0), stop=(k == 4))
                    p.act(scr[:, 0:T], psc_[:, 0:T], AF.Identity,
                          bias=gcb[:, c:c + 1], scale=gam_pp[:, 0:1])
                    p.tt(xn_f[:, c * T:(c + 1) * T], scr[:, 0:T],
                         a_prev[:, c * T:(c + 1) * T], ALU.add)
                p.copy(sq_bf[:], xn_f[:], eng="gpsimd")
                ln_stats(p, sq_bf, q_bf, ones1, ps_s1, ps_s2, T, mu2_, rs2_,
                         scr1_)
                p.mm(ps_b[:, 0:T], ones128, mu2_)
                p.act(mu_sb[:, 0:T], ps_b[:, 0:T], AF.Copy)
                p.mm(ps_b[:, 0:T], ones128, rs2_)
                p.act(rs_sb[:, 0:T], ps_b[:, 0:T], AF.Copy)
                apply_ln(p, attn2_bf, sq_bf, mu_sb, rs_sb, scr, T,
                         g_vec=vec[:, EC:2 * EC], b_vec=vec[:, 2 * EC:3 * EC])
                p.dma(fm_slice(attn2c, t0p, T), tile_ap(attn2_bf, EC, T),
                      eng="gpsimd")

            for ti in range(NT):
                t0 = ti * T
                a_cur = attn_a if ti % 2 == 0 else attn_b_
                a_prev = attn_b_ if ti % 2 == 0 else attn_a
                p.dma(tile_ap(q_bf, EC, T), fm_slice(Qc, t0, T))
                for mc in range(EC):
                    psx = ps_p if mc % 2 == 0 else ps_pB
                    for kc in range(EC):
                        p.mm(psx[:, 0:T],
                             wb16[:, kc * E + mc * 128: kc * E + mc * 128 + 128],
                             q_bf[:, kc * T:(kc + 1) * T],
                             start=(kc == 0), stop=(kc == EC - 1))
                    p.act(scr[:, 0:T], psx[:, 0:T], AF.Tanh, bias=vec[:, mc:mc + 1])
                    p.ts(pq_bf[:, mc * T:(mc + 1) * T], scr[:, 0:T], 1.0, None,
                         ALU.add, eng="gpsimd")
                for c in range(EC):
                    p.mm(ps_dr[0:16, 0:T], denT[:, c * 16:(c + 1) * 16],
                         pq_bf[:, c * T:(c + 1) * T],
                         start=(c == 0), stop=(c == EC - 1))
                p.ts(rden[0:16, 0:T], ps_dr[0:16, 0:T], DEN_EPS, None, ALU.add)
                p.recip(rden[0:16, 0:T], rden[0:16, 0:T])
                p.copy(rden_b[0:16, 0:T], rden[0:16, 0:T], eng="gpsimd")
                for c in range(EC):
                    psn = ps_a if c % 2 == 0 else ps_pB
                    p.mm(psn[:, 0:T], numT[:, c * 128:(c + 1) * 128],
                         pq_bf[:, c * T:(c + 1) * T])
                    p.mm(ps_dr[:, 0:T], sel_b[0:16, c * 128:(c + 1) * 128],
                         rden_b[0:16, 0:T])
                    p.act(rdb[:, 0:T], ps_dr[:, 0:T], AF.Copy)
                    p.tt(a_cur[:, c * T:(c + 1) * T], psn[:, 0:T], rdb[:, 0:T],
                         ALU.mult)
                if ti > 0:
                    conv_ln_store(ti - 1, a_cur, a_prev)
            last_buf = attn_b_ if (NT - 1) % 2 == 1 else attn_a
            conv_ln_store(NT - 1, last_buf, last_buf)
        p.fence()

        with ExitStack() as ph:
            stage = ph.enter_context(p.sbuf("stageb", [128, MLP]))
            stage2 = ph.enter_context(p.sbuf("stageb2", [128, MLP]))
            w1_bf = ph.enter_context(p.sbuf("w1_bf", [128, EC * MLP], BF16))
            ps_p = ph.enter_context(p.psum("ps_pb", [128, T], F32))
            ps_p2 = ph.enter_context(p.psum("ps_pb2", [128, T], F32))
            h1_bf2 = ph.enter_context(p.sbuf("h1_bf2", [128, T], BF16))
            h1_bf = ph.enter_context(p.sbuf("h1_bf", [128, T], BF16))
            a2_bf = ph.enter_context(p.sbuf("a2b_bf", [128, EC * T], BF16))
            b1_sb = ph.enter_context(p.sbuf("b1_sb", [128, MC], F32))
            p.dma(b1_sb[:], b1)
            load_weight_bf16_2q(p, w1_bf, W1, E, MLP, stage, stage2)
            for ti in range(NT):
                t0 = ti * T
                p.dma(tile_ap(a2_bf, EC, T), fm_slice(attn2c, t0, T))
                for mc in range(MC):
                    psx = ps_p if mc % 2 == 0 else ps_p2
                    hbx = h1_bf if mc % 2 == 0 else h1_bf2
                    for kc in range(EC):
                        p.mm(psx[:, 0:T],
                             w1_bf[:, kc * MLP + mc * 128: kc * MLP + mc * 128 + 128],
                             a2_bf[:, kc * T:(kc + 1) * T],
                             start=(kc == 0), stop=(kc == EC - 1))
                    p.act(hbx[:, 0:T], psx[:, 0:T], AF.Gelu,
                          bias=b1_sb[:, mc:mc + 1])
                    p.dma(fm_slice_one(h1c, t0, T, mc), row3(hbx.ap()[:, 0:T], T),
                          eng="gpsimd")
        p.fence()

        with ExitStack() as ph:
            stage = ph.enter_context(p.sbuf("stagec", [128, MLP]))
            stage2 = ph.enter_context(p.sbuf("stagec2", [128, MLP]))
            w2_bf = ph.enter_context(p.sbuf("w2_bf", [128, MC * E], BF16))
            ps_p = ph.enter_context(p.psum("ps_pc", [128, T], F32))
            ps_p2 = ph.enter_context(p.psum("ps_pc2", [128, T], F32))
            xn_in = ph.enter_context(p.sbuf("xn_inC", [128, EC * T], BF16))
            scrCB = (ph.enter_context(p.sbuf("scrC0", [128, T], F32)),
                     ph.enter_context(p.sbuf("scrC1", [128, T], F32)))
            h1t = ph.enter_context(p.sbuf("h1t", [128, MC * T], BF16))
            b2_sb = ph.enter_context(p.sbuf("b2_sb", [128, EC], F32))
            s_bf = ph.enter_context(p.sbuf("s_bfC", [128, EC * T], BF16))
            a2_bf = ph.enter_context(p.sbuf("a2c_bf", [128, EC * T], BF16))
            p.dma(b2_sb[:], b2)
            load_weight_bf16_2q(p, w2_bf, W2, MLP, E, stage, stage2)
            for ti in range(NT):
                t0 = ti * T
                p.dma(tile_ap(h1t, MC, T), fm_slice(h1c, t0, T, nchunk=MC))
                p.dma(tile_ap(a2_bf, EC, T), fm_slice(attn2c, t0, T))
                p.dma(tile_ap(xn_in, EC, T), fm_slice(xnc, t0, T))
                for mc in range(EC):
                    psx = ps_p if mc % 2 == 0 else ps_p2
                    scr = scrCB[mc % 2]
                    for kc in range(MC):
                        p.mm(psx[:, 0:T],
                             w2_bf[:, kc * E + mc * 128: kc * E + mc * 128 + 128],
                             h1t[:, kc * T:(kc + 1) * T],
                             start=(kc == 0), stop=(kc == MC - 1))
                    p.act(scr[:, 0:T], psx[:, 0:T], AF.Identity,
                          bias=b2_sb[:, mc:mc + 1])
                    p.tt(scr[:, 0:T], scr[:, 0:T], a2_bf[:, mc * T:(mc + 1) * T],
                         ALU.add, eng="gpsimd")
                    p.tt(scr[:, 0:T], scr[:, 0:T], xn_in[:, mc * T:(mc + 1) * T],
                         ALU.add)
                    p.copy(s_bf[:, mc * T:(mc + 1) * T], scr[:, 0:T],
                           eng="gpsimd")
                p.dma(fm_slice(s_c, t0, T), tile_ap(s_bf, EC, T), eng="gpsimd")
        m4c = p.mark()
        p.fence()

        p.set_serial(True, "B")
        with ExitStack() as ph:
            sbd = lambda n, s_, d=F32: ph.enter_context(p.sbuf(n, s_, d))
            psd = lambda n, s_, d=F32: ph.enter_context(p.psum(n, s_, d))
            wdT = sbd("wdT", [128, EC * IC * 128], BF16)
            swd = sbd("swd", [128, EC])
            h2t = sbd("h2t", [128, IC * T], BF16)
            rsd_pp = sbd("rsd_pp", [128, 1])
            sd_pp = sbd("sd_pp", [128, 1])
            sdr = sbd("sdr", [1, 4])
            gm2 = sbd("gm2", [1, 1])
            ps_p = psd("ps_pd", [128, T])
            ps_p2 = psd("ps_pd2", [128, T])
            ps_4 = psd("ps_4d", [128, 4])
            psq5 = psd("psqd", [128, 128], BF16)

            p.dep_on(mW2, m4c)
            emit_sh_read(p, sh2, nc_sb, sh_rd, sh_msk, sh_cnd, gm2[0:1, 0:1])
            make_scale_g(p, gm2[0:1, 0:1], clip_down, sdr, ones128, ps_4,
                         rsd_pp, sd_pp)
            with ExitStack() as phw:
                sbw = lambda n, s_, d=F32: phw.enter_context(p.sbuf(n, s_, d))
                stage = sbw("staged", [128, IF])
                yq = sbw("yqd", [128, IF])
                i1 = sbw("i1d", [128, IF], I32)
                i2 = sbw("i2d", [128, IF], I32)
                tfq = sbw("tfqd", [128, IF])
                wi_b5 = sbw("wi_bd", [128, IF], BF16)
                rm = sbw("rmd", [128, 1])
                for hc in range(EC):
                    p.dma(stage[:, 0:IF], w_down[hc * 128:(hc + 1) * 128, :])
                    p.red(rm, stage[:, 0:IF], ALU.max, absval=True)
                    p.ts(swd[:, hc:hc + 1], rm, 1.0 / 127.0, None, ALU.mult)
                    p.recip(rm, rm)
                    p.ts(rm, rm, 127.0, None, ALU.mult)
                    p.act(yq[:, 0:IF], stage[:, 0:IF], AF.Copy, bias=256.5,
                          scale=rm)
                    quant_round(p, yq, i1, i2, tfq, wi_b5, IF)
                    for ic in range(IC):
                        p.trans(psq5[:, 0:128], wi_b5[:, ic * 128:(ic + 1) * 128],
                                ident_b[:])
                        p.copy(wdT[:, (hc * IC + ic) * 128:(hc * IC + ic) * 128 + 128],
                               psq5[:, 0:128])
                    p.tt(swd[:, hc:hc + 1], swd[:, hc:hc + 1], sd_pp[:, 0:1],
                         ALU.mult)
            p.fence()
            DB = []
            for _i in range(2):
                DB.append(dict(
                    h2q=sbd(f"h2qD{_i}", [128, IC * T], BF16),
                    s_b=sbd(f"s_bD{_i}", [128, EC * T], BF16),
                    xt=sbd(f"xtD{_i}", [128, EC * T]),
                    yq=sbd(f"yqD{_i}", [128, T]),
                    i1=sbd(f"i1D{_i}", [128, T], I32),
                    i2=sbd(f"i2D{_i}", [128, T], I32),
                    tfq=sbd(f"tfD{_i}", [128, T]),
                    scr=sbd(f"scrD{_i}", [128, T]),
                    scr2=sbd(f"scr2D{_i}", [128, T]),
                ))
            p.set_serial(False)

            for ti in range(NT):
                t0 = ti * T
                db = DB[ti % 2]
                h2q, s_b, xt = db["h2q"], db["s_b"], db["xt"]
                p.dma(tile_ap(h2t, IC, T), fm_slice(h2c, t0, T, nchunk=IC))
                p.dma(tile_ap(s_b, EC, T), fm_slice(s_c, t0, T))
                p.dma(tile_ap(xt, EC, T), fm_slice(x, t0, T))
                for ic in range(IC):
                    qb = DB[ic % 2]
                    yq, i1, i2, tfq = qb["yq"], qb["i1"], qb["i2"], qb["tfq"]
                    p.act(yq[:, 0:T], h2t[:, ic * T:(ic + 1) * T], AF.Copy,
                          bias=256.5, scale=rsd_pp[:, 0:1])
                    p.copy(i1[:, 0:T], yq[:, 0:T])
                    p.stt(i2[:, 0:T], yq[:, 0:T], -0.5, i1[:, 0:T], ALU.add,
                          ALU.subtract)
                    p.stt(tfq[:, 0:T], i1[:, 0:T], -256.0, i2[:, 0:T], ALU.add,
                          ALU.add)
                    p.ts(h2q[:, ic * T:(ic + 1) * T], tfq[:, 0:T], 127.0, -128.0,
                         ALU.min, ALU.max, eng="gpsimd")
                for mc in range(EC):
                    mb = DB[mc % 2]
                    psx = ps_p if mc % 2 == 0 else ps_p2
                    scr, scr2 = mb["scr"], mb["scr2"]
                    for ic in range(IC):
                        p.mm(psx[:, 0:T],
                             wdT[:, (mc * IC + ic) * 128:(mc * IC + ic) * 128 + 128],
                             h2q[:, ic * T:(ic + 1) * T],
                             start=(ic == 0), stop=(ic == IC - 1))
                    p.act(scr[:, 0:T], psx[:, 0:T], AF.Silu, scale=swd[:, mc:mc + 1])
                    p.tt(scr[:, 0:T], scr[:, 0:T], s_b[:, mc * T:(mc + 1) * T],
                         ALU.mult)
                    p.tt(scr2[:, 0:T], scr[:, 0:T], xt[:, mc * T:(mc + 1) * T],
                         ALU.add, eng="gpsimd")
                    p.dma(fm_slice_one(o_out, t0, T, mc), row3(scr2.ap()[:, 0:T], T),
                          eng="gpsimd")

        p.emit(ctxs)
    return p


def _pp(v, nchunk):
    return np.ascontiguousarray(np.asarray(v, np.float32).reshape(nchunk, 128).T)


def _sel_np():
    s = np.zeros((EC * 16, 128), np.float32)
    for c in range(EC):
        s[c * 16 + 2 * c, 0:64] = 1.0
        s[c * 16 + 2 * c + 1, 64:128] = 1.0
    return s


_BUILT = {}


def _get(name, fn):
    if name not in _BUILT:
        _BUILT[name] = fn()
    return _BUILT[name]


MODE = 1
BUILD_STOP = None
EXP = set()


def _extras(b):
    pair = b // 2
    others = [q for q in range(NCORES) if q // 2 != pair]
    return others[0:3] if b % 2 == 0 else others[3:6]


def kernel(**inp):
    if MODE == 1:
        return kernel_fused(**inp)
    return kernel_3launch(**inp)


def kernel_fused(**inp):
    g = lambda k: np.asarray(inp[k])
    x = np.asarray(inp["L0_attn_x"], np.float32)
    xfm = [np.ascontiguousarray(x[b].T) for b in range(NCORES)]

    vecs1 = np.concatenate([_pp(g("L0_attn_bq"), EC), _pp(g("L0_attn_lnq_g"), EC),
                            _pp(g("L0_attn_lnq_b"), EC), _pp(g("L0_attn_bv"), EC),
                            _pp(g("L0_attn_lnv_g"), EC), _pp(g("L0_attn_lnv_b"), EC)],
                           axis=1)
    veck = np.concatenate([_pp(g("L0_attn_bk"), EC), _pp(g("L0_attn_lnk_g"), EC),
                           _pp(g("L0_attn_lnk_b"), EC), _pp(g("L0_attn_bphi_k"), EC)],
                          axis=1)
    veca = np.concatenate([_pp(g("L0_attn_bphi_q"), EC), _pp(g("L0_attn_lna_g"), EC),
                           _pp(g("L0_attn_lna_b"), EC), _pp(g("L0_attn_conv_b"), EC)],
                          axis=1)
    cwT = np.ascontiguousarray(g("L0_attn_conv_w").astype(np.float32)
                               .transpose(2, 1, 0).reshape(5 * 64, E))
    n1, n2 = (np.random.uniform(257.0, 1021.0, size=2)
              .astype(np.float32))
    nonce4 = np.ascontiguousarray(np.tile(np.array([[n1, n2]], np.float32),
                                          (4, 1)))
    nonce_r = np.ascontiguousarray(
        np.concatenate([np.full(8, n1, np.float32),
                        np.full(8, n2, np.float32)]).reshape(1, 16))

    pf = _get("F", build_fused)
    common = {
        "Wq": np.ascontiguousarray(g("L0_attn_Wq"), dtype=np.float32),
        "Wv": np.ascontiguousarray(g("L0_attn_Wv"), dtype=np.float32),
        "vecs": vecs1,
        "Wk": np.ascontiguousarray(g("L0_attn_Wk"), dtype=np.float32),
        "Wpk": np.ascontiguousarray(g("L0_attn_Wphi_k"), dtype=np.float32),
        "veck": veck,
        "bpk_row": np.asarray(inp["L0_attn_bphi_k"], np.float32).reshape(1, E),
        "w_gate": np.ascontiguousarray(g("L0_ffn_w_gate"), dtype=np.float32),
        "w_up": np.ascontiguousarray(g("L0_ffn_w_up"), dtype=np.float32),
        "clip_x": np.asarray(inp["L0_ffn_clip_x"], np.float32).reshape(1, 1),
        "Wpq": np.ascontiguousarray(g("L0_attn_Wphi_q"), dtype=np.float32),
        "veca": veca, "cwT": cwT,
        "gamma": np.asarray(inp["L0_attn_gamma"], np.float32).reshape(1, 1),
        "W1": np.ascontiguousarray(g("L0_attn_W1"), dtype=np.float32),
        "b1": _pp(g("L0_attn_b1"), MC),
        "W2": np.ascontiguousarray(g("L0_attn_W2"), dtype=np.float32),
        "b2": _pp(g("L0_attn_b2"), EC),
        "w_down": np.ascontiguousarray(g("L0_ffn_w_down"), dtype=np.float32),
        "clip_down": np.asarray(inp["L0_ffn_clip_down"], np.float32).reshape(1, 1),
        "sel": _sel_np(), "nonce4": nonce4, "nonce_r": nonce_r,
    }
    in_maps = []
    for b in range(NCORES):
        m = dict(common)
        m["x"] = xfm[b]
        m["xe"] = np.ascontiguousarray(
            np.concatenate([xfm[e] for e in _extras(b)], axis=0))
        in_maps.append(m)
    res = run_bass_kernel_spmd(pf.nc, in_maps, core_ids=list(range(NCORES))).results
    out = np.stack([np.asarray(res[b]["out"]).T for b in range(NCORES)])
    return np.ascontiguousarray(out, dtype=np.float32)


def kernel_3launch(**inp):
    g = lambda k: np.asarray(inp[k])
    x = np.asarray(inp["L0_attn_x"], np.float32)
    xfm = [np.ascontiguousarray(x[b].T) for b in range(NCORES)]

    vecs1 = np.concatenate([_pp(g("L0_attn_bq"), EC), _pp(g("L0_attn_lnq_g"), EC),
                            _pp(g("L0_attn_lnq_b"), EC), _pp(g("L0_attn_bv"), EC),
                            _pp(g("L0_attn_lnv_g"), EC), _pp(g("L0_attn_lnv_b"), EC)],
                           axis=1)
    veck = np.concatenate([_pp(g("L0_attn_bk"), EC), _pp(g("L0_attn_lnk_g"), EC),
                           _pp(g("L0_attn_lnk_b"), EC), _pp(g("L0_attn_bphi_k"), EC)],
                          axis=1)
    veca = np.concatenate([_pp(g("L0_attn_bphi_q"), EC), _pp(g("L0_attn_lna_g"), EC),
                           _pp(g("L0_attn_lna_b"), EC), _pp(g("L0_attn_conv_b"), EC)],
                          axis=1)
    cwT = np.ascontiguousarray(g("L0_attn_conv_w").astype(np.float32)
                               .transpose(2, 1, 0).reshape(5 * 64, E))

    p1 = _get("L1", build_L1)
    in1 = [{"x": xfm[b], "Wq": np.ascontiguousarray(g("L0_attn_Wq"), dtype=np.float32),
            "Wv": np.ascontiguousarray(g("L0_attn_Wv"), dtype=np.float32),
            "vecs": vecs1} for b in range(NCORES)]
    r1 = run_bass_kernel_spmd(p1.nc, in1, core_ids=list(range(NCORES))).results

    mxs = np.ascontiguousarray(
        np.stack([np.asarray(r1[b]["mx"]) for b in range(NCORES)]).reshape(1, 128))

    p2 = _get("L2", build_L2)
    in2 = [{"x": xfm[b], "stats": np.asarray(r1[b]["stats"]),
            "qg": np.asarray(r1[b]["qg"]), "mxs": mxs,
            "clip_x": np.asarray(inp["L0_ffn_clip_x"], np.float32).reshape(1, 1),
            "Wk": np.ascontiguousarray(g("L0_attn_Wk"), dtype=np.float32),
            "Wpk": np.ascontiguousarray(g("L0_attn_Wphi_k"), dtype=np.float32),
            "veck": veck,
            "bpk_row": np.asarray(inp["L0_attn_bphi_k"], np.float32).reshape(1, E),
            "w_gate": np.ascontiguousarray(g("L0_ffn_w_gate"), dtype=np.float32),
            "w_up": np.ascontiguousarray(g("L0_ffn_w_up"), dtype=np.float32),
            "Vc": np.asarray(r1[b]["Vc"])} for b in range(NCORES)]
    r2 = run_bass_kernel_spmd(p2.nc, in2, core_ids=list(range(NCORES))).results

    mxh = np.ascontiguousarray(
        np.stack([np.asarray(r2[b]["mxh"]) for b in range(NCORES)]).reshape(1, 128))

    p3 = _get("L3", build_L3)
    in3 = [{"x": xfm[b], "stats": np.asarray(r1[b]["stats"]),
            "Qc": np.asarray(r1[b]["Qc"]), "kv": np.asarray(r2[b]["kv"]),
            "ks": np.asarray(r2[b]["ks"]),
            "Wpq": np.ascontiguousarray(g("L0_attn_Wphi_q"), dtype=np.float32),
            "veca": veca, "cwT": cwT,
            "gamma": np.asarray(inp["L0_attn_gamma"], np.float32).reshape(1, 1),
            "W1": np.ascontiguousarray(g("L0_attn_W1"), dtype=np.float32),
            "b1": _pp(g("L0_attn_b1"), MC),
            "W2": np.ascontiguousarray(g("L0_attn_W2"), dtype=np.float32),
            "b2": _pp(g("L0_attn_b2"), EC),
            "w_down": np.ascontiguousarray(g("L0_ffn_w_down"), dtype=np.float32),
            "clip_down": np.asarray(inp["L0_ffn_clip_down"], np.float32).reshape(1, 1),
            "mxh_all": mxh, "h2c": np.asarray(r2[b]["h2c"]), "sel": _sel_np()}
           for b in range(NCORES)]
    r3 = run_bass_kernel_spmd(p3.nc, in3, core_ids=list(range(NCORES))).results

    out = np.stack([np.asarray(r3[b]["out"]).T for b in range(NCORES)])
    return np.ascontiguousarray(out, dtype=np.float32)


# version 23 (final)
# speedup vs baseline: 1.1874x; 1.1874x over previous
import numpy as np
import ml_dtypes
from contextlib import ExitStack
import concourse.bass as bass
import concourse.mybir as mybir
from concourse.bass_utils import run_bass_kernel_spmd

F32 = mybir.dt.float32
BF16 = mybir.dt.bfloat16
I32 = mybir.dt.int32
AF = mybir.ActivationFunctionType
ALU = mybir.AluOpType
AX = mybir.AxisListType

NCORES = 8
L = 4096
E = 1024
EC = 8
H = 16
D = 64
MLP = 4096
MC = 32
IF = 2816
IC = 22
T = 512
NT = L // T
TC = T // 128
LN_EPS = 1e-5
DEN_EPS = 1e-6


class P:

    COMPUTE = ("tensor", "vector", "scalar", "gpsimd")
    CHANNELS = ("tensor", "vector", "scalar", "gpsimd", "dsync", "dgp")

    def __init__(self, serial=False):
        self.nc = bass.Bass("TRN2", target_bir_lowering=False)
        self.steps = []
        self.outs = []
        self.serial = serial
        self._pending = []
        self._barrier = {}
        self._ser_now = "g" if serial else None
        self._cur_fence = {}
        self._alloc_fence = {}

    def set_serial(self, v, tag="g"):
        self._ser_now = tag if v else None

    def fence(self):
        snap = {}
        for i, st in enumerate(self.steps):
            snap[st["ch"]] = i
        self._cur_fence = snap

    def barrier(self):
        snap = {}
        for i, st in enumerate(self.steps):
            snap[st["ch"]] = i
        self._barrier = dict(snap)

    def nm(self, n):
        if not hasattr(self, "_names"):
            self._names = {}
        i = self._names.get(n, 0)
        self._names[n] = i + 1
        return n if i == 0 else f"{n}_{i}"

    def sbuf(self, n, s, d=F32):
        nm = self.nm(n)
        self._alloc_fence[nm] = self._cur_fence
        return self.nc.sbuf_tensor(nm, s, d)

    def psum(self, n, s, d=F32):
        esz = 2 if d == BF16 else 4
        per_bank = 2048 // esz
        cols = ((s[1] + per_bank - 1) // per_bank) * per_bank
        nm = self.nm(n)
        self._alloc_fence[nm] = self._cur_fence
        return self.nc.psum_tensor(nm, [s[0], cols], d)

    def din(self, name, shape, dtype=F32):
        return self.nc.dram_tensor(name, list(shape), dtype, kind="ExternalInput").ap()

    def dout(self, name, shape, dtype=F32):
        self.outs.append(name)
        return self.nc.dram_tensor(name, list(shape), dtype, kind="ExternalOutput").ap()

    def dtmp(self, name, shape, dtype=F32):
        return self.nc.dram_tensor(name, list(shape), dtype, kind="Internal").ap()

    @staticmethod
    def A(x):
        if hasattr(x, "ap") and callable(getattr(x, "ap")):
            try:
                return x.ap()
            except TypeError:
                return x
        return x

    @staticmethod
    def _tn(x):
        x = P.A(x)
        t = getattr(x, "tensor", None)
        return t.name if t is not None else None

    def _add(self, ch, inc, fn, rd=(), wr=()):
        rdn = {n for n in (P._tn(a) for a in rd) if n is not None}
        wrn = {n for n in (P._tn(a) for a in wr) if n is not None}
        extra = self._pending
        self._pending = []
        self.steps.append(dict(ch=ch, inc=inc, fn=fn, rd=rdn, wr=wrn,
                               extra=extra, bar=self._barrier,
                               ser=self._ser_now))
        return len(self.steps) - 1

    def mark(self):
        return len(self.steps) - 1

    def dep_on(self, *idxs):
        self._pending = [i for i in idxs if i is not None and i >= 0]

    def dma(self, out, in_, eng="sync", **kw):
        ch = "dsync" if eng == "sync" else "dgp"
        self._add(ch, 16,
                  lambda e, a=P.A(out), b=P.A(in_), kw=kw: e.dma_start(out=a, in_=b, **kw),
                  rd=(in_,), wr=(out,))

    def mm(self, out, lhsT, rhs, start=True, stop=True):
        self._add("tensor", 1,
                  lambda e, a=P.A(out), b=P.A(lhsT), c=P.A(rhs), s0=start, s1=stop:
                  e.matmul(a, b, c, start=s0, stop=s1),
                  rd=(lhsT, rhs, out), wr=(out,))

    def trans(self, out, in_, ident):
        self._add("tensor", 1,
                  lambda e, a=P.A(out), b=P.A(in_), c=P.A(ident): e.transpose(a, b, c),
                  rd=(in_, ident), wr=(out,))

    def act(self, out, in_, func, bias=0.0, scale=1.0):
        self._add("scalar", 1,
                  lambda e, a=P.A(out), b=P.A(in_), f=func, bi=P.A(bias), sc=P.A(scale):
                  e.activation(a, b, f, bias=bi, scale=sc),
                  rd=(in_, bias, scale), wr=(out,))

    def ts(self, out, in0, s1, s2, op0, op1=None, eng="vector"):
        if op1 is None:
            self._add(eng, 1,
                      lambda e, a=P.A(out), b=P.A(in0), x=P.A(s1), o0=op0:
                      e.tensor_scalar(a, b, x, None, o0),
                      rd=(in0, s1), wr=(out,))
        else:
            self._add(eng, 1,
                      lambda e, a=P.A(out), b=P.A(in0), x=P.A(s1), y=P.A(s2), o0=op0, o1=op1:
                      e.tensor_scalar(a, b, x, y, o0, o1),
                      rd=(in0, s1, s2), wr=(out,))

    def tt(self, out, in0, in1, op, eng="vector"):
        self._add(eng, 1,
                  lambda e, a=P.A(out), b=P.A(in0), c=P.A(in1), o=op:
                  e.tensor_tensor(a, b, c, o),
                  rd=(in0, in1), wr=(out,))

    def stt(self, out, in0, scalar, in1, op0, op1, eng="vector"):
        self._add(eng, 1,
                  lambda e, a=P.A(out), b=P.A(in0), s=P.A(scalar), c=P.A(in1), o0=op0, o1=op1:
                  e.scalar_tensor_tensor(a, b, s, c, o0, o1),
                  rd=(in0, scalar, in1), wr=(out,))

    def red(self, out, in_, op, absval=False, eng="vector"):
        self._add(eng, 1,
                  lambda e, a=P.A(out), b=P.A(in_), o=op, v=absval:
                  e.tensor_reduce(a, b, AX.X, o, apply_absolute_value=v),
                  rd=(in_,), wr=(out,))

    def copy(self, out, in_, eng="vector"):
        self._add(eng, 1,
                  lambda e, a=P.A(out), b=P.A(in_): e.tensor_copy(a, b),
                  rd=(in_,), wr=(out,))

    def recip(self, out, in_):
        self._add("vector", 1,
                  lambda e, a=P.A(out), b=P.A(in_): e.reciprocal(a, b),
                  rd=(in_,), wr=(out,))

    def memset(self, ap, v, eng="gpsimd"):
        self._add(eng, 1, lambda e, a=P.A(ap), x=v: e.memset(a, x), wr=(ap,))

    def identity_fill(self, ap, n):
        self.memset(ap, 0.0)
        self._add("gpsimd", 1, lambda e, a=P.A(ap), m=n: e.affine_select(
            out=a, in_=a, compare_op=ALU.not_equal, fill=1.0,
            base=0, pattern=[[-1, m]], channel_multiplier=1),
            rd=(ap,), wr=(ap,))

    def sw_pair_write(self, sh, w4_ap):
        w4 = P.A(w4_ap)

        def fn(e, sem, inc):
            for arm in e.switch_core_id(n=128):
                role = int(arm) % 2
                e.dma_start(out=sh[role * 4:(role + 1) * 4, :],
                            in_=w4[0:4, :]).then_inc(sem, inc)
        fn._takes_sem = True
        self._add("dgp", 16, fn, rd=(w4_ap,), wr=(sh,))

    def emit(self, ctxs):
        nc = self.nc
        steps = self.steps
        n = len(steps)
        last_w = {}
        readers = {}
        last_ser = {}
        afence = self._alloc_fence
        deps = [set() for _ in range(n)]
        for i, st in enumerate(steps):
            d = set(st["extra"])
            d.update(st["bar"].values())
            for t in st["rd"]:
                j = last_w.get(t)
                if j is not None:
                    d.add(j)
                f = afence.get(t)
                if f:
                    d.update(f.values())
            for t in st["wr"]:
                j = last_w.get(t)
                if j is not None:
                    d.add(j)
                for j in readers.get(t, ()):
                    d.add(j)
                f = afence.get(t)
                if f:
                    d.update(f.values())
            tag = st["ser"]
            if tag:
                j = last_ser.get(tag)
                if j is not None:
                    d.add(j)
                last_ser[tag] = i
            d.discard(i)
            deps[i] = d
            for t in st["wr"]:
                last_w[t] = i
                readers[t] = []
            for t in st["rd"]:
                readers.setdefault(t, []).append(i)
        for i, st in enumerate(steps):
            if st["ch"] == "tensor":
                deps[i] = {j for j in deps[i] if steps[j]["ch"] != "tensor"}
        needed = [st["ch"] in ("dsync", "dgp") for st in steps]
        for i in range(n):
            for j in deps[i]:
                needed[j] = True
        last_of = {}
        for i, st in enumerate(steps):
            last_of[st["ch"]] = i
        for ch, i in last_of.items():
            needed[i] = True
        inc_upto = {ch: 0 for ch in P.CHANNELS}
        count_at = [0] * n
        for i, st in enumerate(steps):
            if needed[i]:
                inc_upto[st["ch"]] += st["inc"]
            count_at[i] = inc_upto[st["ch"]]
        totals = dict(inc_upto)
        self.deps = deps
        self.count_at = count_at
        self.needed = needed

        sems = {ch: ctxs.enter_context(nc.semaphore(f"sem_{ch}"))
                for ch in P.CHANNELS}
        block = ctxs.enter_context(nc.Block())

        eng_of_ch = {"tensor": "tensor", "vector": "vector", "scalar": "scalar",
                     "gpsimd": "gpsimd", "dsync": "sync", "dgp": "gpsimd"}

        def body_for(eng_name):
            my_chs = [ch for ch, e in eng_of_ch.items() if e == eng_name]

            def body(eng):
                wm = {ch: 0 for ch in P.CHANNELS}
                for i, st in enumerate(steps):
                    if eng_of_ch[st["ch"]] != eng_name:
                        continue
                    req = {}
                    for j in deps[i]:
                        cj = steps[j]["ch"]
                        req[cj] = max(req.get(cj, 0), count_at[j])
                    for cj, v in sorted(req.items()):
                        if v > wm[cj]:
                            eng.wait_ge(sems[cj], v)
                            wm[cj] = v
                    if getattr(st["fn"], "_takes_sem", False):
                        st["fn"](eng, sems[st["ch"]], st["inc"])
                    else:
                        inst = st["fn"](eng)
                        if needed[i]:
                            inst.then_inc(sems[st["ch"]], st["inc"])
                if eng_name == "sync":
                    for ch in P.CHANNELS:
                        if totals[ch] > 0:
                            eng.wait_ge(sems[ch], totals[ch])
            return body

        block.sync(body_for("sync"))
        block.vector(body_for("vector"))
        block.scalar(body_for("scalar"))
        block.tensor(body_for("tensor"))
        block.gpsimd(body_for("gpsimd"))


def fm_slice(x_hbm, t0, tn, nchunk=EC):
    r = x_hbm.rearrange("(c p) l -> p c l", p=128)
    return r[:, 0:nchunk, t0:t0 + tn]


def fm_slice_one(x_hbm, t0, tn, c):
    r = x_hbm.rearrange("(c p) l -> p c l", p=128)
    return r[:, c:c + 1, t0:t0 + tn]


def tile_ap(sb, nchunk, tn):
    return sb.ap().rearrange("p (c t) -> p c t", t=tn)[:, 0:nchunk, :]


def row3(sb_ap, tn):
    return sb_ap.rearrange("p (o t) -> p o t", o=1)


def load_weight_bf16(p, sb_bf, w_hbm, kdim, mdim, stage):
    for kc in range(kdim // 128):
        p.dma(stage[:, 0:mdim], w_hbm[kc * 128:(kc + 1) * 128, :])
        p.copy(sb_bf[:, kc * mdim:(kc + 1) * mdim], stage[:, 0:mdim])


def ln_stats(p, src_bf, sq_bf, ones1, ps1, ps2, tn, mu, rstd, scr, nchunk=EC):
    inv_n = 1.0 / (nchunk * 128)
    p.act(sq_bf[:, 0:nchunk * tn], src_bf[:, 0:nchunk * tn], AF.Square)
    for c in range(nchunk):
        p.mm(ps1[0:1, 0:tn], ones1, src_bf[:, c * tn:(c + 1) * tn],
             start=(c == 0), stop=(c == nchunk - 1))
    for c in range(nchunk):
        p.mm(ps2[0:1, 0:tn], ones1, sq_bf[:, c * tn:(c + 1) * tn],
             start=(c == 0), stop=(c == nchunk - 1))
    p.ts(mu, ps1[0:1, 0:tn], inv_n, None, ALU.mult)
    p.tt(scr, mu, mu, ALU.mult)
    p.stt(rstd, ps2[0:1, 0:tn], inv_n, scr, ALU.mult, ALU.subtract)
    p.ts(rstd, rstd, LN_EPS, None, ALU.add)
    p.act(rstd, rstd, AF.Sqrt)
    p.recip(rstd, rstd)


def ln_stats3(p, src_bf, sq_full, ones1, ps1, ps2, tn, mu, rstd, scr_row,
              nchunk=EC):
    inv_n = 1.0 / (nchunk * 128)
    p.act(sq_full[:, 0:nchunk * tn], src_bf[:, 0:nchunk * tn], AF.Square)
    for c in range(nchunk):
        seg = slice(c * tn, (c + 1) * tn)
        p.mm(ps1[0:1, 0:tn], ones1, src_bf[:, seg], start=(c == 0),
             stop=(c == nchunk - 1))
        p.mm(ps2[0:1, 0:tn], ones1, sq_full[:, seg], start=(c == 0),
             stop=(c == nchunk - 1))
    p.ts(mu, ps1[0:1, 0:tn], inv_n, None, ALU.mult)
    p.tt(scr_row, mu, mu, ALU.mult)
    p.stt(rstd, ps2[0:1, 0:tn], inv_n, scr_row, ALU.mult, ALU.subtract)
    p.ts(rstd, rstd, LN_EPS, None, ALU.add)
    p.act(rstd, rstd, AF.Sqrt)
    p.recip(rstd, rstd)


def apply_ln(p, dst_bf, src_bf, ps_mu, ps_rs, scr_f, tn, g_vec, b_vec,
             nchunk=EC, eng="vector"):
    for c in range(nchunk):
        seg = slice(c * tn, (c + 1) * tn)
        p.tt(scr_f[:, 0:tn], src_bf[:, seg], ps_mu[:, 0:tn], ALU.subtract, eng=eng)
        p.tt(scr_f[:, 0:tn], scr_f[:, 0:tn], ps_rs[:, 0:tn], ALU.mult, eng=eng)
        p.ts(dst_bf[:, seg], scr_f[:, 0:tn], g_vec[:, c:c + 1],
             b_vec[:, c:c + 1], ALU.mult, ALU.add, eng=eng)


def quant_round(p, y_f32, i1, i2, tf, out_bf, n):
    p.copy(i1[:, 0:n], y_f32[:, 0:n])
    p.stt(i2[:, 0:n], y_f32[:, 0:n], -0.5, i1[:, 0:n], ALU.add, ALU.subtract)
    p.stt(tf[:, 0:n], i1[:, 0:n], -256.0, i2[:, 0:n], ALU.add, ALU.add)
    p.ts(out_bf[:, 0:n], tf[:, 0:n], 127.0, -128.0, ALU.min, ALU.max)


def fm_slice_off(x_hbm, t0, tn, c0, nchunk=EC):
    r = x_hbm.rearrange("(c p) l -> p c l", p=128)
    return r[:, c0:c0 + nchunk, t0:t0 + tn]


def load_weight_bf16_2q(p, sb_bf, w_hbm, kdim, mdim, stg0, stg1):
    for kc in range(kdim // 128):
        stg = stg0 if kc % 2 == 0 else stg1
        p.dma(stg[:, 0:mdim], w_hbm[kc * 128:(kc + 1) * 128, :],
              eng=("sync" if kc % 2 == 0 else "gpsimd"))
        p.copy(sb_bf[:, kc * mdim:(kc + 1) * mdim], stg[:, 0:mdim])


def ln_stats2(p, src_bf, sqs, ones1, ps1, ps2, tn, mu, rstd, scr_row, nchunk=EC):
    inv_n = 1.0 / (nchunk * 128)
    for c in range(nchunk):
        sq = sqs[c % 2]
        seg = slice(c * tn, (c + 1) * tn)
        p.act(sq[:, 0:tn], src_bf[:, seg], AF.Square)
        p.mm(ps1[0:1, 0:tn], ones1, src_bf[:, seg], start=(c == 0),
             stop=(c == nchunk - 1))
        p.mm(ps2[0:1, 0:tn], ones1, sq[:, 0:tn], start=(c == 0),
             stop=(c == nchunk - 1))
    p.ts(mu, ps1[0:1, 0:tn], inv_n, None, ALU.mult)
    p.tt(scr_row, mu, mu, ALU.mult)
    p.stt(rstd, ps2[0:1, 0:tn], inv_n, scr_row, ALU.mult, ALU.subtract)
    p.ts(rstd, rstd, LN_EPS, None, ALU.add)
    p.act(rstd, rstd, AF.Sqrt)
    p.recip(rstd, rstd)


def load_stats(p, stats_hbm, srow, ones128, ps_mu, ps_rs, t0, tn):
    p.dma(srow[0:1, 0:tn], stats_hbm[0:1, t0:t0 + tn])
    p.dma(srow[0:1, tn:2 * tn], stats_hbm[1:2, t0:t0 + tn])
    p.mm(ps_mu[:, 0:tn], ones128, srow[0:1, 0:tn])
    p.mm(ps_rs[:, 0:tn], ones128, srow[0:1, tn:2 * tn])


def bcast_sb(p, ps_mu, ps_rs, mu_sb, rs_sb, tn):
    p.act(mu_sb[:, 0:tn], ps_mu[:, 0:tn], AF.Copy)
    p.act(rs_sb[:, 0:tn], ps_rs[:, 0:tn], AF.Copy)


def xn_chunk(p, dst_f, xt, ps_mu, ps_rs, c, tn, eng="vector"):
    seg = slice(c * tn, (c + 1) * tn)
    p.tt(dst_f[:, 0:tn], xt[:, seg], ps_mu[:, 0:tn], ALU.subtract, eng=eng)
    p.tt(dst_f[:, 0:tn], dst_f[:, 0:tn], ps_rs[:, 0:tn], ALU.mult, eng=eng)


def cross_part_max(p, vmax, bounce_dram, row_sb, out_sb):
    p.dma(bounce_dram, vmax[:])
    p.dma(row_sb[0:1, 0:128], bounce_dram.rearrange("p c -> c p"))
    p.red(out_sb, row_sb[0:1, 0:128], ALU.max)


def make_scale(p, mx_row_sb, clip_hbm, sdr, ones128, ps4, rs_pp, s_pp):
    p.red(sdr[0:1, 0:1], mx_row_sb[0:1, 0:128], ALU.max)
    p.dma(sdr[0:1, 1:2], clip_hbm)
    p.act(sdr[0:1, 1:2], sdr[0:1, 1:2], AF.Sigmoid)
    p.tt(sdr[0:1, 0:1], sdr[0:1, 0:1], sdr[0:1, 1:2], ALU.mult)
    p.ts(sdr[0:1, 0:1], sdr[0:1, 0:1], 1.0 / 127.0, None, ALU.mult)
    p.recip(sdr[0:1, 2:3], sdr[0:1, 0:1])
    p.mm(ps4[:, 0:4], ones128, sdr[0:1, 0:4])
    p.copy(rs_pp[:, 0:1], ps4[:, 2:3])
    p.copy(s_pp[:, 0:1], ps4[:, 0:1])


def emit_sh_payload(p, bnc4, nonce4_in, w4, sh):
    p.memset(w4[:], 0.0, eng="vector")
    p.dma(w4[0:4, 0:1], bnc4[0:4, 0:1], allow_slow_non_contiguous=True)
    p.dma(w4[0:4, 3:5], nonce4_in)
    p.stt(w4[0:4, 1:2], w4[0:4, 0:1], 3.0, w4[0:4, 3:4], ALU.mult, ALU.add)
    p.stt(w4[0:4, 2:3], w4[0:4, 0:1], -5.0, w4[0:4, 4:5], ALU.mult, ALU.add)
    p.sw_pair_write(sh, w4)


def emit_sh_read(p, sh, nonce_sb, rd, msk, cnd, gm):
    p.dma(rd[0:1, 0:128],
          sh.rearrange("p c -> (p c)").rearrange("(o f) -> o f", o=1))
    r3 = rd.ap()[0:1, 0:128].rearrange("o (s c) -> o s c", c=16)
    vals, chk1, chk2 = r3[:, :, 0], r3[:, :, 1], r3[:, :, 2]
    p.ts(cnd[0:1, 8:16], vals, 3.0, None, ALU.mult)
    p.tt(cnd[0:1, 8:16], cnd[0:1, 8:16], nonce_sb[0:1, 0:8], ALU.add)
    p.tt(msk[0:1, 0:8], cnd.ap()[0:1, 8:16].bitcast(I32), chk1.bitcast(I32),
         ALU.is_equal)
    p.ts(cnd[0:1, 8:16], vals, -5.0, None, ALU.mult)
    p.tt(cnd[0:1, 8:16], cnd[0:1, 8:16], nonce_sb[0:1, 8:16], ALU.add)
    p.tt(msk[0:1, 8:16], cnd.ap()[0:1, 8:16].bitcast(I32), chk2.bitcast(I32),
         ALU.is_equal)
    p.tt(msk[0:1, 0:8], msk[0:1, 0:8], msk[0:1, 8:16], ALU.bitwise_and)
    p.copy(cnd[0:1, 0:8], msk[0:1, 0:8])
    p.tt(cnd[0:1, 0:8], cnd[0:1, 0:8], vals, ALU.mult)
    p.tt(cnd[0:1, 0:4], cnd[0:1, 0:4], cnd[0:1, 4:8], ALU.max)
    p.tt(cnd[0:1, 0:2], cnd[0:1, 0:2], cnd[0:1, 2:4], ALU.max)
    p.tt(gm, cnd[0:1, 0:1], cnd[0:1, 1:2], ALU.max)


def make_scale_g(p, gm, clip_hbm, sdr, ones128, ps4, rs_pp, s_pp):
    p.copy(sdr[0:1, 0:1], gm)
    p.dma(sdr[0:1, 1:2], clip_hbm)
    p.act(sdr[0:1, 1:2], sdr[0:1, 1:2], AF.Sigmoid)
    p.tt(sdr[0:1, 0:1], sdr[0:1, 0:1], sdr[0:1, 1:2], ALU.mult)
    p.ts(sdr[0:1, 0:1], sdr[0:1, 0:1], 1.0 / 127.0, None, ALU.mult)
    p.recip(sdr[0:1, 2:3], sdr[0:1, 0:1])
    p.mm(ps4[:, 0:4], ones128, sdr[0:1, 0:4])
    p.copy(rs_pp[:, 0:1], ps4[:, 2:3])
    p.copy(s_pp[:, 0:1], ps4[:, 0:1])


def build_L1():
    p = P()
    nc = p.nc
    x = p.din("x", [E, L])
    Wq = p.din("Wq", [E, E])
    Wv = p.din("Wv", [E, E])
    vecs = p.din("vecs", [128, 6 * EC])

    o_stats = p.dout("stats", [2, L])
    o_Q = p.dout("Qc", [E, L], BF16)
    o_V = p.dout("Vc", [E, L], BF16)
    o_qg = p.dout("qg", [128, EC])
    o_mx = p.dout("mx", [1, 16])
    bnc = p.dtmp("bnc", [128, 1])

    ctxs = ExitStack()
    with ctxs:
        sb = lambda n, s, d=F32: ctxs.enter_context(p.sbuf(n, s, d))
        ps = lambda n, s, d=F32: ctxs.enter_context(p.psum(n, s, d))

        stage = sb("stage", [128, E])
        wq_bf = sb("wq_bf", [128, EC * E], BF16)
        wv_bf = sb("wv_bf", [128, EC * E], BF16)
        vec = sb("vec", [128, 6 * EC])
        ones1 = sb("ones1", [128, 1], BF16)
        ones128 = sb("ones128", [1, 128], F32)
        xt = sb("xt", [128, EC * T])
        xn_f = sb("xn_f", [128, T])
        xn_bf = sb("xn_bf", [128, EC * T], BF16)
        sq_bf = sb("sq_bf", [128, EC * T], BF16)
        q_bf = sb("q_bf", [128, EC * T], BF16)
        out_bf = sb("out_bf", [128, EC * T], BF16)
        scr = sb("scr", [128, T])
        strow = sb("strow", [1, 5 * T])
        qsum = sb("qsum", [128, EC])
        vmax = sb("vmax", [128, 1])
        rtmp = sb("rtmp", [128, 1])
        mrow = sb("mrow", [1, 16])
        rrow = sb("rrow", [1, 128])
        ps_s1 = ps("ps_s1", [1, T])
        ps_s2 = ps("ps_s2", [1, T])
        ps_mu = ps("ps_mu", [128, T])
        ps_rs = ps("ps_rs", [128, T])
        ps_p = ps("ps_p", [128, T])

        p.memset(ones1[:], 1.0)
        p.memset(ones1f[:], 1.0)
        p.memset(ones128[:], 1.0)
        p.memset(qsum[:], 0.0)
        p.memset(vmax[:], 0.0)
        p.memset(mrow[:], 0.0)
        p.dma(vec[:], vecs)
        load_weight_bf16(p, wq_bf, Wq, E, E, stage)
        load_weight_bf16(p, wv_bf, Wv, E, E, stage)

        mu = strow[0:1, 0:T]
        rs = strow[0:1, T:2 * T]
        scr1 = strow[0:1, 2 * T:3 * T]
        mu2 = strow[0:1, 3 * T:4 * T]
        rs2 = strow[0:1, 4 * T:5 * T]

        for ti in range(NT):
            t0 = ti * T
            p.dma(tile_ap(xt, EC, T), fm_slice(x, t0, T))
            p.copy(xn_bf[:], xt[:])
            ln_stats(p, xn_bf, sq_bf, ones1, ps_s1, ps_s2, T, mu, rs, scr1)
            p.dma(o_stats[0:1, t0:t0 + T], mu)
            p.dma(o_stats[1:2, t0:t0 + T], rs)
            p.mm(ps_mu[:, 0:T], ones128, mu)
            p.mm(ps_rs[:, 0:T], ones128, rs)
            for c in range(EC):
                xn_chunk(p, xn_f, xt, ps_mu, ps_rs, c, T)
                p.red(rtmp, xn_f[:, 0:T], ALU.max, absval=True)
                p.tt(vmax, vmax, rtmp, ALU.max)
                p.copy(xn_bf[:, c * T:(c + 1) * T], xn_f[:, 0:T])

            for (w_bf, b0, hbm, do_qsum) in [(wq_bf, 0, o_Q, True),
                                             (wv_bf, 3, o_V, False)]:
                for mc in range(EC):
                    for kc in range(EC):
                        p.mm(ps_p[:, 0:T],
                             w_bf[:, kc * E + mc * 128: kc * E + mc * 128 + 128],
                             xn_bf[:, kc * T:(kc + 1) * T],
                             start=(kc == 0), stop=(kc == EC - 1))
                    p.ts(q_bf[:, mc * T:(mc + 1) * T], ps_p[:, 0:T],
                         vec[:, b0 * EC + mc: b0 * EC + mc + 1], None, ALU.add)
                ln_stats(p, q_bf, sq_bf, ones1, ps_s1, ps_s2, T, mu2, rs2, scr1)
                p.mm(ps_mu[:, 0:T], ones128, mu2)
                p.mm(ps_rs[:, 0:T], ones128, rs2)
                apply_ln(p, out_bf, q_bf, ps_mu, ps_rs, scr, T,
                         g_vec=vec[:, (b0 + 1) * EC:(b0 + 2) * EC],
                         b_vec=vec[:, (b0 + 2) * EC:(b0 + 3) * EC])
                p.dma(fm_slice(hbm, t0, T), tile_ap(out_bf, EC, T), eng="gpsimd")
                if do_qsum:
                    for c in range(EC):
                        p.red(rtmp, out_bf[:, c * T:(c + 1) * T], ALU.add)
                        p.tt(qsum[:, c:c + 1], qsum[:, c:c + 1], rtmp, ALU.add)

        p.ts(qsum[:], qsum[:], 1.0 / (L * np.sqrt(D)), None, ALU.mult)
        p.dma(o_qg, qsum[:])
        cross_part_max(p, vmax, bnc, rrow, mrow[0:1, 0:1])
        p.dma(o_mx, mrow[:])
        p.emit(ctxs)
    return p


def build_L2():
    p = P()
    nc = p.nc
    x = p.din("x", [E, L])
    stats = p.din("stats", [2, L])
    qg = p.din("qg", [128, EC])
    mxs = p.din("mxs", [1, 128])
    clip_x = p.din("clip_x", [1, 1])
    Wk = p.din("Wk", [E, E])
    Wpk = p.din("Wpk", [E, E])
    veck = p.din("veck", [128, 4 * EC])
    bpk_row = p.din("bpk_row", [1, E])
    w_gate = p.din("w_gate", [IF, E])
    w_up = p.din("w_up", [IF, E])
    Vc = p.din("Vc", [E, L], BF16)

    o_h2 = p.dout("h2c", [IF, L], BF16)
    o_kv = p.dout("kv", [128, H * 32])
    o_ks = p.dout("ks", [16, E])
    o_mxh = p.dout("mxh", [1, 16])
    Kc = p.dtmp("Kc", [E, L], BF16)
    bnc = p.dtmp("bnc", [128, 1])
    bnc16 = p.dtmp("bnc16", [16, 1])

    ctxs = ExitStack()
    with ctxs:
        sb = lambda n, s, d=F32: ctxs.enter_context(p.sbuf(n, s, d))
        ps = lambda n, s, d=F32: ctxs.enter_context(p.psum(n, s, d))

        stage = sb("stage", [128, E])
        ones1 = sb("ones1", [128, 1], BF16)
        ones128 = sb("ones128", [1, 128], F32)
        ident_b = sb("ident_b", [128, 128], BF16)
        ident_f = sb("ident_f", [128, 128], F32)
        xt = sb("xt", [128, EC * T])
        scr = sb("scr", [128, T])
        strow = sb("strow", [1, 5 * T])
        scores = sb("scores", [16, L])
        srow16 = sb("srow16", [16, 4])
        hmax = sb("hmax", [128, 1])
        rtmp = sb("rtmp", [128, 1])
        mrow = sb("mrow", [1, 16])
        rrow = sb("rrow", [1, 128])
        sdr = sb("sdr", [1, 4])
        rsx_pp = sb("rsx_pp", [128, 1])
        sx_pp = sb("sx_pp", [128, 1])
        a_tm_f = sb("a_tm_f", [128, (L // 128) * 16])
        a_tm_b = sb("a_tm_b", [128, (L // 128) * 16], BF16)
        a0r_b = sb("a0r_b", [1, 16], BF16)
        pk0_b = sb("pk0_b", [1, E], BF16)
        pk0_f = sb("pk0_f", [1, E])

        mu2 = strow[0:1, 3 * T:4 * T]
        rs2 = strow[0:1, 4 * T:5 * T]
        scr1 = strow[0:1, 2 * T:3 * T]

        p.memset(ones1[:], 1.0)
        p.memset(ones1f[:], 1.0)
        p.memset(ones128[:], 1.0)
        p.memset(hmax[:], 0.0)
        p.memset(mrow[:], 0.0)
        p.identity_fill(ident_b[:], 128)
        p.identity_fill(ident_f[:], 128)

        with ExitStack() as pha:
            sba = lambda n, s, d=F32: pha.enter_context(p.sbuf(n, s, d))
            psa = lambda n, s, d=F32: pha.enter_context(p.psum(n, s, d))
            wgT = sba("wgT", [128, IC * EC * 128], BF16)
            wuT = sba("wuT", [128, IC * EC * 128], BF16)
            sg = sba("sg", [128, IC])
            su = sba("su", [128, IC])
            xq_bf = sba("xq_bf", [128, EC * T], BF16)
            yq = sba("yq", [128, E])
            i1 = sba("i1", [128, E], I32)
            i2 = sba("i2", [128, E], I32)
            tfq = sba("tfq", [128, E])
            wi_b = sba("wi_b", [128, E], BF16)
            gs = sba("gs", [128, T])
            us = sba("us", [128, T])
            h2f = sba("h2f", [128, T])
            h2b = sba("h2b", [128, T], BF16)
            rm = sba("rm", [128, 1])
            ps_mu = psa("ps_mu", [128, T])
            ps_rs = psa("ps_rs", [128, T])
            ps_g = psa("ps_g", [128, T])
            ps_4 = psa("ps_4", [128, 4])
            psq = psa("psq", [128, 128], BF16)

            p.dma(rrow[0:1, 0:128], mxs)
            make_scale(p, rrow, clip_x, sdr, ones128, ps_4, rsx_pp, sx_pp)

            for (w_hbm, wT, sc) in [(w_gate, wgT, sg), (w_up, wuT, su)]:
                for ic in range(IC):
                    p.dma(stage[:, 0:E], w_hbm[ic * 128:(ic + 1) * 128, :])
                    p.red(rm, stage[:, 0:E], ALU.max, absval=True)
                    p.ts(sc[:, ic:ic + 1], rm, 1.0 / 127.0, None, ALU.mult)
                    p.recip(rm, rm)
                    p.ts(rm, rm, 127.0, None, ALU.mult)
                    p.act(yq[:, 0:E], stage[:, 0:E], AF.Copy, bias=256.5, scale=rm)
                    quant_round(p, yq, i1, i2, tfq, wi_b, E)
                    for kc in range(EC):
                        p.trans(psq[:, 0:128], wi_b[:, kc * 128:(kc + 1) * 128], ident_b[:])
                        p.copy(wT[:, (ic * EC + kc) * 128:(ic * EC + kc) * 128 + 128],
                               psq[:, 0:128])
                for ic in range(IC):
                    p.tt(sc[:, ic:ic + 1], sc[:, ic:ic + 1], sx_pp[:, 0:1], ALU.mult)

            for ti in range(NT):
                t0 = ti * T
                p.dma(tile_ap(xt, EC, T), fm_slice(x, t0, T))
                load_stats(p, stats, strow, ones128, ps_mu, ps_rs, t0, T)
                for c in range(EC):
                    xn_chunk(p, scr, xt, ps_mu, ps_rs, c, T)
                    p.act(yq[:, 0:T], scr[:, 0:T], AF.Copy, bias=256.5,
                          scale=rsx_pp[:, 0:1])
                    quant_round(p, yq, i1, i2, tfq, h2b, T)
                    p.copy(xq_bf[:, c * T:(c + 1) * T], h2b[:, 0:T])
                for ic in range(IC):
                    for kc in range(EC):
                        p.mm(ps_g[:, 0:T],
                             wgT[:, (ic * EC + kc) * 128:(ic * EC + kc) * 128 + 128],
                             xq_bf[:, kc * T:(kc + 1) * T],
                             start=(kc == 0), stop=(kc == EC - 1))
                    p.act(gs[:, 0:T], ps_g[:, 0:T], AF.Silu, scale=sg[:, ic:ic + 1])
                    for kc in range(EC):
                        p.mm(ps_g[:, 0:T],
                             wuT[:, (ic * EC + kc) * 128:(ic * EC + kc) * 128 + 128],
                             xq_bf[:, kc * T:(kc + 1) * T],
                             start=(kc == 0), stop=(kc == EC - 1))
                    p.ts(us[:, 0:T], ps_g[:, 0:T], su[:, ic:ic + 1], None, ALU.mult)
                    p.tt(h2f[:, 0:T], gs[:, 0:T], us[:, 0:T], ALU.mult)
                    p.copy(h2b[:, 0:T], h2f[:, 0:T])
                    p.dma(fm_slice_one(o_h2, t0, T, ic), row3(h2b.ap()[:, 0:T], T),
                          eng="gpsimd")
                    p.red(rtmp, h2f[:, 0:T], ALU.max, absval=True)
                    p.tt(hmax, hmax, rtmp, ALU.max)

            cross_part_max(p, hmax, bnc, rrow, mrow[0:1, 0:1])
            p.dma(o_mxh, mrow[:])

        with ExitStack() as phb:
            sbb = lambda n, s, d=F32: phb.enter_context(p.sbuf(n, s, d))
            psb = lambda n, s, d=F32: phb.enter_context(p.psum(n, s, d))
            wk_bf = sbb("wk_bf", [128, EC * E], BF16)
            vec = sbb("vec", [128, 4 * EC])
            qsl = sbb("qsl", [128, EC * 16], BF16)
            qg_sb = sbb("qg_sb", [128, EC])
            xn_bf = sbb("xn_bf", [128, EC * T], BF16)
            k_bf = sbb("k_bf", [128, EC * T], BF16)
            sq_bf = sbb("sq_bf", [128, EC * T], BF16)
            out_bf = sbb("out_bf", [128, EC * T], BF16)
            ps_s1 = psb("ps_s1", [1, T])
            ps_s2 = psb("ps_s2", [1, T])
            ps_mu = psb("ps_mu", [128, T])
            ps_rs = psb("ps_rs", [128, T])
            ps_p = psb("ps_p", [128, T])
            ps_sc = psb("ps_sc", [16, T])

            p.dma(vec[:], veck)
            load_weight_bf16(p, wk_bf, Wk, E, E, stage)
            p.dma(qg_sb[:], qg)
            p.memset(qsl[:], 0.0)
            for c in range(EC):
                p.copy(qsl[0:64, c * 16 + 2 * c: c * 16 + 2 * c + 1],
                       qg_sb[0:64, c:c + 1])
                p.copy(qsl[64:128, c * 16 + 2 * c + 1: c * 16 + 2 * c + 2],
                       qg_sb[64:128, c:c + 1])
            p.set_serial(False)

            for ti in range(NT):
                t0 = ti * T
                p.dma(tile_ap(xt, EC, T), fm_slice(x, t0, T))
                load_stats(p, stats, strow, ones128, ps_mu, ps_rs, t0, T)
                for c in range(EC):
                    xn_chunk(p, scr, xt, ps_mu, ps_rs, c, T)
                    p.copy(xn_bf[:, c * T:(c + 1) * T], scr[:, 0:T])
                for mc in range(EC):
                    for kc in range(EC):
                        p.mm(ps_p[:, 0:T],
                             wk_bf[:, kc * E + mc * 128: kc * E + mc * 128 + 128],
                             xn_bf[:, kc * T:(kc + 1) * T],
                             start=(kc == 0), stop=(kc == EC - 1))
                    p.ts(k_bf[:, mc * T:(mc + 1) * T], ps_p[:, 0:T],
                         vec[:, mc:mc + 1], None, ALU.add)
                ln_stats(p, k_bf, sq_bf, ones1, ps_s1, ps_s2, T, mu2, rs2, scr1)
                p.mm(ps_mu[:, 0:T], ones128, mu2)
                p.mm(ps_rs[:, 0:T], ones128, rs2)
                apply_ln(p, out_bf, k_bf, ps_mu, ps_rs, scr, T,
                         g_vec=vec[:, EC:2 * EC], b_vec=vec[:, 2 * EC:3 * EC])
                p.dma(fm_slice(Kc, t0, T), tile_ap(out_bf, EC, T), eng="gpsimd")
                for kc in range(EC):
                    p.mm(ps_sc[0:16, 0:T], qsl[:, kc * 16:(kc + 1) * 16],
                         out_bf[:, kc * T:(kc + 1) * T],
                         start=(kc == 0), stop=(kc == EC - 1))
                p.copy(scores[0:16, t0:t0 + T], ps_sc[0:16, 0:T])

        p.red(srow16[0:16, 0:1], scores[:], ALU.max)
        p.ts(srow16[0:16, 0:1], srow16[0:16, 0:1], 0.0, None, ALU.max)
        p.ts(srow16[0:16, 1:2], srow16[0:16, 0:1], -1.0, None, ALU.mult)
        p.act(scores[:], scores[:], AF.Exp, bias=srow16[0:16, 1:2])
        p.red(srow16[0:16, 2:3], scores[:], ALU.add)
        p.act(srow16[0:16, 3:4], srow16[0:16, 1:2], AF.Exp)
        p.tt(srow16[0:16, 2:3], srow16[0:16, 2:3], srow16[0:16, 3:4], ALU.add)
        p.recip(srow16[0:16, 2:3], srow16[0:16, 2:3])
        p.ts(srow16[0:16, 2:3], srow16[0:16, 2:3], float(L), None, ALU.mult)
        p.ts(scores[:], scores[:], srow16[0:16, 2:3], None, ALU.mult)
        p.tt(srow16[0:16, 3:4], srow16[0:16, 3:4], srow16[0:16, 2:3], ALU.mult)
        p.dma(bnc16, srow16[0:16, 3:4])
        p.dma(rrow[0:1, 0:16], bnc16.rearrange("p c -> c p"))
        p.copy(a0r_b[0:1, 0:16], rrow[0:1, 0:16])
        with ExitStack() as phs:
            ptr_f = phs.enter_context(p.psum("ptr_f", [128, 16], F32))
            for tc in range(L // 128):
                p.trans(ptr_f[:, 0:16], scores[0:16, tc * 128:(tc + 1) * 128],
                        ident_f[0:16, 0:16])
                p.copy(a_tm_f[:, tc * 16:(tc + 1) * 16], ptr_f[:, 0:16])
        p.copy(a_tm_b[:], a_tm_f[:])
        p.dma(pk0_f[0:1, 0:E], bpk_row)
        p.act(pk0_f[0:1, 0:E], pk0_f[0:1, 0:E], AF.Tanh)
        p.ts(pk0_b[0:1, 0:E], pk0_f[0:1, 0:E], 1.0, None, ALU.add)

        with ExitStack() as phc:
            sbc = lambda n, s, d=F32: phc.enter_context(p.sbuf(n, s, d))
            psc = lambda n, s, d=F32: phc.enter_context(p.psum(n, s, d))
            wpk_bf = sbc("wpk_bf", [128, EC * E], BF16)
            vec = sbc("vec2", [128, 4 * EC])
            k_bf = sbc("k_bf2", [128, EC * T], BF16)
            v_bf = sbc("v_bf", [128, EC * T], BF16)
            pk_bf = sbc("pk_bf", [128, EC * T], BF16)
            pkt_b = sbc("pkt_b", [128, E], BF16)
            vst_b = sbc("vst_b", [128, E], BF16)
            kvout = sbc("kvout", [128, H * 32])
            ksout = sbc("ksout", [16, E])
            ps_p = psc("ps_p3", [128, T])
            ptr1 = psc("ptr1", [128, 1024], BF16)
            ptr2 = psc("ptr2", [128, 1024], BF16)
            ps_kv = psc("ps_kv", [128, H * 32])
            ps_ks = psc("ps_ks", [16, E])

            p.dma(vec[:], veck)
            load_weight_bf16(p, wpk_bf, Wpk, E, E, stage)
            p.memset(ps_kv[:], 0.0, eng="vector")
            p.memset(ps_ks[0:16, :], 0.0, eng="vector")
            for c in range(EC):
                p.mm(ps_ks[0:16, c * 128:(c + 1) * 128], a0r_b[0:1, 0:16],
                     pk0_b[0:1, c * 128:(c + 1) * 128], start=False, stop=False)
            p.set_serial(False)

            for ti in range(NT):
                t0 = ti * T
                p.dma(tile_ap(k_bf, EC, T), fm_slice(Kc, t0, T))
                p.dma(tile_ap(v_bf, EC, T), fm_slice(Vc, t0, T))
                for mc in range(EC):
                    for kc in range(EC):
                        p.mm(ps_p[:, 0:T],
                             wpk_bf[:, kc * E + mc * 128: kc * E + mc * 128 + 128],
                             k_bf[:, kc * T:(kc + 1) * T],
                             start=(kc == 0), stop=(kc == EC - 1))
                    p.act(scr[:, 0:T], ps_p[:, 0:T], AF.Tanh,
                          bias=vec[:, 3 * EC + mc:3 * EC + mc + 1])
                    p.ts(pk_bf[:, mc * T:(mc + 1) * T], scr[:, 0:T], 1.0, None,
                         ALU.add)
                for tc in range(TC):
                    gtc = ti * TC + tc
                    last = (ti == NT - 1 and tc == TC - 1)
                    for c in range(EC):
                        p.trans(ptr1[:, 0:128], pk_bf[:, c * T + tc * 128: c * T + tc * 128 + 128],
                                ident_b[:])
                        p.copy(pkt_b[:, c * 128:(c + 1) * 128], ptr1[:, 0:128])
                        p.trans(ptr2[:, 0:128], v_bf[:, c * T + tc * 128: c * T + tc * 128 + 128],
                                ident_b[:])
                        p.ts(vst_b[:, c * 128:c * 128 + 64], ptr2[:, 0:64],
                             a_tm_f[:, gtc * 16 + 2 * c: gtc * 16 + 2 * c + 1],
                             None, ALU.mult)
                        p.ts(vst_b[:, c * 128 + 64:c * 128 + 128], ptr2[:, 64:128],
                             a_tm_f[:, gtc * 16 + 2 * c + 1: gtc * 16 + 2 * c + 2],
                             None, ALU.mult)
                    for c in range(EC):
                        p.mm(ps_ks[0:16, c * 128:(c + 1) * 128],
                             a_tm_b[:, gtc * 16:(gtc + 1) * 16],
                             pkt_b[:, c * 128:(c + 1) * 128], start=False, stop=last)
                    for h in range(H):
                        c, r = h // 2, h % 2
                        p.mm(ps_kv[r * 64:(r + 1) * 64, c * 64:(c + 1) * 64],
                             pkt_b[:, h * 64:(h + 1) * 64],
                             vst_b[:, h * 64:(h + 1) * 64],
                             start=False, stop=last)

            p.copy(kvout[:], ps_kv[:])
            p.dma(o_kv, kvout[:])
            p.copy(ksout[0:16, :], ps_ks[0:16, :])
            p.dma(o_ks, ksout[0:16, :])

        p.emit(ctxs)
    return p


def build_L3():
    p = P()
    nc = p.nc
    x = p.din("x", [E, L])
    stats = p.din("stats", [2, L])
    Qc = p.din("Qc", [E, L], BF16)
    kv = p.din("kv", [128, H * 32])
    ks = p.din("ks", [16, E])
    Wpq = p.din("Wpq", [E, E])
    veca = p.din("veca", [128, 4 * EC])
    cwT = p.din("cwT", [5 * 64, E])
    gamma = p.din("gamma", [1, 1])
    W1 = p.din("W1", [E, MLP])
    b1 = p.din("b1", [128, MC])
    W2 = p.din("W2", [MLP, E])
    b2 = p.din("b2", [128, EC])
    w_down = p.din("w_down", [E, IF])
    clip_down = p.din("clip_down", [1, 1])
    mxh_all = p.din("mxh_all", [1, 128])
    h2c = p.din("h2c", [IF, L], BF16)
    sel = p.din("sel", [EC * 16, 128])

    o_out = p.dout("out", [E, L])
    attn2c = p.dtmp("attn2c", [E, L], BF16)
    h1c = p.dtmp("h1c", [MLP, L], BF16)
    s_c = p.dtmp("s_c", [E, L], BF16)

    ctxs = ExitStack()
    with ctxs:
        sb = lambda n, s, d=F32: ctxs.enter_context(p.sbuf(n, s, d))

        stage = sb("stage", [128, MLP])
        vec = sb("vec", [128, 4 * EC])
        ones1 = sb("ones1", [128, 1], BF16)
        ones128 = sb("ones128", [1, 128], F32)
        ident_b = sb("ident_b", [128, 128], BF16)
        xt = sb("xt", [128, EC * T])
        scr = sb("scr", [128, T])
        scr2 = sb("scr2", [128, T])
        strow = sb("strow", [1, 5 * T])

        mu2 = strow[0:1, 3 * T:4 * T]
        rs2 = strow[0:1, 4 * T:5 * T]
        scr1 = strow[0:1, 2 * T:3 * T]

        p.memset(ones1[:], 1.0)
        p.memset(ones1f[:], 1.0)
        p.memset(ones128[:], 1.0)
        p.identity_fill(ident_b[:], 128)
        p.dma(vec[:], veca)

        with ExitStack() as pha:
            sba = lambda n, s, d=F32: pha.enter_context(p.sbuf(n, s, d))
            psa = lambda n, s, d=F32: pha.enter_context(p.psum(n, s, d))
            wpq_bf = sba("wpq_bf", [128, EC * E], BF16)
            numT = sba("numT", [128, EC * 128], BF16)
            denT = sba("denT", [128, EC * 16], BF16)
            sel_b = sba("sel_b", [16, EC * 128], BF16)
            cvT = sba("cvT", [128, EC * 5 * 128], BF16)
            kv_f = sba("kv_f", [128, H * 32])
            kv_b = sba("kv_b", [128, H * 32], BF16)
            ksf = sba("ksf", [16, E])
            ks_b = sba("ks_b", [16, E], BF16)
            selt = sba("selt", [128, 128])
            q_bf = sba("q_bf", [128, EC * T], BF16)
            sq_bf = sba("sq_bf", [128, EC * T], BF16)
            pq_bf = sba("pq_bf", [128, EC * T], BF16)
            xn_f = sba("xn_f", [128, EC * T])
            attn_a = sba("attn_a", [128, EC * T], BF16)
            attn_b_ = sba("attn_b", [128, EC * T], BF16)
            tail2 = sba("tail2", [128, EC * 2], BF16)
            conv_win = sba("conv_win", [128, EC * (T + 4)], BF16)
            attn2_bf = sba("attn2_bf", [128, EC * T], BF16)
            rden = sba("rden", [16, T])
            rden_b = sba("rden_b", [16, T], BF16)
            rdb = sba("rdb", [128, T])
            gam_pp = sba("gam_pp", [128, 1])
            gcb = sba("gcb", [128, EC])
            ps_p = psa("ps_p", [128, T])
            ps_den = psa("ps_den", [16, T])
            ps_a = psa("ps_a", [128, T])
            ps_rdb = psa("ps_rdb", [128, T])

            load_weight_bf16(p, wpq_bf, Wpq, E, E, stage)
            for c in range(EC):
                p.dma(selt[0:16, 0:128], sel[c * 16:(c + 1) * 16, :])
                p.copy(sel_b[0:16, c * 128:(c + 1) * 128], selt[0:16, 0:128])
            p.memset(cvT[:], 0.0)
            for c in range(EC):
                for k in range(5):
                    col = (c * 5 + k) * 128
                    p.dma(selt[0:64, 0:64],
                          cwT[k * 64:(k + 1) * 64, 2 * c * 64:(2 * c + 1) * 64])
                    p.copy(cvT[0:64, col:col + 64], selt[0:64, 0:64])
                    p.dma(selt[64:128, 0:64],
                          cwT[k * 64:(k + 1) * 64, (2 * c + 1) * 64:(2 * c + 2) * 64])
                    p.copy(cvT[64:128, col + 64:col + 128], selt[64:128, 0:64])
            p.dma(kv_f[:], kv)
            p.copy(kv_b[:], kv_f[:])
            p.memset(numT[:], 0.0)
            for c in range(EC):
                p.copy(numT[0:64, c * 128: c * 128 + 64],
                       kv_b[0:64, c * 64:(c + 1) * 64])
                p.copy(numT[64:128, c * 128 + 64: c * 128 + 128],
                       kv_b[64:128, c * 64:(c + 1) * 64])
            p.dma(ksf[0:16, :], ks)
            p.copy(ks_b[0:16, :], ksf[0:16, :])
            p.memset(denT[:], 0.0)
            with ExitStack() as pht:
                ptr_b = pht.enter_context(p.psum("ptr_b", [128, 16], BF16))
                ps_t4 = pht.enter_context(p.psum("ps_t4", [128, 4], F32))
                p.dma(strow[0:1, 0:1], gamma)
                p.mm(ps_t4[:, 0:1], ones128, strow[0:1, 0:1])
                p.copy(gam_pp[:, 0:1], ps_t4[:, 0:1])
                p.ts(gcb[:], vec[:, 3 * EC:4 * EC], gam_pp[:, 0:1], None, ALU.mult)
                for c in range(EC):
                    p.trans(ptr_b[:, 0:16], ks_b[0:16, c * 128:(c + 1) * 128],
                            ident_b[0:16, 0:16])
                    p.copy(denT[0:64, c * 16 + 2 * c: c * 16 + 2 * c + 1],
                           ptr_b[0:64, 2 * c:2 * c + 1])
                    p.copy(denT[64:128, c * 16 + 2 * c + 1: c * 16 + 2 * c + 2],
                           ptr_b[64:128, 2 * c + 1:2 * c + 2])
            p.barrier()
            p.set_serial(False)
            ps_s1 = psa("ps_s1", [1, T])
            ps_s2 = psa("ps_s2", [1, T])
            ps_mu = psa("ps_mu", [128, T])
            ps_rs = psa("ps_rs", [128, T])

            def conv_ln_store(ti_prev, a_cur, a_prev):
                if "S4" in DBG_SER:
                    p.set_serial(True)
                t0p = ti_prev * T
                for c in range(EC):
                    w0 = c * (T + 4)
                    if ti_prev == 0:
                        p.memset(conv_win[:, w0:w0 + 2], 0.0, eng="vector")
                    else:
                        p.copy(conv_win[:, w0:w0 + 2], tail2[:, c * 2:(c + 1) * 2])
                    p.copy(conv_win[:, w0 + 2:w0 + 2 + T],
                           a_prev[:, c * T:(c + 1) * T])
                    if ti_prev == NT - 1:
                        p.memset(conv_win[:, w0 + 2 + T:w0 + 4 + T], 0.0, eng="vector")
                    else:
                        p.copy(conv_win[:, w0 + 2 + T:w0 + 4 + T],
                               a_cur[:, c * T:c * T + 2])
                    p.copy(tail2[:, c * 2:(c + 1) * 2],
                           a_prev[:, (c + 1) * T - 2:(c + 1) * T])
                for c in range(EC):
                    for k in range(5):
                        p.mm(ps_a[:, 0:T],
                             cvT[:, (c * 5 + k) * 128:(c * 5 + k) * 128 + 128],
                             conv_win[:, c * (T + 4) + k: c * (T + 4) + k + T],
                             start=(k == 0), stop=(k == 4))
                    p.ts(scr[:, 0:T], ps_a[:, 0:T], gam_pp[:, 0:1],
                         gcb[:, c:c + 1], ALU.mult, ALU.add)
                    p.tt(xn_f[:, c * T:(c + 1) * T], scr[:, 0:T],
                         a_prev[:, c * T:(c + 1) * T], ALU.add)
                p.copy(sq_bf[:], xn_f[:])
                ln_stats(p, sq_bf, q_bf, ones1, ps_s1, ps_s2, T, mu2, rs2, scr1)
                p.mm(ps_mu[:, 0:T], ones128, mu2)
                p.mm(ps_rs[:, 0:T], ones128, rs2)
                apply_ln(p, attn2_bf, sq_bf, ps_mu, ps_rs, scr, T,
                         g_vec=vec[:, EC:2 * EC], b_vec=vec[:, 2 * EC:3 * EC])
                p.dma(fm_slice(attn2c, t0p, T), tile_ap(attn2_bf, EC, T),
                      eng="gpsimd")
                if "S4" in DBG_SER:
                    p.set_serial(False)

            for ti in range(NT):
                t0 = ti * T
                a_cur = attn_a if ti % 2 == 0 else attn_b_
                a_prev = attn_b_ if ti % 2 == 0 else attn_a
                if "S1" in DBG_SER:
                    p.set_serial(True)
                p.dma(tile_ap(q_bf, EC, T), fm_slice(Qc, t0, T))
                for mc in range(EC):
                    for kc in range(EC):
                        p.mm(ps_p[:, 0:T],
                             wpq_bf[:, kc * E + mc * 128: kc * E + mc * 128 + 128],
                             q_bf[:, kc * T:(kc + 1) * T],
                             start=(kc == 0), stop=(kc == EC - 1))
                    p.act(scr[:, 0:T], ps_p[:, 0:T], AF.Tanh, bias=vec[:, mc:mc + 1])
                    p.ts(pq_bf[:, mc * T:(mc + 1) * T], scr[:, 0:T], 1.0, None,
                         ALU.add)
                if "S1" in DBG_SER:
                    p.set_serial(False)
                if "S2" in DBG_SER:
                    p.set_serial(True)
                for c in range(EC):
                    p.mm(ps_den[0:16, 0:T], denT[:, c * 16:(c + 1) * 16],
                         pq_bf[:, c * T:(c + 1) * T],
                         start=(c == 0), stop=(c == EC - 1))
                p.ts(rden[0:16, 0:T], ps_den[0:16, 0:T], DEN_EPS, None, ALU.add)
                p.recip(rden[0:16, 0:T], rden[0:16, 0:T])
                p.copy(rden_b[0:16, 0:T], rden[0:16, 0:T])
                if "S2" in DBG_SER:
                    p.set_serial(False)
                if "S3" in DBG_SER:
                    p.set_serial(True)
                for c in range(EC):
                    p.mm(ps_a[:, 0:T], numT[:, c * 128:(c + 1) * 128],
                         pq_bf[:, c * T:(c + 1) * T])
                    p.mm(ps_rdb[:, 0:T], sel_b[0:16, c * 128:(c + 1) * 128],
                         rden_b[0:16, 0:T])
                    p.copy(rdb[:, 0:T], ps_rdb[:, 0:T])
                    p.tt(a_cur[:, c * T:(c + 1) * T], ps_a[:, 0:T], rdb[:, 0:T],
                         ALU.mult)
                if "S3" in DBG_SER:
                    p.set_serial(False)
                if ti > 0:
                    conv_ln_store(ti - 1, a_cur, a_prev)
            last_buf = attn_b_ if (NT - 1) % 2 == 1 else attn_a
            conv_ln_store(NT - 1, last_buf, last_buf)

        with ExitStack() as phb:
            w1_bf = phb.enter_context(p.sbuf("w1_bf", [128, EC * MLP], BF16))
            ps_p = phb.enter_context(p.psum("ps_pb", [128, T], F32))
            h1_bf = phb.enter_context(p.sbuf("h1_bf", [128, T], BF16))
            a2_bf = phb.enter_context(p.sbuf("a2b_bf", [128, EC * T], BF16))
            b1_sb = phb.enter_context(p.sbuf("b1_sb", [128, MC], F32))
            p.dma(b1_sb[:], b1)
            load_weight_bf16_2q(p, w1_bf, W1, E, MLP, stage, stage2)
            for ti in range(NT):
                t0 = ti * T
                p.dma(tile_ap(a2_bf, EC, T), fm_slice(attn2c, t0, T))
                for mc in range(MC):
                    for kc in range(EC):
                        p.mm(ps_p[:, 0:T],
                             w1_bf[:, kc * MLP + mc * 128: kc * MLP + mc * 128 + 128],
                             a2_bf[:, kc * T:(kc + 1) * T],
                             start=(kc == 0), stop=(kc == EC - 1))
                    p.act(h1_bf[:, 0:T], ps_p[:, 0:T], AF.Gelu,
                          bias=b1_sb[:, mc:mc + 1])
                    p.dma(fm_slice_one(h1c, t0, T, mc), row3(h1_bf.ap()[:, 0:T], T),
                          eng="gpsimd")

        with ExitStack() as phc:
            w2_bf = phc.enter_context(p.sbuf("w2_bf", [128, MC * E], BF16))
            ps_p = phc.enter_context(p.psum("ps_pc", [128, T], F32))
            ps_mu = phc.enter_context(p.psum("ps_muc", [128, T], F32))
            ps_rs = phc.enter_context(p.psum("ps_rsc", [128, T], F32))
            h1t = phc.enter_context(p.sbuf("h1t", [128, MC * T], BF16))
            b2_sb = phc.enter_context(p.sbuf("b2_sb", [128, EC], F32))
            s_bf = phc.enter_context(p.sbuf("s_bf", [128, EC * T], BF16))
            a2_bf = phc.enter_context(p.sbuf("a2c_bf", [128, EC * T], BF16))
            p.dma(b2_sb[:], b2)
            load_weight_bf16_2q(p, w2_bf, W2, MLP, E, stage, stage2)
            for ti in range(NT):
                t0 = ti * T
                h1t = h1B[ti % 2]
                p.dma(tile_ap(h1t, MC, T), fm_slice(h1c, t0, T, nchunk=MC))
                p.dma(tile_ap(a2_bf, EC, T), fm_slice(attn2c, t0, T))
                p.dma(tile_ap(xt, EC, T), fm_slice(x, t0, T))
                load_stats(p, stats, strow, ones128, ps_mu, ps_rs, t0, T)
                for mc in range(EC):
                    for kc in range(MC):
                        p.mm(ps_p[:, 0:T],
                             w2_bf[:, kc * E + mc * 128: kc * E + mc * 128 + 128],
                             h1t[:, kc * T:(kc + 1) * T],
                             start=(kc == 0), stop=(kc == MC - 1))
                    p.ts(scr[:, 0:T], ps_p[:, 0:T], b2_sb[:, mc:mc + 1], None, ALU.add)
                    p.tt(scr[:, 0:T], scr[:, 0:T], a2_bf[:, mc * T:(mc + 1) * T],
                         ALU.add)
                    xn_chunk(p, scr2, xt, ps_mu, ps_rs, mc, T)
                    p.tt(scr[:, 0:T], scr[:, 0:T], scr2[:, 0:T], ALU.add)
                    p.copy(s_bf[:, mc * T:(mc + 1) * T], scr[:, 0:T])
                p.dma(fm_slice(s_c, t0, T), tile_ap(s_bf, EC, T), eng="gpsimd")

        with ExitStack() as phd:
            sbd = lambda n, s, d=F32: phd.enter_context(p.sbuf(n, s, d))
            psd = lambda n, s, d=F32: phd.enter_context(p.psum(n, s, d))
            wdT = sbd("wdT", [128, EC * IC * 128], BF16)
            swd = sbd("swd", [128, EC])
            h2t = sbd("h2t", [128, IC * T], BF16)
            h2q = sbd("h2q", [128, IC * T], BF16)
            yq = sbd("yq", [128, IF])
            i1 = sbd("i1", [128, IF], I32)
            i2 = sbd("i2", [128, IF], I32)
            tfq = sbd("tfq", [128, IF])
            s_b = sbd("s_b", [128, EC * T], BF16)
            rsd_pp = sbd("rsd_pp", [128, 1])
            sd_pp = sbd("sd_pp", [128, 1])
            rrow = sbd("rrow", [1, 128])
            sdr = sbd("sdr", [1, 4])
            rm = sbd("rm", [128, 1])
            wi_b = sbd("wi_b", [128, IF], BF16)
            ps_p = psd("ps_pd", [128, T])
            ps_4 = psd("ps_4d", [128, 4])
            psq = psd("psqd", [128, 128], BF16)

            p.dma(rrow[0:1, 0:128], mxh_all)
            make_scale(p, rrow, clip_down, sdr, ones128, ps_4, rsd_pp, sd_pp)
            for hc in range(EC):
                p.dma(stage[:, 0:IF], w_down[hc * 128:(hc + 1) * 128, :])
                p.red(rm, stage[:, 0:IF], ALU.max, absval=True)
                p.ts(swd[:, hc:hc + 1], rm, 1.0 / 127.0, None, ALU.mult)
                p.recip(rm, rm)
                p.ts(rm, rm, 127.0, None, ALU.mult)
                p.act(yq[:, 0:IF], stage[:, 0:IF], AF.Copy, bias=256.5, scale=rm)
                quant_round(p, yq, i1, i2, tfq, wi_b, IF)
                for ic in range(IC):
                    p.trans(psq[:, 0:128], wi_b[:, ic * 128:(ic + 1) * 128], ident_b[:])
                    p.copy(wdT[:, (hc * IC + ic) * 128:(hc * IC + ic) * 128 + 128],
                           psq[:, 0:128])
                p.tt(swd[:, hc:hc + 1], swd[:, hc:hc + 1], sd_pp[:, 0:1], ALU.mult)
            p.set_serial(False)

            for ti in range(NT):
                t0 = ti * T
                p.dma(tile_ap(h2t, IC, T), fm_slice(h2c, t0, T, nchunk=IC))
                p.dma(tile_ap(s_b, EC, T), fm_slice(s_c, t0, T))
                p.dma(tile_ap(xt, EC, T), fm_slice(x, t0, T))
                for ic in range(IC):
                    p.act(yq[:, 0:T], h2t[:, ic * T:(ic + 1) * T], AF.Copy,
                          bias=256.5, scale=rsd_pp[:, 0:1])
                    p.copy(i1[:, 0:T], yq[:, 0:T])
                    p.stt(i2[:, 0:T], yq[:, 0:T], -0.5, i1[:, 0:T], ALU.add,
                          ALU.subtract)
                    p.stt(tfq[:, 0:T], i1[:, 0:T], -256.0, i2[:, 0:T], ALU.add,
                          ALU.add)
                    p.ts(h2q[:, ic * T:(ic + 1) * T], tfq[:, 0:T], 127.0, -128.0,
                         ALU.min, ALU.max)
                for mc in range(EC):
                    for ic in range(IC):
                        p.mm(ps_p[:, 0:T],
                             wdT[:, (mc * IC + ic) * 128:(mc * IC + ic) * 128 + 128],
                             h2q[:, ic * T:(ic + 1) * T],
                             start=(ic == 0), stop=(ic == IC - 1))
                    p.act(scr[:, 0:T], ps_p[:, 0:T], AF.Silu, scale=swd[:, mc:mc + 1])
                    p.tt(scr[:, 0:T], scr[:, 0:T], s_b[:, mc * T:(mc + 1) * T],
                         ALU.mult)
                    p.tt(scr2[:, 0:T], scr[:, 0:T], xt[:, mc * T:(mc + 1) * T],
                         ALU.add)
                    p.dma(fm_slice_one(o_out, t0, T, mc), row3(scr2.ap()[:, 0:T], T),
                          eng="gpsimd")

        p.emit(ctxs)
    return p


SERIAL = False
DBG_SER = set()


def build_fused_v1():
    p = P(serial=SERIAL)
    x = p.din("x", [E, L])
    xe = p.din("xe", [3 * E, L])
    Wq = p.din("Wq", [E, E])
    Wv = p.din("Wv", [E, E])
    vecs = p.din("vecs", [128, 6 * EC])
    Wk = p.din("Wk", [E, E])
    Wpk = p.din("Wpk", [E, E])
    veck = p.din("veck", [128, 4 * EC])
    bpk_row = p.din("bpk_row", [1, E])
    w_gate = p.din("w_gate", [IF, E])
    w_up = p.din("w_up", [IF, E])
    clip_x = p.din("clip_x", [1, 1])
    Wpq = p.din("Wpq", [E, E])
    veca = p.din("veca", [128, 4 * EC])
    cwT = p.din("cwT", [5 * 64, E])
    gamma = p.din("gamma", [1, 1])
    W1 = p.din("W1", [E, MLP])
    b1 = p.din("b1", [128, MC])
    W2 = p.din("W2", [MLP, E])
    b2 = p.din("b2", [128, EC])
    w_down = p.din("w_down", [E, IF])
    clip_down = p.din("clip_down", [1, 1])
    sel = p.din("sel", [EC * 16, 128])
    nonce4 = p.din("nonce4", [4, 2])
    nonce_r = p.din("nonce_r", [1, 16])

    o_out = p.dout("out", [E, L])

    DBG = False
    dz = p.dout if DBG else p.dtmp
    stats = dz("stats", [2, L])
    stats_e = p.dtmp("stats_e", [2, 3 * L])
    Qc = dz("Qc", [E, L], BF16)
    Vc = dz("Vc", [E, L], BF16)
    qg_d = dz("qg_d", [128, EC])
    Kc = dz("Kc", [E, L], BF16)
    scores_d = dz("scores_d", [16, L])
    h2c = dz("h2c", [IF, L], BF16)
    kv_d = dz("kv_d", [128, H * 32])
    ks_d = dz("ks_d", [16, E])
    attn2c = dz("attn2c", [E, L], BF16)
    h1c = dz("h1c", [MLP, L], BF16)
    s_c = dz("s_c", [E, L], BF16)
    bnc = p.dtmp("bnc", [128, 1])
    bnc16 = p.dtmp("bnc16", [16, 1])
    bnc4a = p.dtmp("bnc4a", [4, 16])
    bnc4b = p.dtmp("bnc4b", [4, 16])
    sh1 = p.nc.dram_tensor("sh1", [8, 16], F32, kind="Internal",
                           addr_space="Shared").ap()
    sh2 = p.nc.dram_tensor("sh2", [8, 16], F32, kind="Internal",
                           addr_space="Shared").ap()

    ctxs = ExitStack()
    with ctxs:
        sb = lambda n, s, d=F32: ctxs.enter_context(p.sbuf(n, s, d))

        ones1 = sb("ones1", [128, 1], BF16)
        ones128 = sb("ones128", [1, 128], F32)
        ident_b = sb("ident_b", [128, 128], BF16)
        strow = sb("strow", [1, 5 * T])
        rrow = sb("rrow", [1, 128])
        mrow = sb("mrow", [1, 16])
        rtmp = sb("rtmp", [128, 1])
        w4 = sb("w4", [4, 16])
        nc_sb = sb("nc_sb", [1, 16])
        sh_rd = sb("sh_rd", [1, 128])
        sh_msk = sb("sh_msk", [1, 16], I32)
        sh_cnd = sb("sh_cnd", [1, 16])

        mu = strow[0:1, 0:T]
        rs = strow[0:1, T:2 * T]
        scr1 = strow[0:1, 2 * T:3 * T]
        mu2 = strow[0:1, 3 * T:4 * T]
        rs2 = strow[0:1, 4 * T:5 * T]

        p.memset(ones1[:], 1.0)
        p.memset(ones1f[:], 1.0)
        p.memset(ones128[:], 1.0)
        p.identity_fill(ident_b[:], 128)
        p.dma(nc_sb[:], nonce_r)

        with ExitStack() as ph:
            sba = lambda n, s, d=F32: ph.enter_context(p.sbuf(n, s, d))
            psa = lambda n, s, d=F32: ph.enter_context(p.psum(n, s, d))
            stage = sba("stage", [128, E])
            wq_bf = sba("wq_bf", [128, EC * E], BF16)
            wv_bf = sba("wv_bf", [128, EC * E], BF16)
            vec = sba("vec", [128, 6 * EC])
            qsum = sba("qsum", [128, EC])
            vmax = sba("vmax", [128, 1])
            P1B = []
            for _i in range(2):
                P1B.append(dict(
                    xt=sba(f"xtP1{_i}", [128, EC * T]),
                    scr=sba(f"scrP1{_i}", [128, T]),
                    xn_f=sba(f"xn_f{_i}", [128, T]),
                    xn_bf=sba(f"xn_bf{_i}", [128, EC * T], BF16),
                    sq_bf=sba(f"sq_bf{_i}", [128, EC * T], BF16),
                    q_bf=sba(f"q_bf{_i}", [128, EC * T], BF16),
                    out_bf=sba(f"out_bf{_i}", [128, EC * T], BF16),
                    mu_sb=sba(f"mu_sb{_i}", [128, T]),
                    rs_sb=sba(f"rs_sb{_i}", [128, T]),
                    rtmp=sba(f"rtmpP1{_i}", [128, 1]),
                    strow=sba(f"strowP1{_i}", [1, 5 * T]),
                    ps_s1=psa(f"ps_s1{_i}", [1, T]),
                    ps_s2=psa(f"ps_s2{_i}", [1, T]),
                    ps_mu=psa(f"ps_mu{_i}", [128, T]),
                    ps_p=psa(f"ps_p{_i}", [128, T]),
                ))

            p.memset(qsum[:], 0.0)
            p.memset(vmax[:], 0.0)
            p.memset(mrow[:], 0.0)
            p.dma(vec[:], vecs)
            load_weight_bf16(p, wq_bf, Wq, E, E, stage)
            load_weight_bf16(p, wv_bf, Wv, E, E, stage)

            for ti in range(NT):
                bb = P1B[ti % 2]
                xt, scr = bb["xt"], bb["scr"]
                xn_f, xn_bf = bb["xn_f"], bb["xn_bf"]
                sq_bf, q_bf, out_bf = bb["sq_bf"], bb["q_bf"], bb["out_bf"]
                mu_sb, rs_sb, rtmp_ = bb["mu_sb"], bb["rs_sb"], bb["rtmp"]
                ps_s1, ps_s2 = bb["ps_s1"], bb["ps_s2"]
                ps_mu, ps_p = bb["ps_mu"], bb["ps_p"]
                st_ = bb["strow"]
                mu_, rs_ = st_[0:1, 0:T], st_[0:1, T:2 * T]
                scr1_ = st_[0:1, 2 * T:3 * T]
                mu2_, rs2_ = st_[0:1, 3 * T:4 * T], st_[0:1, 4 * T:5 * T]
                t0 = ti * T
                p.dma(tile_ap(xt, EC, T), fm_slice(x, t0, T))
                p.copy(xn_bf[:], xt[:])
                ln_stats(p, xn_bf, sq_bf, ones1, ps_s1, ps_s2, T, mu_, rs_, scr1_)
                p.dma(stats[0:1, t0:t0 + T], mu_)
                p.dma(stats[1:2, t0:t0 + T], rs_)
                p.mm(ps_mu[:, 0:T], ones128, mu_)
                p.act(mu_sb[:, 0:T], ps_mu[:, 0:T], AF.Copy)
                p.mm(ps_mu[:, 0:T], ones128, rs_)
                p.act(rs_sb[:, 0:T], ps_mu[:, 0:T], AF.Copy)
                for c in range(EC):
                    xn_chunk(p, xn_f, xt, mu_sb, rs_sb, c, T, eng="gpsimd")
                    p.red(rtmp_, xn_f[:, 0:T], ALU.max, absval=True)
                    p.tt(vmax, vmax, rtmp_, ALU.max)
                    p.copy(xn_bf[:, c * T:(c + 1) * T], xn_f[:, 0:T], eng="gpsimd")
                for (w_bf, b0, hbm, do_qsum) in [(wq_bf, 0, Qc, True),
                                                 (wv_bf, 3, Vc, False)]:
                    for mc in range(EC):
                        psx = ps_p if mc % 2 == 0 else ps_s1
                        psx = ps_p
                        for kc in range(EC):
                            p.mm(psx[:, 0:T],
                                 w_bf[:, kc * E + mc * 128: kc * E + mc * 128 + 128],
                                 xn_bf[:, kc * T:(kc + 1) * T],
                                 start=(kc == 0), stop=(kc == EC - 1))
                        p.act(q_bf[:, mc * T:(mc + 1) * T], psx[:, 0:T], AF.Identity,
                              bias=vec[:, b0 * EC + mc: b0 * EC + mc + 1])
                    ln_stats(p, q_bf, sq_bf, ones1, ps_s1, ps_s2, T, mu2_, rs2_, scr1_)
                    p.mm(ps_mu[:, 0:T], ones128, mu2_)
                    p.act(mu_sb[:, 0:T], ps_mu[:, 0:T], AF.Copy)
                    p.mm(ps_mu[:, 0:T], ones128, rs2_)
                    p.act(rs_sb[:, 0:T], ps_mu[:, 0:T], AF.Copy)
                    apply_ln(p, out_bf, q_bf, mu_sb, rs_sb, scr, T,
                             g_vec=vec[:, (b0 + 1) * EC:(b0 + 2) * EC],
                             b_vec=vec[:, (b0 + 2) * EC:(b0 + 3) * EC],
                             eng="gpsimd")
                    p.dma(fm_slice(hbm, t0, T), tile_ap(out_bf, EC, T), eng="gpsimd")
                    if do_qsum:
                        for c in range(EC):
                            p.red(rtmp_, out_bf[:, c * T:(c + 1) * T], ALU.add)
                            p.tt(qsum[:, c:c + 1], qsum[:, c:c + 1], rtmp_, ALU.add)

            p.ts(qsum[:], qsum[:], 1.0 / (L * np.sqrt(D)), None, ALU.mult)
            p.dma(qg_d, qsum[:])
            cross_part_max(p, vmax, bnc, rrow, mrow[0:1, 0:1])
            p.dma(bnc4a[0:1, 0:1], mrow[0:1, 0:1])

            for eb in range(3):
                p.memset(vmax[:], 0.0, eng="vector")
                for ti in range(NT):
                    bb = P1B[ti % 2]
                    xt, xn_f, xn_bf = bb["xt"], bb["xn_f"], bb["xn_bf"]
                    sq_bf, mu_sb, rs_sb = bb["sq_bf"], bb["mu_sb"], bb["rs_sb"]
                    rtmp_ = bb["rtmp"]
                    ps_s1, ps_s2, ps_mu = bb["ps_s1"], bb["ps_s2"], bb["ps_mu"]
                    st_ = bb["strow"]
                    mu_, rs_ = st_[0:1, 0:T], st_[0:1, T:2 * T]
                    scr1_ = st_[0:1, 2 * T:3 * T]
                    t0 = ti * T
                    p.dma(tile_ap(xt, EC, T),
                          xe.rearrange("(c p) l -> p c l", p=128)
                          [:, eb * EC:(eb + 1) * EC, t0:t0 + T])
                    p.copy(xn_bf[:], xt[:])
                    ln_stats(p, xn_bf, sq_bf, ones1, ps_s1, ps_s2, T, mu_, rs_, scr1_)
                    p.dma(stats_e[0:1, eb * L + t0: eb * L + t0 + T], mu_)
                    p.dma(stats_e[1:2, eb * L + t0: eb * L + t0 + T], rs_)
                    p.mm(ps_mu[:, 0:T], ones128, mu_)
                    p.act(mu_sb[:, 0:T], ps_mu[:, 0:T], AF.Copy)
                    p.mm(ps_mu[:, 0:T], ones128, rs_)
                    p.act(rs_sb[:, 0:T], ps_mu[:, 0:T], AF.Copy)
                    for c in range(EC):
                        xn_chunk(p, xn_f, xt, mu_sb, rs_sb, c, T, eng="gpsimd")
                        p.red(rtmp_, xn_f[:, 0:T], ALU.max, absval=True)
                        p.tt(vmax, vmax, rtmp_, ALU.max)
                cross_part_max(p, vmax, bnc, rrow, mrow[0:1, 0:1])
                p.dma(bnc4a[1 + eb:2 + eb, 0:1], mrow[0:1, 0:1])

        p.barrier()
        emit_sh_payload(p, bnc4a, nonce4, w4, sh1)
        mW1 = p.mark()

        p.set_serial(True)
        with ExitStack() as ph:
            sbb = lambda n, s, d=F32: ph.enter_context(p.sbuf(n, s, d))
            psb = lambda n, s, d=F32: ph.enter_context(p.psum(n, s, d))
            stage = sbb("stage", [128, E])
            wk_bf = sbb("wk_bf", [128, EC * E], BF16)
            vec = sbb("vec", [128, 4 * EC])
            qsl = sbb("qsl", [128, EC * 16], BF16)
            qg_sb = sbb("qg_sb", [128, EC])
            BB = []
            for _i in range(2):
                BB.append(dict(
                    xt=sbb(f"xtB{_i}", [128, EC * T]),
                    scr=sbb(f"scrB{_i}", [128, T]),
                    xn_bf=sbb(f"xn_bfB{_i}", [128, EC * T], BF16),
                    k_bf=sbb(f"k_bfB{_i}", [128, EC * T], BF16),
                    sq_bf=sbb(f"sq_bfB{_i}", [128, EC * T], BF16),
                    out_bf=sbb(f"out_bfB{_i}", [128, EC * T], BF16),
                    scb=sbb(f"scbB{_i}", [16, T]),
                    mu_sb=sbb(f"mu_sbB{_i}", [128, T]),
                    rs_sb=sbb(f"rs_sbB{_i}", [128, T]),
                    strow=sbb(f"strowB{_i}", [1, 5 * T]),
                    ps_mu=psb(f"ps_muB{_i}", [128, T]),
                    ps_p=psb(f"ps_pB{_i}", [128, T]),
                    ps_sc=psb(f"ps_scB{_i}", [16, T]),
                ))
            ps_s1 = psb("ps_s1B", [1, T])
            ps_s2 = psb("ps_s2B", [1, T])

            p.dma(vec[:], veck)
            load_weight_bf16(p, wk_bf, Wk, E, E, stage)
            p.dma(qg_sb[:], qg_d)
            p.memset(qsl[:], 0.0)
            for c in range(EC):
                p.copy(qsl[0:64, c * 16 + 2 * c: c * 16 + 2 * c + 1],
                       qg_sb[0:64, c:c + 1])
                p.copy(qsl[64:128, c * 16 + 2 * c + 1: c * 16 + 2 * c + 2],
                       qg_sb[64:128, c:c + 1])
            p.set_serial(False)

            for ti in range(NT):
                bb = BB[ti % 2]
                xt, scr = bb["xt"], bb["scr"]
                xn_bf, k_bf = bb["xn_bf"], bb["k_bf"]
                sq_bf, out_bf, scb = bb["sq_bf"], bb["out_bf"], bb["scb"]
                mu_sb, rs_sb = bb["mu_sb"], bb["rs_sb"]
                ps_mu, ps_p = bb["ps_mu"], bb["ps_p"]
                st_ = bb["strow"]
                scr1_ = st_[0:1, 2 * T:3 * T]
                mu2_, rs2_ = st_[0:1, 3 * T:4 * T], st_[0:1, 4 * T:5 * T]
                t0 = ti * T
                p.dma(tile_ap(xt, EC, T), fm_slice(x, t0, T))
                p.dma(st_[0:1, 0:T], stats[0:1, t0:t0 + T])
                p.dma(st_[0:1, T:2 * T], stats[1:2, t0:t0 + T])
                p.mm(ps_mu[:, 0:T], ones128, st_[0:1, 0:T])
                p.act(mu_sb[:, 0:T], ps_mu[:, 0:T], AF.Copy)
                p.mm(ps_mu[:, 0:T], ones128, st_[0:1, T:2 * T])
                p.act(rs_sb[:, 0:T], ps_mu[:, 0:T], AF.Copy)
                for c in range(EC):
                    xn_chunk(p, scr, xt, mu_sb, rs_sb, c, T, eng="gpsimd")
                    p.copy(xn_bf[:, c * T:(c + 1) * T], scr[:, 0:T], eng="gpsimd")
                for mc in range(EC):
                    for kc in range(EC):
                        p.mm(ps_p[:, 0:T],
                             wk_bf[:, kc * E + mc * 128: kc * E + mc * 128 + 128],
                             xn_bf[:, kc * T:(kc + 1) * T],
                             start=(kc == 0), stop=(kc == EC - 1))
                    p.act(k_bf[:, mc * T:(mc + 1) * T], ps_p[:, 0:T], AF.Identity,
                          bias=vec[:, mc:mc + 1])
                ln_stats(p, k_bf, sq_bf, ones1, ps_s1, ps_s2, T, mu2_, rs2_, scr1_)
                p.mm(ps_mu[:, 0:T], ones128, mu2_)
                p.act(mu_sb[:, 0:T], ps_mu[:, 0:T], AF.Copy)
                p.mm(ps_mu[:, 0:T], ones128, rs2_)
                p.act(rs_sb[:, 0:T], ps_mu[:, 0:T], AF.Copy)
                apply_ln(p, out_bf, k_bf, mu_sb, rs_sb, scr, T,
                         g_vec=vec[:, EC:2 * EC], b_vec=vec[:, 2 * EC:3 * EC],
                         eng="gpsimd")
                p.dma(fm_slice(Kc, t0, T), tile_ap(out_bf, EC, T), eng="gpsimd")
                ps_sc = bb["ps_sc"]
                for kc in range(EC):
                    p.mm(ps_sc[0:16, 0:T], qsl[:, kc * 16:(kc + 1) * 16],
                         out_bf[:, kc * T:(kc + 1) * T],
                         start=(kc == 0), stop=(kc == EC - 1))
                p.copy(scb[0:16, 0:T], ps_sc[0:16, 0:T])
                p.dma(scores_d[0:16, t0:t0 + T], scb[0:16, 0:T])
        mB = p.mark()
        p.set_serial(False)
        p.barrier()

        with ExitStack() as ph:
            sba = lambda n, s, d=F32: ph.enter_context(p.sbuf(n, s, d))
            psa = lambda n, s, d=F32: ph.enter_context(p.psum(n, s, d))
            stage = sba("stage", [128, E])
            wgT = sba("wgT", [128, IC * EC * 128], BF16)
            wuT = sba("wuT", [128, IC * EC * 128], BF16)
            sg = sba("sg", [128, IC])
            su = sba("su", [128, IC])
            rm = sba("rm", [128, 1])
            hmax = sba("hmax", [128, 1])
            sdr = sba("sdr", [1, 4])
            rsx_pp = sba("rsx_pp", [128, 1])
            sx_pp = sba("sx_pp", [128, 1])
            gm1 = sba("gm1", [1, 1])
            ps_4 = psa("ps_4", [128, 4])
            psq = psa("psq", [128, 128], BF16)

            p.dep_on(mW1, mB)
            emit_sh_read(p, sh1, nc_sb, sh_rd, sh_msk, sh_cnd, gm1[0:1, 0:1])
            make_scale_g(p, gm1[0:1, 0:1], clip_x, sdr, ones128, ps_4,
                         rsx_pp, sx_pp)

            with ExitStack() as phw:
                sbw = lambda n, s, d=F32: phw.enter_context(p.sbuf(n, s, d))
                yq = sbw("yqw", [128, E])
                i1 = sbw("i1w", [128, E], I32)
                i2 = sbw("i2w", [128, E], I32)
                tfq = sbw("tfqw", [128, E])
                wi_b = sbw("wi_bw", [128, E], BF16)
                for (w_hbm, wT, sc) in [(w_gate, wgT, sg), (w_up, wuT, su)]:
                    for ic in range(IC):
                        p.dma(stage[:, 0:E], w_hbm[ic * 128:(ic + 1) * 128, :])
                        p.red(rm, stage[:, 0:E], ALU.max, absval=True)
                        p.ts(sc[:, ic:ic + 1], rm, 1.0 / 127.0, None, ALU.mult)
                        p.recip(rm, rm)
                        p.ts(rm, rm, 127.0, None, ALU.mult)
                        p.act(yq[:, 0:E], stage[:, 0:E], AF.Copy, bias=256.5, scale=rm)
                        quant_round(p, yq, i1, i2, tfq, wi_b, E)
                        for kc in range(EC):
                            p.trans(psq[:, 0:128], wi_b[:, kc * 128:(kc + 1) * 128], ident_b[:])
                            p.copy(wT[:, (ic * EC + kc) * 128:(ic * EC + kc) * 128 + 128],
                                   psq[:, 0:128])
                    for ic in range(IC):
                        p.tt(sc[:, ic:ic + 1], sc[:, ic:ic + 1], sx_pp[:, 0:1], ALU.mult)
            p.barrier()
            AB = []
            for _i in range(2):
                AB.append(dict(
                    xt=sba(f"xtA{_i}", [128, EC * T]),
                    scr=sba(f"scrA{_i}", [128, T]),
                    srow=sba(f"srowA{_i}", [1, 2 * T]),
                    xq_bf=sba(f"xq_bf{_i}", [128, EC * T], BF16),
                    yq=sba(f"yqA{_i}", [128, T]),
                    i1=sba(f"i1A{_i}", [128, T], I32),
                    i2=sba(f"i2A{_i}", [128, T], I32),
                    tfq=sba(f"tfqA{_i}", [128, T]),
                    gs=sba(f"gsA{_i}", [128, T]),
                    us=sba(f"usA{_i}", [128, T]),
                    h2f=sba(f"h2fA{_i}", [128, T]),
                    h2b=sba(f"h2bA{_i}", [128, T], BF16),
                    mu_sb=sba(f"mu_sbA{_i}", [128, T]),
                    rs_sb=sba(f"rs_sbA{_i}", [128, T]),
                    ps_mu=psa(f"ps_muA{_i}", [128, T]),
                    ps_g=psa(f"ps_gA{_i}", [128, T]),
                    ps_g2=psa(f"ps_g2A{_i}", [128, T]),
                ))

            def ffn_batch(src_hbm, c0, st_hbm, st_off, store_h2, base_ti=0):
                for ti in range(NT):
                    bb = AB[(base_ti + ti) % 2]
                    t0 = ti * T
                    xt_, scr_ = bb["xt"], bb["scr"]
                    p.dma(tile_ap(xt_, EC, T),
                          src_hbm.rearrange("(c p) l -> p c l", p=128)
                          [:, c0:c0 + EC, t0:t0 + T])
                    load_stats(p, st_hbm, bb["srow"], ones128, bb["ps_mu"],
                               bb["ps_g2"], st_off + t0, T)
                    bcast_sb(p, bb["ps_mu"], bb["ps_g2"], bb["mu_sb"],
                             bb["rs_sb"], T)
                    for c in range(EC):
                        xn_chunk(p, scr_, xt_, bb["mu_sb"], bb["rs_sb"], c, T,
                                 eng="gpsimd")
                        p.act(bb["yq"][:, 0:T], scr_[:, 0:T], AF.Copy, bias=256.5,
                              scale=rsx_pp[:, 0:1])
                        quant_round(p, bb["yq"], bb["i1"], bb["i2"], bb["tfq"],
                                    bb["h2b"], T)
                        p.copy(bb["xq_bf"][:, c * T:(c + 1) * T],
                               bb["h2b"][:, 0:T])
                    for ic in range(IC):
                        for kc in range(EC):
                            p.mm(bb["ps_g"][:, 0:T],
                                 wgT[:, (ic * EC + kc) * 128:(ic * EC + kc) * 128 + 128],
                                 bb["xq_bf"][:, kc * T:(kc + 1) * T],
                                 start=(kc == 0), stop=(kc == EC - 1))
                        p.act(bb["gs"][:, 0:T], bb["ps_g"][:, 0:T], AF.Silu,
                              scale=sg[:, ic:ic + 1])
                        for kc in range(EC):
                            p.mm(bb["ps_g2"][:, 0:T],
                                 wuT[:, (ic * EC + kc) * 128:(ic * EC + kc) * 128 + 128],
                                 bb["xq_bf"][:, kc * T:(kc + 1) * T],
                                 start=(kc == 0), stop=(kc == EC - 1))
                        p.act(bb["us"][:, 0:T], bb["ps_g2"][:, 0:T], AF.Identity,
                              scale=su[:, ic:ic + 1])
                        p.tt(bb["h2f"][:, 0:T], bb["gs"][:, 0:T],
                             bb["us"][:, 0:T], ALU.mult, eng="gpsimd")
                        if store_h2:
                            p.copy(bb["h2b"][:, 0:T], bb["h2f"][:, 0:T],
                                   eng="gpsimd")
                            p.dma(fm_slice_one(h2c, t0, T, ic),
                                  row3(bb["h2b"].ap()[:, 0:T], T), eng="gpsimd")
                        p.red(bb["yq"][:, 0:1].bitcast(F32) if False else rtmp,
                              bb["h2f"][:, 0:T], ALU.max, absval=True)
                        p.tt(hmax, hmax, rtmp, ALU.max)

            p.memset(hmax[:], 0.0, eng="vector")
            ffn_batch(x, 0, stats, 0, True, base_ti=0)
            cross_part_max(p, hmax, bnc, rrow, mrow[0:1, 0:1])
            p.dma(bnc4b[0:1, 0:1], mrow[0:1, 0:1])
            for eb in range(3):
                p.memset(hmax[:], 0.0, eng="vector")
                ffn_batch(xe, eb * EC, stats_e, eb * L, False, base_ti=eb + 1)
                cross_part_max(p, hmax, bnc, rrow, mrow[0:1, 0:1])
                p.dma(bnc4b[1 + eb:2 + eb, 0:1], mrow[0:1, 0:1])

        p.barrier()
        emit_sh_payload(p, bnc4b, nonce4, w4, sh2)
        mW2 = p.mark()

        p.set_serial(True)
        with ExitStack() as ph:
            sbc = lambda n, s, d=F32: ph.enter_context(p.sbuf(n, s, d))
            psc = lambda n, s, d=F32: ph.enter_context(p.psum(n, s, d))
            stage = sbc("stage", [128, E])
            ident_f = sbc("ident_f", [128, 128], F32)
            scores = sbc("scores", [16, L])
            srow16 = sbc("srow16", [16, 4])
            a_tm_f = sbc("a_tm_f", [128, (L // 128) * 16])
            a_tm_b = sbc("a_tm_b", [128, (L // 128) * 16], BF16)
            a0r_b = sbc("a0r_b", [1, 16], BF16)
            scr = sbc("scrP3", [128, T])
            pk0_b = sbc("pk0_b", [1, E], BF16)
            pk0_f = sbc("pk0_f", [1, E])
            wpk_bf = sbc("wpk_bf", [128, EC * E], BF16)
            vec = sbc("vec2", [128, 4 * EC])
            k_bf = sbc("k_bf2", [128, EC * T], BF16)
            v_bf = sbc("v_bf", [128, EC * T], BF16)
            pk_bf = sbc("pk_bf", [128, EC * T], BF16)
            pkt_b = sbc("pkt_b", [128, E], BF16)
            vst_b = sbc("vst_b", [128, E], BF16)
            kvout = sbc("kvout", [128, H * 32])
            ksout = sbc("ksout", [16, E])

            p.identity_fill(ident_f[:], 128)
            p.dma(scores[0:16, :], scores_d)
            p.red(srow16[0:16, 0:1], scores[:], ALU.max)
            p.ts(srow16[0:16, 0:1], srow16[0:16, 0:1], 0.0, None, ALU.max)
            p.ts(srow16[0:16, 1:2], srow16[0:16, 0:1], -1.0, None, ALU.mult)
            p.act(scores[:], scores[:], AF.Exp, bias=srow16[0:16, 1:2])
            p.red(srow16[0:16, 2:3], scores[:], ALU.add)
            p.act(srow16[0:16, 3:4], srow16[0:16, 1:2], AF.Exp)
            p.tt(srow16[0:16, 2:3], srow16[0:16, 2:3], srow16[0:16, 3:4], ALU.add)
            p.recip(srow16[0:16, 2:3], srow16[0:16, 2:3])
            p.ts(srow16[0:16, 2:3], srow16[0:16, 2:3], float(L), None, ALU.mult)
            p.ts(scores[:], scores[:], srow16[0:16, 2:3], None, ALU.mult)
            p.tt(srow16[0:16, 3:4], srow16[0:16, 3:4], srow16[0:16, 2:3], ALU.mult)
            p.dma(bnc16, srow16[0:16, 3:4])
            p.dma(rrow[0:1, 0:16], bnc16.rearrange("p c -> c p"))
            p.copy(a0r_b[0:1, 0:16], rrow[0:1, 0:16])
            with ExitStack() as phs:
                ptr_f = phs.enter_context(p.psum("ptr_f", [128, 16], F32))
                for tc in range(L // 128):
                    p.trans(ptr_f[:, 0:16], scores[0:16, tc * 128:(tc + 1) * 128],
                            ident_f[0:16, 0:16])
                    p.copy(a_tm_f[:, tc * 16:(tc + 1) * 16], ptr_f[:, 0:16])
            p.barrier()
            p.copy(a_tm_b[:], a_tm_f[:])
            p.dma(pk0_f[0:1, 0:E], bpk_row)
            p.act(pk0_f[0:1, 0:E], pk0_f[0:1, 0:E], AF.Tanh)
            p.ts(pk0_b[0:1, 0:E], pk0_f[0:1, 0:E], 1.0, None, ALU.add)

            ps_p = psc("ps_p3", [128, T])
            ps_pB = psc("ps_p3B", [128, T])
            ptr1 = psc("ptr1", [128, 1024], BF16)
            ptr2 = psc("ptr2", [128, 1024], BF16)
            ps_kv = psc("ps_kv", [128, H * 32])
            ps_ks = psc("ps_ks", [16, E])

            p.dma(vec[:], veck)
            load_weight_bf16(p, wpk_bf, Wpk, E, E, stage)
            p.memset(ps_kv[:], 0.0, eng="vector")
            p.memset(ps_ks[0:16, :], 0.0, eng="vector")
            for c in range(EC):
                p.mm(ps_ks[0:16, c * 128:(c + 1) * 128], a0r_b[0:1, 0:16],
                     pk0_b[0:1, c * 128:(c + 1) * 128], start=False, stop=False)
            p.set_serial(False)

            for ti in range(NT):
                t0 = ti * T
                p.dma(tile_ap(k_bf, EC, T), fm_slice(Kc, t0, T))
                p.dma(tile_ap(v_bf, EC, T), fm_slice(Vc, t0, T))
                for mc in range(EC):
                    psx = ps_p if mc % 2 == 0 else ps_pB
                    for kc in range(EC):
                        p.mm(psx[:, 0:T],
                             wpk_bf[:, kc * E + mc * 128: kc * E + mc * 128 + 128],
                             k_bf[:, kc * T:(kc + 1) * T],
                             start=(kc == 0), stop=(kc == EC - 1))
                    p.act(scr[:, 0:T], psx[:, 0:T], AF.Tanh,
                          bias=vec[:, 3 * EC + mc:3 * EC + mc + 1])
                    p.ts(pk_bf[:, mc * T:(mc + 1) * T], scr[:, 0:T], 1.0, None,
                         ALU.add, eng="gpsimd")
                for tc in range(TC):
                    gtc = ti * TC + tc
                    last = (ti == NT - 1 and tc == TC - 1)
                    for c in range(EC):
                        p.trans(ptr1[:, 0:128],
                                pk_bf[:, c * T + tc * 128: c * T + tc * 128 + 128],
                                ident_b[:])
                        p.act(pkt_b[:, c * 128:(c + 1) * 128], ptr1[:, 0:128],
                              AF.Copy)
                        p.trans(ptr2[:, 0:128],
                                v_bf[:, c * T + tc * 128: c * T + tc * 128 + 128],
                                ident_b[:])
                        p.ts(vst_b[:, c * 128:c * 128 + 64], ptr2[:, 0:64],
                             a_tm_f[:, gtc * 16 + 2 * c: gtc * 16 + 2 * c + 1],
                             None, ALU.mult)
                        p.ts(vst_b[:, c * 128 + 64:c * 128 + 128], ptr2[:, 64:128],
                             a_tm_f[:, gtc * 16 + 2 * c + 1: gtc * 16 + 2 * c + 2],
                             None, ALU.mult)
                    for c in range(EC):
                        p.mm(ps_ks[0:16, c * 128:(c + 1) * 128],
                             a_tm_b[:, gtc * 16:(gtc + 1) * 16],
                             pkt_b[:, c * 128:(c + 1) * 128], start=False, stop=last)
                    for h in range(H):
                        c, r = h // 2, h % 2
                        p.mm(ps_kv[r * 64:(r + 1) * 64, c * 64:(c + 1) * 64],
                             pkt_b[:, h * 64:(h + 1) * 64],
                             vst_b[:, h * 64:(h + 1) * 64],
                             start=False, stop=last)

            p.copy(kvout[:], ps_kv[:])
            p.dma(kv_d, kvout[:])
            p.copy(ksout[0:16, :], ps_ks[0:16, :])
            p.dma(ks_d, ksout[0:16, :])

        p.barrier()
        p.set_serial(True)
        with ExitStack() as ph:
            sba = lambda n, s, d=F32: ph.enter_context(p.sbuf(n, s, d))
            psa = lambda n, s, d=F32: ph.enter_context(p.psum(n, s, d))
            stage = sba("stage", [128, E])
            vec = sba("vec", [128, 4 * EC])
            wpq_bf = sba("wpq_bf", [128, EC * E], BF16)
            numT = sba("numT", [128, EC * 128], BF16)
            denT = sba("denT", [128, EC * 16], BF16)
            sel_b = sba("sel_b", [16, EC * 128], BF16)
            cvT = sba("cvT", [128, EC * 5 * 128], BF16)
            kv_f = sba("kv_f", [128, H * 32])
            kv_b = sba("kv_b", [128, H * 32], BF16)
            ksf = sba("ksf", [16, E])
            ks_b = sba("ks_b", [16, E], BF16)
            scr = sba("scrP4", [128, T])
            selt = sba("selt", [128, 128])
            q_bf = sba("q_bf", [128, EC * T], BF16)
            sq_bf = sba("sq_bf", [128, EC * T], BF16)
            pq_bf = sba("pq_bf", [128, EC * T], BF16)
            xn_f = sba("xn_f", [128, EC * T])
            attn_a = sba("attn_a", [128, EC * T], BF16)
            attn_b_ = sba("attn_b", [128, EC * T], BF16)
            tail2 = sba("tail2", [128, EC * 2], BF16)
            conv_win = sba("conv_win", [128, EC * (T + 4)], BF16)
            attn2_bf = sba("attn2_bf", [128, EC * T], BF16)
            rden = sba("rden", [16, T])
            rden_b = sba("rden_b", [16, T], BF16)
            rdb = sba("rdb", [128, T])
            gam_pp = sba("gam_pp", [128, 1])
            gcb = sba("gcb", [128, EC])
            mu_sb = sba("mu_sbP", [128, T])
            rs_sb = sba("rs_sbP", [128, T])
            ps_p = psa("ps_p", [128, T])
            ps_dr = psa("ps_dr", [128, T])
            ps_a = psa("ps_a", [128, T])
            ps_pB = psa("ps_pB", [128, T])

            p.dma(vec[:], veca)
            load_weight_bf16(p, wpq_bf, Wpq, E, E, stage)
            for c in range(EC):
                p.dma(selt[0:16, 0:128], sel[c * 16:(c + 1) * 16, :])
                p.copy(sel_b[0:16, c * 128:(c + 1) * 128], selt[0:16, 0:128])
            p.memset(cvT[:], 0.0)
            for c in range(EC):
                for k in range(5):
                    col = (c * 5 + k) * 128
                    p.dma(selt[0:64, 0:64],
                          cwT[k * 64:(k + 1) * 64, 2 * c * 64:(2 * c + 1) * 64])
                    p.copy(cvT[0:64, col:col + 64], selt[0:64, 0:64])
                    p.dma(selt[64:128, 0:64],
                          cwT[k * 64:(k + 1) * 64, (2 * c + 1) * 64:(2 * c + 2) * 64])
                    p.copy(cvT[64:128, col + 64:col + 128], selt[64:128, 0:64])
            p.dma(kv_f[:], kv_d)
            p.copy(kv_b[:], kv_f[:])
            p.memset(numT[:], 0.0)
            for c in range(EC):
                p.copy(numT[0:64, c * 128: c * 128 + 64],
                       kv_b[0:64, c * 64:(c + 1) * 64])
                p.copy(numT[64:128, c * 128 + 64: c * 128 + 128],
                       kv_b[64:128, c * 64:(c + 1) * 64])
            p.dma(ksf[0:16, :], ks_d)
            p.copy(ks_b[0:16, :], ksf[0:16, :])
            p.memset(denT[:], 0.0)
            with ExitStack() as pht:
                ptr_b = pht.enter_context(p.psum("ptr_b", [128, 16], BF16))
                ps_t4 = pht.enter_context(p.psum("ps_t4", [128, 4], F32))
                p.dma(strow[0:1, 0:1], gamma)
                p.mm(ps_t4[:, 0:1], ones128, strow[0:1, 0:1])
                p.copy(gam_pp[:, 0:1], ps_t4[:, 0:1])
                p.ts(gcb[:], vec[:, 3 * EC:4 * EC], gam_pp[:, 0:1], None, ALU.mult)
                for c in range(EC):
                    p.trans(ptr_b[:, 0:16], ks_b[0:16, c * 128:(c + 1) * 128],
                            ident_b[0:16, 0:16])
                    p.copy(denT[0:64, c * 16 + 2 * c: c * 16 + 2 * c + 1],
                           ptr_b[0:64, 2 * c:2 * c + 1])
                    p.copy(denT[64:128, c * 16 + 2 * c + 1: c * 16 + 2 * c + 2],
                           ptr_b[64:128, 2 * c + 1:2 * c + 2])
            p.barrier()
            p.set_serial(False)
            ps_s1 = psa("ps_s1", [1, T])
            ps_s2 = psa("ps_s2", [1, T])
            ps_b = psa("ps_b", [128, T])

            def conv_ln_store(ti_prev, a_cur, a_prev):
                t0p = ti_prev * T
                for c in range(EC):
                    w0 = c * (T + 4)
                    if ti_prev == 0:
                        p.memset(conv_win[:, w0:w0 + 2], 0.0, eng="vector")
                    else:
                        p.copy(conv_win[:, w0:w0 + 2], tail2[:, c * 2:(c + 1) * 2])
                    p.copy(conv_win[:, w0 + 2:w0 + 2 + T],
                           a_prev[:, c * T:(c + 1) * T])
                    if ti_prev == NT - 1:
                        p.memset(conv_win[:, w0 + 2 + T:w0 + 4 + T], 0.0,
                                 eng="vector")
                    else:
                        p.copy(conv_win[:, w0 + 2 + T:w0 + 4 + T],
                               a_cur[:, c * T:c * T + 2])
                    p.copy(tail2[:, c * 2:(c + 1) * 2],
                           a_prev[:, (c + 1) * T - 2:(c + 1) * T])
                for c in range(EC):
                    psc_ = ps_a if c % 2 == 0 else ps_p
                    for k in range(5):
                        p.mm(psc_[:, 0:T],
                             cvT[:, (c * 5 + k) * 128:(c * 5 + k) * 128 + 128],
                             conv_win[:, c * (T + 4) + k: c * (T + 4) + k + T],
                             start=(k == 0), stop=(k == 4))
                    p.act(scr[:, 0:T], psc_[:, 0:T], AF.Identity,
                          bias=gcb[:, c:c + 1], scale=gam_pp[:, 0:1])
                    p.tt(xn_f[:, c * T:(c + 1) * T], scr[:, 0:T],
                         a_prev[:, c * T:(c + 1) * T], ALU.add)
                p.copy(sq_bf[:], xn_f[:], eng="gpsimd")
                ln_stats(p, sq_bf, q_bf, ones1, ps_s1, ps_s2, T, mu2, rs2, scr1)
                p.mm(ps_b[:, 0:T], ones128, mu2)
                p.act(mu_sb[:, 0:T], ps_b[:, 0:T], AF.Copy)
                p.mm(ps_b[:, 0:T], ones128, rs2)
                p.act(rs_sb[:, 0:T], ps_b[:, 0:T], AF.Copy)
                apply_ln(p, attn2_bf, sq_bf, mu_sb, rs_sb, scr, T,
                         g_vec=vec[:, EC:2 * EC], b_vec=vec[:, 2 * EC:3 * EC])
                p.dma(fm_slice(attn2c, t0p, T), tile_ap(attn2_bf, EC, T),
                      eng="gpsimd")

            for ti in range(NT):
                t0 = ti * T
                a_cur = attn_a if ti % 2 == 0 else attn_b_
                a_prev = attn_b_ if ti % 2 == 0 else attn_a
                p.dma(tile_ap(q_bf, EC, T), fm_slice(Qc, t0, T))
                for mc in range(EC):
                    psx = ps_p if mc % 2 == 0 else ps_pB
                    for kc in range(EC):
                        p.mm(psx[:, 0:T],
                             wpq_bf[:, kc * E + mc * 128: kc * E + mc * 128 + 128],
                             q_bf[:, kc * T:(kc + 1) * T],
                             start=(kc == 0), stop=(kc == EC - 1))
                    p.act(scr[:, 0:T], psx[:, 0:T], AF.Tanh, bias=vec[:, mc:mc + 1])
                    p.ts(pq_bf[:, mc * T:(mc + 1) * T], scr[:, 0:T], 1.0, None,
                         ALU.add, eng="gpsimd")
                for c in range(EC):
                    p.mm(ps_dr[0:16, 0:T], denT[:, c * 16:(c + 1) * 16],
                         pq_bf[:, c * T:(c + 1) * T],
                         start=(c == 0), stop=(c == EC - 1))
                p.ts(rden[0:16, 0:T], ps_dr[0:16, 0:T], DEN_EPS, None, ALU.add)
                p.recip(rden[0:16, 0:T], rden[0:16, 0:T])
                p.copy(rden_b[0:16, 0:T], rden[0:16, 0:T], eng="gpsimd")
                for c in range(EC):
                    psn = ps_a if c % 2 == 0 else ps_pB
                    p.mm(psn[:, 0:T], numT[:, c * 128:(c + 1) * 128],
                         pq_bf[:, c * T:(c + 1) * T])
                    p.mm(ps_dr[:, 0:T], sel_b[0:16, c * 128:(c + 1) * 128],
                         rden_b[0:16, 0:T])
                    p.act(rdb[:, 0:T], ps_dr[:, 0:T], AF.Copy)
                    p.tt(a_cur[:, c * T:(c + 1) * T], psn[:, 0:T], rdb[:, 0:T],
                         ALU.mult)
                if ti > 0:
                    conv_ln_store(ti - 1, a_cur, a_prev)
            last_buf = attn_b_ if (NT - 1) % 2 == 1 else attn_a
            conv_ln_store(NT - 1, last_buf, last_buf)

        p.barrier()
        with ExitStack() as ph:
            stage = ph.enter_context(p.sbuf("stageb", [128, MLP]))
            stage2 = ph.enter_context(p.sbuf("stageb2", [128, MLP]))
            w1_bf = ph.enter_context(p.sbuf("w1_bf", [128, EC * MLP], BF16))
            ps_p = ph.enter_context(p.psum("ps_pb", [128, T], F32))
            ps_p2 = ph.enter_context(p.psum("ps_pb2", [128, T], F32))
            h1_bf2 = ph.enter_context(p.sbuf("h1_bf2", [128, T], BF16))
            h1_bf = ph.enter_context(p.sbuf("h1_bf", [128, T], BF16))
            a2B = (ph.enter_context(p.sbuf("a2b_bf0", [128, EC * T], BF16)),
                   ph.enter_context(p.sbuf("a2b_bf1", [128, EC * T], BF16)))
            b1_sb = ph.enter_context(p.sbuf("b1_sb", [128, MC], F32))
            p.dma(b1_sb[:], b1)
            load_weight_bf16_2q(p, w1_bf, W1, E, MLP, stage, stage2)
            for ti in range(NT):
                t0 = ti * T
                a2_bf = a2B[ti % 2]
                p.dma(tile_ap(a2_bf, EC, T), fm_slice(attn2c, t0, T))
                for mc in range(MC):
                    psx = ps_p if mc % 2 == 0 else ps_p2
                    hbx = h1_bf if mc % 2 == 0 else h1_bf2
                    for kc in range(EC):
                        p.mm(psx[:, 0:T],
                             w1_bf[:, kc * MLP + mc * 128: kc * MLP + mc * 128 + 128],
                             a2_bf[:, kc * T:(kc + 1) * T],
                             start=(kc == 0), stop=(kc == EC - 1))
                    p.act(hbx[:, 0:T], psx[:, 0:T], AF.Gelu,
                          bias=b1_sb[:, mc:mc + 1])
                    p.dma(fm_slice_one(h1c, t0, T, mc), row3(hbx.ap()[:, 0:T], T),
                          eng="gpsimd")

        p.barrier()
        with ExitStack() as ph:
            stage = ph.enter_context(p.sbuf("stagec", [128, MLP]))
            stage2 = ph.enter_context(p.sbuf("stagec2", [128, MLP]))
            w2_bf = ph.enter_context(p.sbuf("w2_bf", [128, MC * E], BF16))
            ps_p = ph.enter_context(p.psum("ps_pc", [128, T], F32))
            ps_p2 = ph.enter_context(p.psum("ps_pc2", [128, T], F32))
            ps_mu = ph.enter_context(p.psum("ps_muc", [128, T], F32))
            ps_rs = ph.enter_context(p.psum("ps_rsc", [128, T], F32))
            mu_sb = ph.enter_context(p.sbuf("mu_sbC", [128, T], F32))
            rs_sb = ph.enter_context(p.sbuf("rs_sbC", [128, T], F32))
            xt = ph.enter_context(p.sbuf("xtC", [128, EC * T], F32))
            scr = ph.enter_context(p.sbuf("scrC", [128, T], F32))
            scr2 = ph.enter_context(p.sbuf("scr2C", [128, T], F32))
            h1B = (ph.enter_context(p.sbuf("h1t0", [128, MC * T], BF16)),
                   ph.enter_context(p.sbuf("h1t1", [128, MC * T], BF16)))
            b2_sb = ph.enter_context(p.sbuf("b2_sb", [128, EC], F32))
            s_bf = ph.enter_context(p.sbuf("s_bf", [128, EC * T], BF16))
            a2_bf = ph.enter_context(p.sbuf("a2c_bf", [128, EC * T], BF16))
            p.dma(b2_sb[:], b2)
            load_weight_bf16_2q(p, w2_bf, W2, MLP, E, stage, stage2)
            for ti in range(NT):
                t0 = ti * T
                h1t = h1B[ti % 2]
                p.dma(tile_ap(h1t, MC, T), fm_slice(h1c, t0, T, nchunk=MC))
                p.dma(tile_ap(a2_bf, EC, T), fm_slice(attn2c, t0, T))
                p.dma(tile_ap(xt, EC, T), fm_slice(x, t0, T))
                load_stats(p, stats, strow, ones128, ps_mu, ps_rs, t0, T)
                bcast_sb(p, ps_mu, ps_rs, mu_sb, rs_sb, T)
                for mc in range(EC):
                    psx = ps_p if mc % 2 == 0 else ps_p2
                    for kc in range(MC):
                        p.mm(psx[:, 0:T],
                             w2_bf[:, kc * E + mc * 128: kc * E + mc * 128 + 128],
                             h1t[:, kc * T:(kc + 1) * T],
                             start=(kc == 0), stop=(kc == MC - 1))
                    p.act(scr[:, 0:T], psx[:, 0:T], AF.Identity,
                          bias=b2_sb[:, mc:mc + 1])
                    p.tt(scr[:, 0:T], scr[:, 0:T], a2_bf[:, mc * T:(mc + 1) * T],
                         ALU.add, eng="gpsimd")
                    xn_chunk(p, scr2, xt, mu_sb, rs_sb, mc, T)
                    p.tt(scr[:, 0:T], scr[:, 0:T], scr2[:, 0:T], ALU.add,
                         eng="gpsimd")
                    p.copy(s_bf[:, mc * T:(mc + 1) * T], scr[:, 0:T], eng="gpsimd")
                p.dma(fm_slice(s_c, t0, T), tile_ap(s_bf, EC, T), eng="gpsimd")
        m4c = p.mark()
        p.barrier()

        p.set_serial(True)
        with ExitStack() as ph:
            sbd = lambda n, s, d=F32: ph.enter_context(p.sbuf(n, s, d))
            psd = lambda n, s, d=F32: ph.enter_context(p.psum(n, s, d))
            wdT = sbd("wdT", [128, EC * IC * 128], BF16)
            swd = sbd("swd", [128, EC])
            h2t = sbd("h2t", [128, IC * T], BF16)
            rsd_pp = sbd("rsd_pp", [128, 1])
            sd_pp = sbd("sd_pp", [128, 1])
            sdr = sbd("sdr", [1, 4])
            gm2 = sbd("gm2", [1, 1])
            ps_p = psd("ps_pd", [128, T])
            ps_p2 = psd("ps_pd2", [128, T])
            ps_4 = psd("ps_4d", [128, 4])
            psq = psd("psqd", [128, 128], BF16)

            p.dep_on(mW2, m4c)
            emit_sh_read(p, sh2, nc_sb, sh_rd, sh_msk, sh_cnd, gm2[0:1, 0:1])
            make_scale_g(p, gm2[0:1, 0:1], clip_down, sdr, ones128, ps_4,
                         rsd_pp, sd_pp)
            with ExitStack() as phw:
                sbw = lambda n, s, d=F32: phw.enter_context(p.sbuf(n, s, d))
                stage = sbw("staged", [128, IF])
                yq = sbw("yqd", [128, IF])
                i1 = sbw("i1d", [128, IF], I32)
                i2 = sbw("i2d", [128, IF], I32)
                tfq = sbw("tfqd", [128, IF])
                wi_b = sbw("wi_bd", [128, IF], BF16)
                rm = sbw("rmd", [128, 1])
                for hc in range(EC):
                    p.dma(stage[:, 0:IF], w_down[hc * 128:(hc + 1) * 128, :])
                    p.red(rm, stage[:, 0:IF], ALU.max, absval=True)
                    p.ts(swd[:, hc:hc + 1], rm, 1.0 / 127.0, None, ALU.mult)
                    p.recip(rm, rm)
                    p.ts(rm, rm, 127.0, None, ALU.mult)
                    p.act(yq[:, 0:IF], stage[:, 0:IF], AF.Copy, bias=256.5, scale=rm)
                    quant_round(p, yq, i1, i2, tfq, wi_b, IF)
                    for ic in range(IC):
                        p.trans(psq[:, 0:128], wi_b[:, ic * 128:(ic + 1) * 128],
                                ident_b[:])
                        p.copy(wdT[:, (hc * IC + ic) * 128:(hc * IC + ic) * 128 + 128],
                               psq[:, 0:128])
                    p.tt(swd[:, hc:hc + 1], swd[:, hc:hc + 1], sd_pp[:, 0:1], ALU.mult)
            p.barrier()
            DB = []
            for _i in range(2):
                DB.append(dict(
                    h2q=sbd(f"h2qD{_i}", [128, IC * T], BF16),
                    s_b=sbd(f"s_bD{_i}", [128, EC * T], BF16),
                    xt=sbd(f"xtD{_i}", [128, EC * T]),
                    yq=sbd(f"yqD{_i}", [128, T]),
                    i1=sbd(f"i1D{_i}", [128, T], I32),
                    i2=sbd(f"i2D{_i}", [128, T], I32),
                    tfq=sbd(f"tfD{_i}", [128, T]),
                    scr=sbd(f"scrD{_i}", [128, T]),
                    scr2=sbd(f"scr2D{_i}", [128, T]),
                ))
            p.set_serial(False)

            for ti in range(NT):
                t0 = ti * T
                db = DB[ti % 2]
                h2q, s_b, xt = db["h2q"], db["s_b"], db["xt"]
                p.dma(tile_ap(h2t, IC, T), fm_slice(h2c, t0, T, nchunk=IC))
                p.dma(tile_ap(s_b, EC, T), fm_slice(s_c, t0, T))
                p.dma(tile_ap(xt, EC, T), fm_slice(x, t0, T))
                for ic in range(IC):
                    qb = DB[ic % 2]
                    yq, i1, i2, tfq = qb["yq"], qb["i1"], qb["i2"], qb["tfq"]
                    p.act(yq[:, 0:T], h2t[:, ic * T:(ic + 1) * T], AF.Copy,
                          bias=256.5, scale=rsd_pp[:, 0:1])
                    p.copy(i1[:, 0:T], yq[:, 0:T])
                    p.stt(i2[:, 0:T], yq[:, 0:T], -0.5, i1[:, 0:T], ALU.add,
                          ALU.subtract)
                    p.stt(tfq[:, 0:T], i1[:, 0:T], -256.0, i2[:, 0:T], ALU.add,
                          ALU.add)
                    p.ts(h2q[:, ic * T:(ic + 1) * T], tfq[:, 0:T], 127.0, -128.0,
                         ALU.min, ALU.max, eng="gpsimd")
                for mc in range(EC):
                    mb = DB[mc % 2]
                    psx = ps_p if mc % 2 == 0 else ps_p2
                    scr, scr2 = mb["scr"], mb["scr2"]
                    for ic in range(IC):
                        p.mm(psx[:, 0:T],
                             wdT[:, (mc * IC + ic) * 128:(mc * IC + ic) * 128 + 128],
                             h2q[:, ic * T:(ic + 1) * T],
                             start=(ic == 0), stop=(ic == IC - 1))
                    p.act(scr[:, 0:T], psx[:, 0:T], AF.Silu, scale=swd[:, mc:mc + 1])
                    p.tt(scr[:, 0:T], scr[:, 0:T], s_b[:, mc * T:(mc + 1) * T],
                         ALU.mult)
                    p.tt(scr2[:, 0:T], scr[:, 0:T], xt[:, mc * T:(mc + 1) * T],
                         ALU.add, eng="gpsimd")
                    p.dma(fm_slice_one(o_out, t0, T, mc), row3(scr2.ap()[:, 0:T], T),
                          eng="gpsimd")

        p.emit(ctxs)
    return p


def build_fused():
    p = P(serial=False)
    x = p.din("x", [E, L])
    xe = p.din("xe", [3 * E, L])
    Wq = p.din("Wq", [E, E])
    Wv = p.din("Wv", [E, E])
    vecs = p.din("vecs", [128, 6 * EC])
    Wk = p.din("Wk", [E, E])
    Wpk = p.din("Wpk", [E, E])
    veck = p.din("veck", [128, 4 * EC])
    bpk_row = p.din("bpk_row", [1, E])
    w_gate = p.din("w_gate", [IF, E])
    w_up = p.din("w_up", [IF, E])
    clip_x = p.din("clip_x", [1, 1])
    Wpq = p.din("Wpq", [E, E])
    veca = p.din("veca", [128, 4 * EC])
    cwT = p.din("cwT", [5 * 64, E])
    gamma = p.din("gamma", [1, 1])
    W1 = p.din("W1", [E, MLP])
    b1 = p.din("b1", [128, MC])
    W2 = p.din("W2", [MLP, E])
    b2 = p.din("b2", [128, EC])
    w_down = p.din("w_down", [E, IF])
    clip_down = p.din("clip_down", [1, 1])
    sel = p.din("sel", [EC * 16, 128])
    nonce4 = p.din("nonce4", [4, 2])
    nonce_r = p.din("nonce_r", [1, 16])

    o_out = p.dout("out", [E, L])

    xnc = p.dtmp("xnc", [E, L], BF16)
    xne = p.dtmp("xne", [3 * E, L], BF16)
    Qc = p.dtmp("Qc", [E, L], BF16)
    Vc = p.dtmp("Vc", [E, L], BF16)
    Kc = p.dtmp("Kc", [E, L], BF16)
    qg_d = p.dtmp("qg_d", [128, EC])
    scores_d = p.dtmp("scores_d", [16, L])
    h2c = p.dtmp("h2c", [IF, L], BF16)
    kv_d = p.dtmp("kv_d", [128, H * 32])
    ks_d = p.dtmp("ks_d", [16, E])
    attn2c = p.dtmp("attn2c", [E, L], BF16)
    h1c = p.dtmp("h1c", [MLP, L], BF16)
    s_c = p.dtmp("s_c", [E, L], BF16)
    bnc = p.dtmp("bnc", [128, 1])
    bnc16 = p.dtmp("bnc16", [16, 1])
    bnc4a = p.dtmp("bnc4a", [4, 16])
    bnc4b = p.dtmp("bnc4b", [4, 16])
    sh1 = p.nc.dram_tensor("sh1", [8, 16], F32, kind="Internal",
                           addr_space="Shared").ap()
    sh2 = p.nc.dram_tensor("sh2", [8, 16], F32, kind="Internal",
                           addr_space="Shared").ap()

    ctxs = ExitStack()
    with ctxs:
        sb = lambda n, s_, d=F32: ctxs.enter_context(p.sbuf(n, s_, d))

        ones1 = sb("ones1", [128, 1], BF16)
        ones128 = sb("ones128", [1, 128], F32)
        ident_b = sb("ident_b", [128, 128], BF16)
        rrow = sb("rrow", [1, 128])
        mrow = sb("mrow", [1, 16])
        rtmp = sb("rtmp", [128, 1])
        w4 = sb("w4", [4, 16])
        nc_sb = sb("nc_sb", [1, 16])
        sh_rd = sb("sh_rd", [1, 128])
        sh_msk = sb("sh_msk", [1, 16], I32)
        sh_cnd = sb("sh_cnd", [1, 16])
        stage_g = sb("stage_g", [128, E])
        stage_g2 = sb("stage_g2", [128, E])
        ones1f = sb("ones1f", [128, 1], F32)

        p.memset(ones1[:], 1.0)
        p.memset(ones1f[:], 1.0)
        p.memset(ones128[:], 1.0)
        p.identity_fill(ident_b[:], 128)
        p.dma(nc_sb[:], nonce_r)

        with ExitStack() as pha:
            sba = lambda n, s_, d=F32: pha.enter_context(p.sbuf(n, s_, d))
            psa = lambda n, s_, d=F32: pha.enter_context(p.psum(n, s_, d))
            arnA = sba("arnA", [128, IC * EC * 128], BF16)
            arnB = sba("arnB", [128, IC * EC * 128], BF16)
            xqA = (sba("xq0", [128, EC * T], BF16),
                   sba("xq1", [128, EC * T], BF16))
            yqA = sba("yqA", [128, T])
            i1A = sba("i1A", [128, T], I32)
            i2A = sba("i2A", [128, T], I32)
            tfA = sba("tfA", [128, T])
            wiqB = (sba("wi_bA0", [128, T], BF16), sba("wi_bA1", [128, T], BF16))
            gsA = sba("gsA", [128, T])
            usA = sba("usA", [128, T])
            h2fA = sba("h2fA", [128, T])
            _h2b = sba("h2b0", [128, T], BF16)
            h2bA = (_h2b, _h2b)
            sg = sba("sgA", [128, IC])
            su = sba("suA", [128, IC])
            rm_a = sba("rm_aA", [128, 1])
            hmax = sba("hmaxA", [128, 1])
            sdr_a = sba("sdr_aA", [1, 4])
            rsx_pp = sba("rsx_ppA", [128, 1])
            sx_pp = sba("sx_ppA", [128, 1])
            gm1 = sba("gm1A", [1, 1])
            ps_g = psa("ps_gF", [128, T])
            ps_u = psa("ps_uF", [128, T])

            def aprep_unit(wi, ic):
                p._ser_now = "A"
                w_hbm, dstT, sc = ((w_gate, arnA, sg) if wi == 0
                                   else (w_up, arnB, su))
                stA = stageA[(wi * IC + ic) % 2]
                wiq = wiqB[(wi * IC + ic) % 2]
                p.dma(stA[:, 0:E], w_hbm[ic * 128:(ic + 1) * 128, :])
                p.red(rm_a, stA[:, 0:E], ALU.max, absval=True)
                p.ts(sc[:, ic:ic + 1], rm_a, 1.0 / 127.0, None, ALU.mult)
                p.recip(rm_a, rm_a)
                p.ts(rm_a, rm_a, 127.0, None, ALU.mult)
                for hf in range(2):
                    p.act(yqA[:, 0:T], stA[:, hf * T:(hf + 1) * T], AF.Copy,
                          bias=256.5, scale=rm_a)
                    quant_round(p, yqA, i1A, i2A, tfA, wiq, T)
                    for q in range(4):
                        kc = hf * 4 + q
                        psqx = psqB[kc % 2]
                        p.trans(psqx[:, 0:128], wiq[:, q * 128:(q + 1) * 128],
                                ident_b[:])
                        p.copy(dstT[:, (ic * EC + kc) * 128:
                                    (ic * EC + kc) * 128 + 128],
                               psqx[:, 0:128])

            def a_units():
                for bi, (src, c0) in enumerate([(xnc, 0), (xne, 0), (xne, EC),
                                                (xne, 2 * EC)]):
                    p._ser_now = None
                    p.memset(hmax[:], 0.0, eng="vector")
                    for ti in range(NT):
                        p._ser_now = None
                        xq = xqA[ti % 2]
                        t0 = ti * T
                        p.dma(tile_ap(xq, EC, T), fm_slice_off(src, t0, T, c0))
                        for c in range(EC):
                            seg = slice(c * T, (c + 1) * T)
                            p.act(yqA[:, 0:T], xq[:, seg], AF.Copy, bias=256.5,
                                  scale=rsx_pp[:, 0:1])
                            p.copy(i1A[:, 0:T], yqA[:, 0:T])
                            p.stt(i2A[:, 0:T], yqA[:, 0:T], -0.5, i1A[:, 0:T],
                                  ALU.add, ALU.subtract)
                            p.stt(tfA[:, 0:T], i1A[:, 0:T], -256.0, i2A[:, 0:T],
                                  ALU.add, ALU.add)
                            p.ts(xq[:, seg], tfA[:, 0:T], 127.0, -128.0,
                                 ALU.min, ALU.max, eng="gpsimd")
                        for ic in range(IC):
                            for kc in range(EC):
                                p.mm(ps_g[:, 0:T],
                                     arnA[:, (ic * EC + kc) * 128:
                                          (ic * EC + kc) * 128 + 128],
                                     xq[:, kc * T:(kc + 1) * T],
                                     start=(kc == 0), stop=(kc == EC - 1))
                            p.act(gsA[:, 0:T], ps_g[:, 0:T], AF.Silu,
                                  scale=sg[:, ic:ic + 1])
                            for kc in range(EC):
                                p.mm(ps_u[:, 0:T],
                                     arnB[:, (ic * EC + kc) * 128:
                                          (ic * EC + kc) * 128 + 128],
                                     xq[:, kc * T:(kc + 1) * T],
                                     start=(kc == 0), stop=(kc == EC - 1))
                            p.ts(usA[:, 0:T], ps_u[:, 0:T], su[:, ic:ic + 1],
                                 None, ALU.mult)
                            p.tt(h2fA[:, 0:T], gsA[:, 0:T], usA[:, 0:T],
                                 ALU.mult)
                            if bi == 0:
                                h2b = h2bA[ic % 2]
                                p.copy(h2b[:, 0:T], h2fA[:, 0:T], eng="gpsimd")
                                p.dma(fm_slice_one(h2c, t0, T, ic),
                                      row3(h2b.ap()[:, 0:T], T), eng="gpsimd")
                            p.red(rtmp, h2fA[:, 0:T], ALU.max, absval=True)
                            p.tt(hmax, hmax, rtmp, ALU.max)
                        yield
                    cross_part_max(p, hmax, bnc, rrow, mrow[0:1, 0:1])
                    p.dma(bnc4b[bi:bi + 1, 0:1], mrow[0:1, 0:1])
                emit_sh_payload(p, bnc4b, nonce4, w4, sh2)
                yield

            ag = a_units()

            def adv(n):
                for _ in range(n):
                    next(ag, None)

            with ExitStack() as ph:
                sbr = lambda n, s_, d=F32: ph.enter_context(p.sbuf(n, s_, d))
                psr = lambda n, s_, d=F32: ph.enter_context(p.psum(n, s_, d))
                RS = []
                for _i in range(2):
                    RS.append(dict(
                        xt=sbr(f"xtR{_i}", [128, EC * T]),
                        xn_f=sbr(f"xn_fR{_i}", [128, T]),
                        xn_f2=sbr(f"xn_f2R{_i}", [128, T]),
                        rt=sbr(f"rtR{_i}", [128, 1]),
                        rows=sbr(f"rowsR{_i}", [1, 2 * T]),
                        ps_mu=psr(f"ps_muR{_i}", [128, T]),
                    ))
                _xob = sbr("xob0R", [128, EC * T], BF16)
                xob = (_xob, _xob)
                sqf = sbr("sqfR", [128, EC * T], BF16)
                rs_sb = sbr("rs_sbR", [128, T])
                vmax = sbr("vmaxR", [128, 1])
                stageA = (stage_g, sbr("stageA1", [128, E]))
                ps_s1 = psr("ps_s1R", [1, T])
                ps_s2 = psr("ps_s2R", [1, T])
                psqB = (psr("psqR0", [128, 128], BF16),
                        psr("psqR1", [128, 128], BF16))

                def r_load(bi, ti):
                    p._ser_now = None
                    S = RS[ti % 2]
                    t0 = ti * T
                    src, c0 = (x, 0) if bi == 0 else (xe, (bi - 1) * EC)
                    p.dma(tile_ap(S["xt"], EC, T),
                          fm_slice_off(src, t0, T, c0),
                          eng=("sync" if ti % 2 == 0 else "gpsimd"))

                def r_stats(bi, ti):
                    p._ser_now = None
                    S = RS[ti % 2]
                    xt = S["xt"]
                    rows = S["rows"]
                    mu, rstd = rows[0:1, 0:T], rows[0:1, T:2 * T]
                    inv_n = 1.0 / E
                    p.act(sqf[:, 0:EC * T], xt[:, 0:EC * T], AF.Square)
                    for c in range(EC):
                        seg = slice(c * T, (c + 1) * T)
                        p.mm(ps_s1[0:1, 0:T], ones1f, xt[:, seg],
                             start=(c == 0), stop=(c == EC - 1))
                        p.mm(ps_s2[0:1, 0:T], ones1, sqf[:, seg],
                             start=(c == 0), stop=(c == EC - 1))
                    scr_r = S["xn_f"][0:1, 0:T]
                    p.ts(mu, ps_s1[0:1, 0:T], inv_n, None, ALU.mult)
                    p.tt(scr_r, mu, mu, ALU.mult)
                    p.stt(rstd, ps_s2[0:1, 0:T], inv_n, scr_r,
                          ALU.mult, ALU.subtract)
                    p.ts(rstd, rstd, LN_EPS, None, ALU.add)
                    p.act(rstd, rstd, AF.Sqrt)
                    p.recip(rstd, rstd)

                def r_apply(bi, ti):
                    p._ser_now = None
                    S = RS[ti % 2]
                    xt = S["xt"]
                    xo = xob[ti % 2]
                    rows = S["rows"]
                    mu_, rs_ = rows[0:1, 0:T], rows[0:1, T:2 * T]
                    t0 = ti * T
                    p.mm(S["ps_mu"][:, 0:T], ones128, rs_)
                    p.act(rs_sb[:, 0:T], S["ps_mu"][:, 0:T], AF.Copy)
                    p.mm(S["ps_mu"][:, 0:T], ones128, mu_)
                    for c in range(EC):
                        seg = slice(c * T, (c + 1) * T)
                        xf = S["xn_f"] if c % 2 == 0 else S["xn_f2"]
                        p.tt(xf[:, 0:T], xt[:, seg], S["ps_mu"][:, 0:T],
                             ALU.subtract)
                        p.tt(xo[:, seg], xf[:, 0:T], rs_sb[:, 0:T],
                             ALU.mult, eng="gpsimd")
                    p.red(S["rt"], xo[:, 0:EC * T], ALU.max, absval=True)
                    p.tt(vmax, vmax, S["rt"], ALU.max)
                    dst, dc0 = (xnc, 0) if bi == 0 else (xne, (bi - 1) * EC)
                    p.dma(fm_slice_off(dst, t0, T, dc0), tile_ap(xo, EC, T),
                          eng="gpsimd")

                j = 0
                for bi in range(4):
                    p.memset(vmax[:], 0.0, eng="vector")
                    r_load(bi, 0)
                    r_load(bi, 1)
                    r_stats(bi, 0)
                    for ti in range(NT):
                        if ti + 1 < NT:
                            r_stats(bi, ti + 1)
                        r_apply(bi, ti)
                        if ti + 2 < NT:
                            r_load(bi, ti + 2)
                        u = bi * NT + ti + 1
                        while ("noaprep" not in EXP) and j < 2 * IC and j * 32 < u * 2 * IC:
                            aprep_unit(j // IC, j % IC)
                            j += 1
                    cross_part_max(p, vmax, bnc, rrow, mrow[0:1, 0:1])
                    p.dma(bnc4a[bi:bi + 1, 0:1], mrow[0:1, 0:1])
                while ("noaprep" not in EXP) and j < 2 * IC:
                    aprep_unit(j // IC, j % IC)
                    j += 1
                p._ser_now = None
                emit_sh_payload(p, bnc4a, nonce4, w4, sh1)
                mW1 = p.mark()
            p.fence()
            if BUILD_STOP == "era1":
                p.emit(ctxs)
                return p

            with ExitStack() as ph:
                sbq = lambda n, s_, d=F32: ph.enter_context(p.sbuf(n, s_, d))
                psv = lambda n, s_, d=F32: ph.enter_context(p.psum(n, s_, d))
                wb16 = sbq("wb16Q", [128, EC * E], BF16)
                xn_inB = (sbq("xn_in0", [128, EC * T], BF16),
                          sbq("xn_in1", [128, EC * T], BF16))
                qpreB = (sbq("qpre0", [128, EC * T], BF16),
                         sbq("qpre1", [128, EC * T], BF16))
                sqFQ = sbq("sqFQ", [128, EC * T], BF16)
                rows = sbq("rowsQ", [1, 3 * T])
                muB = (sbq("mu_sbQ0", [128, T]), sbq("mu_sbQ1", [128, T]))
                rsB = (sbq("rs_sbQ0", [128, T]), sbq("rs_sbQ1", [128, T]))
                scrB = (sbq("scrQ0", [128, T]), sbq("scrQ1", [128, T]))
                qsum = sbq("qsumQ", [128, EC])
                vec_sb = sbq("vecQ", [128, 6 * EC])
                ps_p0 = psv("ps_p0Q", [128, T])
                ps_p1 = psv("ps_p1Q", [128, T])
                ps_mu = psv("ps_muQ", [128, T])
                ps_s1 = psv("ps_s1Q", [1, T])
                ps_s2 = psv("ps_s2Q", [1, T])
                mu2_ = rows[0:1, 0:T]
                rs2_ = rows[0:1, T:2 * T]
                scr1_ = rows[0:1, 2 * T:3 * T]

                p.set_serial(True, "B")
                p.dma(vec_sb[:], vecs)
                p.memset(qsum[:], 0.0)
                p.set_serial(False)

                def proj_head(ti, b0, vec):
                    qpre, xn_in = qpreB[ti % 2], xn_inB[ti % 2]
                    t0 = ti * T
                    p.dma(tile_ap(xn_in, EC, T), fm_slice(xnc, t0, T))
                    for mc in range(EC):
                        psx = ps_p0 if mc % 2 == 0 else ps_p1
                        for kc in range(EC):
                            p.mm(psx[:, 0:T],
                                 wb16[:, kc * E + mc * 128: kc * E + mc * 128 + 128],
                                 xn_in[:, kc * T:(kc + 1) * T],
                                 start=(kc == 0), stop=(kc == EC - 1))
                        p.act(qpre[:, mc * T:(mc + 1) * T], psx[:, 0:T],
                              AF.Identity, bias=vec[:, b0 * EC + mc: b0 * EC + mc + 1])

                def proj_tail(ti, b0, vec, out_hbm, do_q):
                    qpre = qpreB[ti % 2]
                    mu_sb, rs_sb, scr = muB[ti % 2], rsB[ti % 2], scrB[ti % 2]
                    ln_stats3(p, qpre, sqFQ, ones1, ps_s1, ps_s2, T, mu2_,
                              rs2_, scr1_)
                    p.mm(ps_mu[:, 0:T], ones128, rs2_)
                    p.act(rs_sb[:, 0:T], ps_mu[:, 0:T], AF.Copy)
                    p.mm(ps_mu[:, 0:T], ones128, mu2_)
                    for c in range(EC):
                        seg = slice(c * T, (c + 1) * T)
                        p.tt(scr[:, 0:T], qpre[:, seg], ps_mu[:, 0:T],
                             ALU.subtract)
                        p.ts(qpre[:, seg], scr[:, 0:T],
                             vec[:, (b0 + 1) * EC + c:(b0 + 1) * EC + c + 1],
                             None, ALU.mult_mult if False else ALU.mult,
                             eng="gpsimd") if False else None
                        p.tt(scr[:, 0:T], scr[:, 0:T], rs_sb[:, 0:T], ALU.mult)
                        p.ts(qpre[:, seg], scr[:, 0:T],
                             vec[:, (b0 + 1) * EC + c:(b0 + 1) * EC + c + 1],
                             vec[:, (b0 + 2) * EC + c:(b0 + 2) * EC + c + 1],
                             ALU.mult, ALU.add, eng="gpsimd")
                    p.dma(fm_slice(out_hbm, ti * T, T), tile_ap(qpre, EC, T),
                          eng="gpsimd")
                    if do_q:
                        for c in range(EC):
                            p.red(rtmp, qpre[:, c * T:(c + 1) * T], ALU.add)
                            p.tt(qsum[:, c:c + 1], qsum[:, c:c + 1], rtmp,
                                 ALU.add)

                for (w_hbm, b0, out_hbm, do_q) in [(Wq, 0, Qc, True),
                                                   (Wv, 3, Vc, False)]:
                    load_weight_bf16_2q(p, wb16, w_hbm, E, E, stage_g, stage_g2)
                    proj_head(0, b0, vec_sb)
                    for ti in range(NT):
                        if ti + 1 < NT:
                            proj_head(ti + 1, b0, vec_sb)
                        proj_tail(ti, b0, vec_sb, out_hbm, do_q)
                p.ts(qsum[:], qsum[:], 1.0 / (L * np.sqrt(D)), None, ALU.mult)
                p.dma(qg_d, qsum[:])
            p.fence()
            if BUILD_STOP == "era2":
                p.emit(ctxs)
                return p

            with ExitStack() as ph:
                sbk = lambda n, s_, d=F32: ph.enter_context(p.sbuf(n, s_, d))
                psk = lambda n, s_, d=F32: ph.enter_context(p.psum(n, s_, d))
                wb16 = sbk("wb16K", [128, EC * E], BF16)
                xn_inB = (sbk("xn_in0K", [128, EC * T], BF16),
                          sbk("xn_in1K", [128, EC * T], BF16))
                _kpre = sbk("kpre0", [128, EC * T], BF16)
                qpreB = (_kpre, _kpre)
                sqFK = sbk("sqFK", [128, EC * T], BF16)
                rows = sbk("rowsK", [1, 3 * T])
                muKB = (sbk("mu_sbK0", [128, T]), sbk("mu_sbK1", [128, T]))
                rsKB = (sbk("rs_sbK0", [128, T]), sbk("rs_sbK1", [128, T]))
                scrKB = (sbk("scrK0", [128, T]), sbk("scrK1", [128, T]))
                qsl = sbk("qslK", [128, EC * 16], BF16)
                qg_sb = sbk("qg_sbK", [128, EC])
                _scbK = sbk("scbK", [16, T])
                scbB = (_scbK, _scbK)
                vec_sb = sbk("vecK", [128, 4 * EC])
                ps_p0 = psk("ps_p0K", [128, T])
                ps_p1 = psk("ps_p1K", [128, T])
                ps_mu = psk("ps_muK", [128, T])
                ps_s1 = psk("ps_s1K", [1, T])
                ps_s2 = psk("ps_s2K", [1, T])
                ps_sc = psk("ps_scK", [16, T])
                mu2_ = rows[0:1, 0:T]
                rs2_ = rows[0:1, T:2 * T]
                scr1_ = rows[0:1, 2 * T:3 * T]

                p.set_serial(True, "B")
                p.dma(vec_sb[:], veck)
                p.dma(qg_sb[:], qg_d)
                p.memset(qsl[:], 0.0)
                for c in range(EC):
                    p.copy(qsl[0:64, c * 16 + 2 * c: c * 16 + 2 * c + 1],
                           qg_sb[0:64, c:c + 1])
                    p.copy(qsl[64:128, c * 16 + 2 * c + 1: c * 16 + 2 * c + 2],
                           qg_sb[64:128, c:c + 1])
                p.set_serial(False)
                load_weight_bf16_2q(p, wb16, Wk, E, E, stage_g, stage_g2)

                def head_k(ti):
                    xn_in = xn_inB[ti % 2]
                    t0 = ti * T
                    p.dma(tile_ap(xn_in, EC, T), fm_slice(xnc, t0, T))

                def proj_tile_k(ti):
                    qpre, xn_in = qpreB[ti % 2], xn_inB[ti % 2]
                    mu_sb, rs_sb, scr = muKB[ti % 2], rsKB[ti % 2], scrKB[ti % 2]
                    t0 = ti * T
                    for mc in range(EC):
                        psx = ps_p0 if mc % 2 == 0 else ps_p1
                        for kc in range(EC):
                            p.mm(psx[:, 0:T],
                                 wb16[:, kc * E + mc * 128: kc * E + mc * 128 + 128],
                                 xn_in[:, kc * T:(kc + 1) * T],
                                 start=(kc == 0), stop=(kc == EC - 1))
                        p.act(qpre[:, mc * T:(mc + 1) * T], psx[:, 0:T],
                              AF.Identity, bias=vec_sb[:, mc:mc + 1])
                    ln_stats3(p, qpre, sqFK, ones1, ps_s1, ps_s2, T, mu2_,
                              rs2_, scr1_)
                    p.mm(ps_mu[:, 0:T], ones128, rs2_)
                    p.act(rs_sb[:, 0:T], ps_mu[:, 0:T], AF.Copy)
                    p.mm(ps_mu[:, 0:T], ones128, mu2_)
                    for c in range(EC):
                        seg = slice(c * T, (c + 1) * T)
                        p.tt(scr[:, 0:T], qpre[:, seg], ps_mu[:, 0:T],
                             ALU.subtract)
                        p.tt(scr[:, 0:T], scr[:, 0:T], rs_sb[:, 0:T], ALU.mult)
                        p.ts(qpre[:, seg], scr[:, 0:T],
                             vec_sb[:, EC + c:EC + c + 1],
                             vec_sb[:, 2 * EC + c:2 * EC + c + 1],
                             ALU.mult, ALU.add, eng="gpsimd")
                    p.dma(fm_slice(Kc, t0, T), tile_ap(qpre, EC, T),
                          eng="gpsimd")
                    scb = scbB[ti % 2]
                    for kc in range(EC):
                        p.mm(ps_sc[0:16, 0:T], qsl[:, kc * 16:(kc + 1) * 16],
                             qpre[:, kc * T:(kc + 1) * T],
                             start=(kc == 0), stop=(kc == EC - 1))
                    p.copy(scb[0:16, 0:T], ps_sc[0:16, 0:T])
                    p.dma(scores_d[0:16, t0:t0 + T], scb[0:16, 0:T])

                head_k(0)
                for ti in range(3):
                    head_k(ti + 1)
                    proj_tile_k(ti)
                p._ser_now = None
                p.dep_on(mW1, p.mark())
                emit_sh_read(p, sh1, nc_sb, sh_rd, sh_msk, sh_cnd,
                             gm1[0:1, 0:1])
                make_scale_g(p, gm1[0:1, 0:1], clip_x, sdr_a, ones128,
                             ps_mu[:, 0:4], rsx_pp, sx_pp)
                for ic in range(IC):
                    p.tt(sg[:, ic:ic + 1], sg[:, ic:ic + 1], sx_pp[:, 0:1],
                         ALU.mult)
                    p.tt(su[:, ic:ic + 1], su[:, ic:ic + 1], sx_pp[:, 0:1],
                         ALU.mult)
                for ti in range(3, NT):
                    if ti + 1 < NT:
                        head_k(ti + 1)
                    proj_tile_k(ti)
                    adv(1)
            p.fence()

            with ExitStack() as ph:
                sbc = lambda n, s_, d=F32: ph.enter_context(p.sbuf(n, s_, d))
                psc = lambda n, s_, d=F32: ph.enter_context(p.psum(n, s_, d))
                wb16 = sbc("wb16P3", [128, EC * E], BF16)
                ident_f = sbc("ident_f3", [128, 128], F32)
                a_tm_f = sbc("a_tm_f3", [128, (L // 128) * 16])
                a_tm_b = sbc("a_tm_b3", [128, (L // 128) * 16], BF16)
                a0r_b = sbc("a0r_b3", [1, 16], BF16)
                a0row = sbc("a0row3", [1, 16])
                pk0_b = sbc("pk0_b3", [1, E], BF16)
                pk0_f = sbc("pk0_f3", [1, E])
                kvout = sbc("kvout3", [128, H * 32])
                ksout = sbc("ksout3", [16, E])
                vec_sb = sbc("vecP3", [128, 4 * EC])

                p.set_serial(True, "B")
                p.identity_fill(ident_f[:], 128)
                p.dma(vec_sb[:], veck)
                with ExitStack() as phs:
                    scores = phs.enter_context(p.sbuf("scores3", [16, L]))
                    srow16 = phs.enter_context(p.sbuf("srow163", [16, 4]))
                    ptr_f = phs.enter_context(p.psum("ptr_f3", [128, 16], F32))
                    p.dma(scores[0:16, :], scores_d)
                    p.red(srow16[0:16, 0:1], scores[:], ALU.max)
                    p.ts(srow16[0:16, 0:1], srow16[0:16, 0:1], 0.0, None,
                         ALU.max)
                    p.ts(srow16[0:16, 1:2], srow16[0:16, 0:1], -1.0, None,
                         ALU.mult)
                    p.act(scores[:], scores[:], AF.Exp, bias=srow16[0:16, 1:2])
                    p.red(srow16[0:16, 2:3], scores[:], ALU.add)
                    p.act(srow16[0:16, 3:4], srow16[0:16, 1:2], AF.Exp)
                    p.tt(srow16[0:16, 2:3], srow16[0:16, 2:3],
                         srow16[0:16, 3:4], ALU.add)
                    p.recip(srow16[0:16, 2:3], srow16[0:16, 2:3])
                    p.ts(srow16[0:16, 2:3], srow16[0:16, 2:3], float(L), None,
                         ALU.mult)
                    p.ts(scores[:], scores[:], srow16[0:16, 2:3], None,
                         ALU.mult)
                    p.tt(srow16[0:16, 3:4], srow16[0:16, 3:4],
                         srow16[0:16, 2:3], ALU.mult)
                    p.dma(bnc16, srow16[0:16, 3:4])
                    p.dma(a0row[0:1, 0:16], bnc16.rearrange("p c -> c p"))
                    p.copy(a0r_b[0:1, 0:16], a0row[0:1, 0:16])
                    for tc in range(L // 128):
                        p.trans(ptr_f[:, 0:16],
                                scores[0:16, tc * 128:(tc + 1) * 128],
                                ident_f[0:16, 0:16])
                        p.copy(a_tm_f[:, tc * 16:(tc + 1) * 16], ptr_f[:, 0:16])
                p.fence()
                p.copy(a_tm_b[:], a_tm_f[:])
                p.dma(pk0_f[0:1, 0:E], bpk_row)
                p.act(pk0_f[0:1, 0:E], pk0_f[0:1, 0:E], AF.Tanh)
                p.ts(pk0_b[0:1, 0:E], pk0_f[0:1, 0:E], 1.0, None, ALU.add)
                p.set_serial(False)
                adv(2)

                k_bf = sbc("k_bf3", [128, EC * T], BF16)
                v_bf = sbc("v_bf3", [128, EC * T], BF16)
                pk_bf = sbc("pk_bf3", [128, EC * T], BF16)
                pkt_b = sbc("pkt_b3", [128, E], BF16)
                vst_b = sbc("vst_b3", [128, E], BF16)
                scr3B = (sbc("scr3a", [128, T]), sbc("scr3b", [128, T]))
                ps_p3a = psc("ps_p3a", [128, T])
                ps_p3b = psc("ps_p3b", [128, T])
                ptr = psc("ptr3", [128, 1024], BF16)
                ps_kv = psc("ps_kv3", [128, H * 32])
                ps_ks = psc("ps_ks3", [16, E])

                p.set_serial(True, "B")
                load_weight_bf16_2q(p, wb16, Wpk, E, E, stage_g, stage_g2)
                p.memset(ps_kv[:], 0.0, eng="vector")
                p.memset(ps_ks[0:16, :], 0.0, eng="vector")
                for c in range(EC):
                    p.mm(ps_ks[0:16, c * 128:(c + 1) * 128], a0r_b[0:1, 0:16],
                         pk0_b[0:1, c * 128:(c + 1) * 128],
                         start=False, stop=False)
                p.set_serial(False)

                for ti in range(NT):
                    t0 = ti * T
                    p.dma(tile_ap(k_bf, EC, T), fm_slice(Kc, t0, T))
                    p.dma(tile_ap(v_bf, EC, T), fm_slice(Vc, t0, T))
                    for mc in range(EC):
                        psx = ps_p3a if mc % 2 == 0 else ps_p3b
                        scr3 = scr3B[mc % 2]
                        for kc in range(EC):
                            p.mm(psx[:, 0:T],
                                 wb16[:, kc * E + mc * 128: kc * E + mc * 128 + 128],
                                 k_bf[:, kc * T:(kc + 1) * T],
                                 start=(kc == 0), stop=(kc == EC - 1))
                        p.act(scr3[:, 0:T], psx[:, 0:T], AF.Tanh,
                              bias=vec_sb[:, 3 * EC + mc:3 * EC + mc + 1])
                        p.ts(pk_bf[:, mc * T:(mc + 1) * T], scr3[:, 0:T], 1.0,
                             None, ALU.add, eng="gpsimd")
                    for tc in range(TC):
                        gtc = ti * TC + tc
                        last = (ti == NT - 1 and tc == TC - 1)
                        for c in range(EC):
                            p.trans(ptr[:, 0:128],
                                    pk_bf[:, c * T + tc * 128: c * T + tc * 128 + 128],
                                    ident_b[:])
                            p.act(pkt_b[:, c * 128:(c + 1) * 128], ptr[:, 0:128],
                                  AF.Copy)
                            p.trans(ptr[:, 128:256],
                                    v_bf[:, c * T + tc * 128: c * T + tc * 128 + 128],
                                    ident_b[:])
                            p.ts(vst_b[:, c * 128:c * 128 + 64], ptr[:, 128:192],
                                 a_tm_f[:, gtc * 16 + 2 * c: gtc * 16 + 2 * c + 1],
                                 None, ALU.mult)
                            p.ts(vst_b[:, c * 128 + 64:c * 128 + 128],
                                 ptr[:, 192:256],
                                 a_tm_f[:, gtc * 16 + 2 * c + 1: gtc * 16 + 2 * c + 2],
                                 None, ALU.mult)
                        for c in range(EC):
                            p.mm(ps_ks[0:16, c * 128:(c + 1) * 128],
                                 a_tm_b[:, gtc * 16:(gtc + 1) * 16],
                                 pkt_b[:, c * 128:(c + 1) * 128],
                                 start=False, stop=last)
                        for h in range(H):
                            c, r = h // 2, h % 2
                            p.mm(ps_kv[r * 64:(r + 1) * 64, c * 64:(c + 1) * 64],
                                 pkt_b[:, h * 64:(h + 1) * 64],
                                 vst_b[:, h * 64:(h + 1) * 64],
                                 start=False, stop=last)
                    adv(3)

                p.copy(kvout[:], ps_kv[:])
                p.dma(kv_d, kvout[:])
                p.copy(ksout[0:16, :], ps_ks[0:16, :])
                p.dma(ks_d, ksout[0:16, :])
                adv(3)
                mW2 = p.mark()
        p.fence()

        p.set_serial(True, "B")
        with ExitStack() as ph:
            sba = lambda n, s_, d=F32: ph.enter_context(p.sbuf(n, s_, d))
            psa = lambda n, s_, d=F32: ph.enter_context(p.psum(n, s_, d))
            wb16 = sba("wb16P4", [128, EC * E], BF16)
            vec = sba("vecP4", [128, 4 * EC])
            numT = sba("numT", [128, EC * 128], BF16)
            denT = sba("denT", [128, EC * 16], BF16)
            sel_b = sba("sel_b", [16, EC * 128], BF16)
            cvT = sba("cvT", [128, EC * 5 * 128], BF16)
            kv_f = sba("kv_f", [128, H * 32])
            kv_b = sba("kv_b", [128, H * 32], BF16)
            ksf = sba("ksf", [16, E])
            ks_b = sba("ks_b", [16, E], BF16)
            scr = sba("scrP4", [128, T])
            selt = sba("selt", [128, 128])
            q_bf = sba("q_bfP4", [128, EC * T], BF16)
            sq_bf = sba("sq_bfP4", [128, EC * T], BF16)
            pq_bf = sba("pq_bfP4", [128, EC * T], BF16)
            xn_f = sba("xn_fP4", [128, EC * T])
            attn_a = sba("attn_a", [128, EC * T], BF16)
            attn_b_ = sba("attn_b", [128, EC * T], BF16)
            tail2 = sba("tail2", [128, EC * 2], BF16)
            conv_win = sba("conv_win", [128, EC * (T + 4)], BF16)
            attn2_bf = sba("attn2_bf", [128, EC * T], BF16)
            rden = sba("rden", [16, T])
            rden_b = sba("rden_b", [16, T], BF16)
            rdb = sba("rdb", [128, T])
            gam_pp = sba("gam_pp", [128, 1])
            gcb = sba("gcb", [128, EC])
            mu_sb = sba("mu_sbP", [128, T])
            rs_sb = sba("rs_sbP", [128, T])
            rows = sba("rowsP4", [1, 3 * T])
            ps_p = psa("ps_p", [128, T])
            ps_dr = psa("ps_dr", [128, T])
            ps_a = psa("ps_a", [128, T])
            ps_pB = psa("ps_pB", [128, T])
            mu2_ = rows[0:1, 0:T]
            rs2_ = rows[0:1, T:2 * T]
            scr1_ = rows[0:1, 2 * T:3 * T]

            p.set_serial(False)
            p.dma(vec[:], veca)
            load_weight_bf16_2q(p, wb16, Wpq, E, E, stage_g, stage_g2)
            p.set_serial(True, "B")
            for c in range(EC):
                p.dma(selt[0:16, 0:128], sel[c * 16:(c + 1) * 16, :])
                p.copy(sel_b[0:16, c * 128:(c + 1) * 128], selt[0:16, 0:128])
            p.memset(cvT[:], 0.0)
            for c in range(EC):
                for k in range(5):
                    col = (c * 5 + k) * 128
                    p.dma(selt[0:64, 0:64],
                          cwT[k * 64:(k + 1) * 64, 2 * c * 64:(2 * c + 1) * 64])
                    p.copy(cvT[0:64, col:col + 64], selt[0:64, 0:64])
                    p.dma(selt[64:128, 0:64],
                          cwT[k * 64:(k + 1) * 64, (2 * c + 1) * 64:(2 * c + 2) * 64])
                    p.copy(cvT[64:128, col + 64:col + 128], selt[64:128, 0:64])
            p.dma(kv_f[:], kv_d)
            p.copy(kv_b[:], kv_f[:])
            p.memset(numT[:], 0.0)
            for c in range(EC):
                p.copy(numT[0:64, c * 128: c * 128 + 64],
                       kv_b[0:64, c * 64:(c + 1) * 64])
                p.copy(numT[64:128, c * 128 + 64: c * 128 + 128],
                       kv_b[64:128, c * 64:(c + 1) * 64])
            p.dma(ksf[0:16, :], ks_d)
            p.copy(ks_b[0:16, :], ksf[0:16, :])
            p.memset(denT[:], 0.0)
            with ExitStack() as pht:
                ptr_b = pht.enter_context(p.psum("ptr_b", [128, 16], BF16))
                ps_t4 = pht.enter_context(p.psum("ps_t4", [128, 4], F32))
                p.dma(rows[0:1, 0:1], gamma)
                p.mm(ps_t4[:, 0:1], ones128, rows[0:1, 0:1])
                p.copy(gam_pp[:, 0:1], ps_t4[:, 0:1])
                p.ts(gcb[:], vec[:, 3 * EC:4 * EC], gam_pp[:, 0:1], None,
                     ALU.mult)
                for c in range(EC):
                    p.trans(ptr_b[:, 0:16], ks_b[0:16, c * 128:(c + 1) * 128],
                            ident_b[0:16, 0:16])
                    p.copy(denT[0:64, c * 16 + 2 * c: c * 16 + 2 * c + 1],
                           ptr_b[0:64, 2 * c:2 * c + 1])
                    p.copy(denT[64:128, c * 16 + 2 * c + 1: c * 16 + 2 * c + 2],
                           ptr_b[64:128, 2 * c + 1:2 * c + 2])
            p.fence()
            p.set_serial(False)
            ps_s1 = psa("ps_s1P4", [1, T])
            ps_s2 = psa("ps_s2P4", [1, T])
            ps_b = psa("ps_bP4", [128, T])

            def conv_ln_store(ti_prev, a_cur, a_prev):
                t0p = ti_prev * T
                for c in range(EC):
                    w0 = c * (T + 4)
                    if ti_prev == 0:
                        p.memset(conv_win[:, w0:w0 + 2], 0.0, eng="vector")
                    else:
                        p.copy(conv_win[:, w0:w0 + 2], tail2[:, c * 2:(c + 1) * 2])
                    p.copy(conv_win[:, w0 + 2:w0 + 2 + T],
                           a_prev[:, c * T:(c + 1) * T])
                    if ti_prev == NT - 1:
                        p.memset(conv_win[:, w0 + 2 + T:w0 + 4 + T], 0.0,
                                 eng="vector")
                    else:
                        p.copy(conv_win[:, w0 + 2 + T:w0 + 4 + T],
                               a_cur[:, c * T:c * T + 2])
                    p.copy(tail2[:, c * 2:(c + 1) * 2],
                           a_prev[:, (c + 1) * T - 2:(c + 1) * T])
                for c in range(EC):
                    psc_ = ps_a if c % 2 == 0 else ps_p
                    for k in range(5):
                        p.mm(psc_[:, 0:T],
                             cvT[:, (c * 5 + k) * 128:(c * 5 + k) * 128 + 128],
                             conv_win[:, c * (T + 4) + k: c * (T + 4) + k + T],
                             start=(k == 0), stop=(k == 4))
                    p.act(scr[:, 0:T], psc_[:, 0:T], AF.Identity,
                          bias=gcb[:, c:c + 1], scale=gam_pp[:, 0:1])
                    p.tt(xn_f[:, c * T:(c + 1) * T], scr[:, 0:T],
                         a_prev[:, c * T:(c + 1) * T], ALU.add)
                p.copy(sq_bf[:], xn_f[:], eng="gpsimd")
                ln_stats(p, sq_bf, q_bf, ones1, ps_s1, ps_s2, T, mu2_, rs2_,
                         scr1_)
                p.mm(ps_b[:, 0:T], ones128, mu2_)
                p.act(mu_sb[:, 0:T], ps_b[:, 0:T], AF.Copy)
                p.mm(ps_b[:, 0:T], ones128, rs2_)
                p.act(rs_sb[:, 0:T], ps_b[:, 0:T], AF.Copy)
                apply_ln(p, attn2_bf, sq_bf, mu_sb, rs_sb, scr, T,
                         g_vec=vec[:, EC:2 * EC], b_vec=vec[:, 2 * EC:3 * EC])
                p.dma(fm_slice(attn2c, t0p, T), tile_ap(attn2_bf, EC, T),
                      eng="gpsimd")

            for ti in range(NT):
                t0 = ti * T
                a_cur = attn_a if ti % 2 == 0 else attn_b_
                a_prev = attn_b_ if ti % 2 == 0 else attn_a
                p.dma(tile_ap(q_bf, EC, T), fm_slice(Qc, t0, T))
                for mc in range(EC):
                    psx = ps_p if mc % 2 == 0 else ps_pB
                    for kc in range(EC):
                        p.mm(psx[:, 0:T],
                             wb16[:, kc * E + mc * 128: kc * E + mc * 128 + 128],
                             q_bf[:, kc * T:(kc + 1) * T],
                             start=(kc == 0), stop=(kc == EC - 1))
                    p.act(scr[:, 0:T], psx[:, 0:T], AF.Tanh, bias=vec[:, mc:mc + 1])
                    p.ts(pq_bf[:, mc * T:(mc + 1) * T], scr[:, 0:T], 1.0, None,
                         ALU.add, eng="gpsimd")
                for c in range(EC):
                    p.mm(ps_dr[0:16, 0:T], denT[:, c * 16:(c + 1) * 16],
                         pq_bf[:, c * T:(c + 1) * T],
                         start=(c == 0), stop=(c == EC - 1))
                p.ts(rden[0:16, 0:T], ps_dr[0:16, 0:T], DEN_EPS, None, ALU.add)
                p.recip(rden[0:16, 0:T], rden[0:16, 0:T])
                p.copy(rden_b[0:16, 0:T], rden[0:16, 0:T], eng="gpsimd")
                for c in range(EC):
                    psn = ps_a if c % 2 == 0 else ps_pB
                    p.mm(psn[:, 0:T], numT[:, c * 128:(c + 1) * 128],
                         pq_bf[:, c * T:(c + 1) * T])
                    p.mm(ps_dr[:, 0:T], sel_b[0:16, c * 128:(c + 1) * 128],
                         rden_b[0:16, 0:T])
                    p.act(rdb[:, 0:T], ps_dr[:, 0:T], AF.Copy)
                    p.tt(a_cur[:, c * T:(c + 1) * T], psn[:, 0:T], rdb[:, 0:T],
                         ALU.mult)
                if ti > 0:
                    conv_ln_store(ti - 1, a_cur, a_prev)
            last_buf = attn_b_ if (NT - 1) % 2 == 1 else attn_a
            conv_ln_store(NT - 1, last_buf, last_buf)
        p.fence()

        with ExitStack() as ph:
            stage = ph.enter_context(p.sbuf("stageb", [128, MLP]))
            stage2 = ph.enter_context(p.sbuf("stageb2", [128, MLP]))
            w1_bf = ph.enter_context(p.sbuf("w1_bf", [128, EC * MLP], BF16))
            ps_p = ph.enter_context(p.psum("ps_pb", [128, T], F32))
            ps_p2 = ph.enter_context(p.psum("ps_pb2", [128, T], F32))
            h1_bf2 = ph.enter_context(p.sbuf("h1_bf2", [128, T], BF16))
            h1_bf = ph.enter_context(p.sbuf("h1_bf", [128, T], BF16))
            a2B = (ph.enter_context(p.sbuf("a2b_bf0", [128, EC * T], BF16)),
                   ph.enter_context(p.sbuf("a2b_bf1", [128, EC * T], BF16)))
            b1_sb = ph.enter_context(p.sbuf("b1_sb", [128, MC], F32))
            p.dma(b1_sb[:], b1)
            load_weight_bf16_2q(p, w1_bf, W1, E, MLP, stage, stage2)
            for ti in range(NT):
                t0 = ti * T
                a2_bf = a2B[ti % 2]
                p.dma(tile_ap(a2_bf, EC, T), fm_slice(attn2c, t0, T))
                for mc in range(MC):
                    psx = ps_p if mc % 2 == 0 else ps_p2
                    hbx = h1_bf if mc % 2 == 0 else h1_bf2
                    for kc in range(EC):
                        p.mm(psx[:, 0:T],
                             w1_bf[:, kc * MLP + mc * 128: kc * MLP + mc * 128 + 128],
                             a2_bf[:, kc * T:(kc + 1) * T],
                             start=(kc == 0), stop=(kc == EC - 1))
                    p.act(hbx[:, 0:T], psx[:, 0:T], AF.Gelu,
                          bias=b1_sb[:, mc:mc + 1])
                    p.dma(fm_slice_one(h1c, t0, T, mc), row3(hbx.ap()[:, 0:T], T),
                          eng="gpsimd")
        p.fence()

        with ExitStack() as ph:
            stage = ph.enter_context(p.sbuf("stagec", [128, MLP]))
            stage2 = ph.enter_context(p.sbuf("stagec2", [128, MLP]))
            w2_bf = ph.enter_context(p.sbuf("w2_bf", [128, MC * E], BF16))
            ps_p = ph.enter_context(p.psum("ps_pc", [128, T], F32))
            ps_p2 = ph.enter_context(p.psum("ps_pc2", [128, T], F32))
            xn_in = ph.enter_context(p.sbuf("xn_inC", [128, EC * T], BF16))
            scrCB = (ph.enter_context(p.sbuf("scrC0", [128, T], F32)),
                     ph.enter_context(p.sbuf("scrC1", [128, T], F32)))
            h1B = (ph.enter_context(p.sbuf("h1t0", [128, MC * T], BF16)),
                   ph.enter_context(p.sbuf("h1t1", [128, MC * T], BF16)))
            b2_sb = ph.enter_context(p.sbuf("b2_sb", [128, EC], F32))
            s_bf = ph.enter_context(p.sbuf("s_bfC", [128, EC * T], BF16))
            a2_bf = ph.enter_context(p.sbuf("a2c_bf", [128, EC * T], BF16))
            p.dma(b2_sb[:], b2)
            load_weight_bf16_2q(p, w2_bf, W2, MLP, E, stage, stage2)
            for ti in range(NT):
                t0 = ti * T
                h1t = h1B[ti % 2]
                p.dma(tile_ap(h1t, MC, T), fm_slice(h1c, t0, T, nchunk=MC))
                p.dma(tile_ap(a2_bf, EC, T), fm_slice(attn2c, t0, T))
                p.dma(tile_ap(xn_in, EC, T), fm_slice(xnc, t0, T))
                for mc in range(EC):
                    psx = ps_p if mc % 2 == 0 else ps_p2
                    scr = scrCB[mc % 2]
                    for kc in range(MC):
                        p.mm(psx[:, 0:T],
                             w2_bf[:, kc * E + mc * 128: kc * E + mc * 128 + 128],
                             h1t[:, kc * T:(kc + 1) * T],
                             start=(kc == 0), stop=(kc == MC - 1))
                    p.act(scr[:, 0:T], psx[:, 0:T], AF.Identity,
                          bias=b2_sb[:, mc:mc + 1])
                    p.tt(scr[:, 0:T], scr[:, 0:T], a2_bf[:, mc * T:(mc + 1) * T],
                         ALU.add, eng="gpsimd")
                    p.tt(scr[:, 0:T], scr[:, 0:T], xn_in[:, mc * T:(mc + 1) * T],
                         ALU.add)
                    p.copy(s_bf[:, mc * T:(mc + 1) * T], scr[:, 0:T],
                           eng="gpsimd")
                p.dma(fm_slice(s_c, t0, T), tile_ap(s_bf, EC, T), eng="gpsimd")
        m4c = p.mark()
        p.fence()

        p.set_serial(True, "B")
        with ExitStack() as ph:
            sbd = lambda n, s_, d=F32: ph.enter_context(p.sbuf(n, s_, d))
            psd = lambda n, s_, d=F32: ph.enter_context(p.psum(n, s_, d))
            wdT = sbd("wdT", [128, EC * IC * 128], BF16)
            swd = sbd("swd", [128, EC])
            h2t = sbd("h2t", [128, IC * T], BF16)
            rsd_pp = sbd("rsd_pp", [128, 1])
            sd_pp = sbd("sd_pp", [128, 1])
            sdr = sbd("sdr", [1, 4])
            gm2 = sbd("gm2", [1, 1])
            ps_p = psd("ps_pd", [128, T])
            ps_p2 = psd("ps_pd2", [128, T])
            ps_4 = psd("ps_4d", [128, 4])
            psq5 = psd("psqd", [128, 128], BF16)

            p.dep_on(mW2, m4c)
            emit_sh_read(p, sh2, nc_sb, sh_rd, sh_msk, sh_cnd, gm2[0:1, 0:1])
            make_scale_g(p, gm2[0:1, 0:1], clip_down, sdr, ones128, ps_4,
                         rsd_pp, sd_pp)
            with ExitStack() as phw:
                sbw = lambda n, s_, d=F32: phw.enter_context(p.sbuf(n, s_, d))
                stage = sbw("staged", [128, IF])
                yq = sbw("yqd", [128, IF])
                i1 = sbw("i1d", [128, IF], I32)
                i2 = sbw("i2d", [128, IF], I32)
                tfq = sbw("tfqd", [128, IF])
                wi_b5 = sbw("wi_bd", [128, IF], BF16)
                rm = sbw("rmd", [128, 1])
                for hc in range(EC):
                    p.dma(stage[:, 0:IF], w_down[hc * 128:(hc + 1) * 128, :])
                    p.red(rm, stage[:, 0:IF], ALU.max, absval=True)
                    p.ts(swd[:, hc:hc + 1], rm, 1.0 / 127.0, None, ALU.mult)
                    p.recip(rm, rm)
                    p.ts(rm, rm, 127.0, None, ALU.mult)
                    p.act(yq[:, 0:IF], stage[:, 0:IF], AF.Copy, bias=256.5,
                          scale=rm)
                    quant_round(p, yq, i1, i2, tfq, wi_b5, IF)
                    for ic in range(IC):
                        p.trans(psq5[:, 0:128], wi_b5[:, ic * 128:(ic + 1) * 128],
                                ident_b[:])
                        p.copy(wdT[:, (hc * IC + ic) * 128:(hc * IC + ic) * 128 + 128],
                               psq5[:, 0:128])
                    p.tt(swd[:, hc:hc + 1], swd[:, hc:hc + 1], sd_pp[:, 0:1],
                         ALU.mult)
            p.fence()
            DB = []
            for _i in range(2):
                DB.append(dict(
                    h2q=sbd(f"h2qD{_i}", [128, IC * T], BF16),
                    s_b=sbd(f"s_bD{_i}", [128, EC * T], BF16),
                    xt=sbd(f"xtD{_i}", [128, EC * T]),
                    yq=sbd(f"yqD{_i}", [128, T]),
                    i1=sbd(f"i1D{_i}", [128, T], I32),
                    i2=sbd(f"i2D{_i}", [128, T], I32),
                    tfq=sbd(f"tfD{_i}", [128, T]),
                    scr=sbd(f"scrD{_i}", [128, T]),
                    scr2=sbd(f"scr2D{_i}", [128, T]),
                ))
            p.set_serial(False)

            for ti in range(NT):
                t0 = ti * T
                db = DB[ti % 2]
                h2q, s_b, xt = db["h2q"], db["s_b"], db["xt"]
                p.dma(tile_ap(h2t, IC, T), fm_slice(h2c, t0, T, nchunk=IC))
                p.dma(tile_ap(s_b, EC, T), fm_slice(s_c, t0, T))
                p.dma(tile_ap(xt, EC, T), fm_slice(x, t0, T))
                for ic in range(IC):
                    qb = DB[ic % 2]
                    yq, i1, i2, tfq = qb["yq"], qb["i1"], qb["i2"], qb["tfq"]
                    p.act(yq[:, 0:T], h2t[:, ic * T:(ic + 1) * T], AF.Copy,
                          bias=256.5, scale=rsd_pp[:, 0:1])
                    p.copy(i1[:, 0:T], yq[:, 0:T])
                    p.stt(i2[:, 0:T], yq[:, 0:T], -0.5, i1[:, 0:T], ALU.add,
                          ALU.subtract)
                    p.stt(tfq[:, 0:T], i1[:, 0:T], -256.0, i2[:, 0:T], ALU.add,
                          ALU.add)
                    p.ts(h2q[:, ic * T:(ic + 1) * T], tfq[:, 0:T], 127.0, -128.0,
                         ALU.min, ALU.max, eng="gpsimd")
                for mc in range(EC):
                    mb = DB[mc % 2]
                    psx = ps_p if mc % 2 == 0 else ps_p2
                    scr, scr2 = mb["scr"], mb["scr2"]
                    for ic in range(IC):
                        p.mm(psx[:, 0:T],
                             wdT[:, (mc * IC + ic) * 128:(mc * IC + ic) * 128 + 128],
                             h2q[:, ic * T:(ic + 1) * T],
                             start=(ic == 0), stop=(ic == IC - 1))
                    p.act(scr[:, 0:T], psx[:, 0:T], AF.Silu, scale=swd[:, mc:mc + 1])
                    p.tt(scr[:, 0:T], scr[:, 0:T], s_b[:, mc * T:(mc + 1) * T],
                         ALU.mult)
                    p.tt(scr2[:, 0:T], scr[:, 0:T], xt[:, mc * T:(mc + 1) * T],
                         ALU.add, eng="gpsimd")
                    p.dma(fm_slice_one(o_out, t0, T, mc), row3(scr2.ap()[:, 0:T], T),
                          eng="gpsimd")

        p.emit(ctxs)
    return p


def _pp(v, nchunk):
    return np.ascontiguousarray(np.asarray(v, np.float32).reshape(nchunk, 128).T)


def _sel_np():
    s = np.zeros((EC * 16, 128), np.float32)
    for c in range(EC):
        s[c * 16 + 2 * c, 0:64] = 1.0
        s[c * 16 + 2 * c + 1, 64:128] = 1.0
    return s


_BUILT = {}


def _get(name, fn):
    if name not in _BUILT:
        _BUILT[name] = fn()
    return _BUILT[name]


MODE = 1
BUILD_STOP = None
EXP = set()


def _extras(b):
    pair = b // 2
    others = [q for q in range(NCORES) if q // 2 != pair]
    return others[0:3] if b % 2 == 0 else others[3:6]


def kernel(**inp):
    if MODE == 1:
        return kernel_fused(**inp)
    return kernel_3launch(**inp)


def kernel_fused(**inp):
    g = lambda k: np.asarray(inp[k])
    x = np.asarray(inp["L0_attn_x"], np.float32)
    xfm = [np.ascontiguousarray(x[b].T) for b in range(NCORES)]

    vecs1 = np.concatenate([_pp(g("L0_attn_bq"), EC), _pp(g("L0_attn_lnq_g"), EC),
                            _pp(g("L0_attn_lnq_b"), EC), _pp(g("L0_attn_bv"), EC),
                            _pp(g("L0_attn_lnv_g"), EC), _pp(g("L0_attn_lnv_b"), EC)],
                           axis=1)
    veck = np.concatenate([_pp(g("L0_attn_bk"), EC), _pp(g("L0_attn_lnk_g"), EC),
                           _pp(g("L0_attn_lnk_b"), EC), _pp(g("L0_attn_bphi_k"), EC)],
                          axis=1)
    veca = np.concatenate([_pp(g("L0_attn_bphi_q"), EC), _pp(g("L0_attn_lna_g"), EC),
                           _pp(g("L0_attn_lna_b"), EC), _pp(g("L0_attn_conv_b"), EC)],
                          axis=1)
    cwT = np.ascontiguousarray(g("L0_attn_conv_w").astype(np.float32)
                               .transpose(2, 1, 0).reshape(5 * 64, E))
    n1, n2 = (np.random.uniform(257.0, 1021.0, size=2)
              .astype(np.float32))
    nonce4 = np.ascontiguousarray(np.tile(np.array([[n1, n2]], np.float32),
                                          (4, 1)))
    nonce_r = np.ascontiguousarray(
        np.concatenate([np.full(8, n1, np.float32),
                        np.full(8, n2, np.float32)]).reshape(1, 16))

    pf = _get("F", build_fused)
    common = {
        "Wq": np.ascontiguousarray(g("L0_attn_Wq"), dtype=np.float32),
        "Wv": np.ascontiguousarray(g("L0_attn_Wv"), dtype=np.float32),
        "vecs": vecs1,
        "Wk": np.ascontiguousarray(g("L0_attn_Wk"), dtype=np.float32),
        "Wpk": np.ascontiguousarray(g("L0_attn_Wphi_k"), dtype=np.float32),
        "veck": veck,
        "bpk_row": np.asarray(inp["L0_attn_bphi_k"], np.float32).reshape(1, E),
        "w_gate": np.ascontiguousarray(g("L0_ffn_w_gate"), dtype=np.float32),
        "w_up": np.ascontiguousarray(g("L0_ffn_w_up"), dtype=np.float32),
        "clip_x": np.asarray(inp["L0_ffn_clip_x"], np.float32).reshape(1, 1),
        "Wpq": np.ascontiguousarray(g("L0_attn_Wphi_q"), dtype=np.float32),
        "veca": veca, "cwT": cwT,
        "gamma": np.asarray(inp["L0_attn_gamma"], np.float32).reshape(1, 1),
        "W1": np.ascontiguousarray(g("L0_attn_W1"), dtype=np.float32),
        "b1": _pp(g("L0_attn_b1"), MC),
        "W2": np.ascontiguousarray(g("L0_attn_W2"), dtype=np.float32),
        "b2": _pp(g("L0_attn_b2"), EC),
        "w_down": np.ascontiguousarray(g("L0_ffn_w_down"), dtype=np.float32),
        "clip_down": np.asarray(inp["L0_ffn_clip_down"], np.float32).reshape(1, 1),
        "sel": _sel_np(), "nonce4": nonce4, "nonce_r": nonce_r,
    }
    in_maps = []
    for b in range(NCORES):
        m = dict(common)
        m["x"] = xfm[b]
        m["xe"] = np.ascontiguousarray(
            np.concatenate([xfm[e] for e in _extras(b)], axis=0))
        in_maps.append(m)
    res = run_bass_kernel_spmd(pf.nc, in_maps, core_ids=list(range(NCORES))).results
    out = np.stack([np.asarray(res[b]["out"]).T for b in range(NCORES)])
    return np.ascontiguousarray(out, dtype=np.float32)


def kernel_3launch(**inp):
    g = lambda k: np.asarray(inp[k])
    x = np.asarray(inp["L0_attn_x"], np.float32)
    xfm = [np.ascontiguousarray(x[b].T) for b in range(NCORES)]

    vecs1 = np.concatenate([_pp(g("L0_attn_bq"), EC), _pp(g("L0_attn_lnq_g"), EC),
                            _pp(g("L0_attn_lnq_b"), EC), _pp(g("L0_attn_bv"), EC),
                            _pp(g("L0_attn_lnv_g"), EC), _pp(g("L0_attn_lnv_b"), EC)],
                           axis=1)
    veck = np.concatenate([_pp(g("L0_attn_bk"), EC), _pp(g("L0_attn_lnk_g"), EC),
                           _pp(g("L0_attn_lnk_b"), EC), _pp(g("L0_attn_bphi_k"), EC)],
                          axis=1)
    veca = np.concatenate([_pp(g("L0_attn_bphi_q"), EC), _pp(g("L0_attn_lna_g"), EC),
                           _pp(g("L0_attn_lna_b"), EC), _pp(g("L0_attn_conv_b"), EC)],
                          axis=1)
    cwT = np.ascontiguousarray(g("L0_attn_conv_w").astype(np.float32)
                               .transpose(2, 1, 0).reshape(5 * 64, E))

    p1 = _get("L1", build_L1)
    in1 = [{"x": xfm[b], "Wq": np.ascontiguousarray(g("L0_attn_Wq"), dtype=np.float32),
            "Wv": np.ascontiguousarray(g("L0_attn_Wv"), dtype=np.float32),
            "vecs": vecs1} for b in range(NCORES)]
    r1 = run_bass_kernel_spmd(p1.nc, in1, core_ids=list(range(NCORES))).results

    mxs = np.ascontiguousarray(
        np.stack([np.asarray(r1[b]["mx"]) for b in range(NCORES)]).reshape(1, 128))

    p2 = _get("L2", build_L2)
    in2 = [{"x": xfm[b], "stats": np.asarray(r1[b]["stats"]),
            "qg": np.asarray(r1[b]["qg"]), "mxs": mxs,
            "clip_x": np.asarray(inp["L0_ffn_clip_x"], np.float32).reshape(1, 1),
            "Wk": np.ascontiguousarray(g("L0_attn_Wk"), dtype=np.float32),
            "Wpk": np.ascontiguousarray(g("L0_attn_Wphi_k"), dtype=np.float32),
            "veck": veck,
            "bpk_row": np.asarray(inp["L0_attn_bphi_k"], np.float32).reshape(1, E),
            "w_gate": np.ascontiguousarray(g("L0_ffn_w_gate"), dtype=np.float32),
            "w_up": np.ascontiguousarray(g("L0_ffn_w_up"), dtype=np.float32),
            "Vc": np.asarray(r1[b]["Vc"])} for b in range(NCORES)]
    r2 = run_bass_kernel_spmd(p2.nc, in2, core_ids=list(range(NCORES))).results

    mxh = np.ascontiguousarray(
        np.stack([np.asarray(r2[b]["mxh"]) for b in range(NCORES)]).reshape(1, 128))

    p3 = _get("L3", build_L3)
    in3 = [{"x": xfm[b], "stats": np.asarray(r1[b]["stats"]),
            "Qc": np.asarray(r1[b]["Qc"]), "kv": np.asarray(r2[b]["kv"]),
            "ks": np.asarray(r2[b]["ks"]),
            "Wpq": np.ascontiguousarray(g("L0_attn_Wphi_q"), dtype=np.float32),
            "veca": veca, "cwT": cwT,
            "gamma": np.asarray(inp["L0_attn_gamma"], np.float32).reshape(1, 1),
            "W1": np.ascontiguousarray(g("L0_attn_W1"), dtype=np.float32),
            "b1": _pp(g("L0_attn_b1"), MC),
            "W2": np.ascontiguousarray(g("L0_attn_W2"), dtype=np.float32),
            "b2": _pp(g("L0_attn_b2"), EC),
            "w_down": np.ascontiguousarray(g("L0_ffn_w_down"), dtype=np.float32),
            "clip_down": np.asarray(inp["L0_ffn_clip_down"], np.float32).reshape(1, 1),
            "mxh_all": mxh, "h2c": np.asarray(r2[b]["h2c"]), "sel": _sel_np()}
           for b in range(NCORES)]
    r3 = run_bass_kernel_spmd(p3.nc, in3, core_ids=list(range(NCORES))).results

    out = np.stack([np.asarray(r3[b]["out"]).T for b in range(NCORES)])
    return np.ascontiguousarray(out, dtype=np.float32)
